# version 02
# baseline (speedup 1.0000x reference)
import numpy as np
import ml_dtypes
import concourse.bass as bass
import concourse.mybir as mybir
import concourse.bass_utils as bass_utils

F32 = mybir.dt.float32
BF16 = mybir.dt.bfloat16
AF = mybir.ActivationFunctionType
ALU = mybir.AluOpType
AX = mybir.AxisListType

RACE_DETECT = False
B, T, D = 8, 8192, 1024
TB = 256
NT = T // TB
EPS = 1e-5
bfloat16 = ml_dtypes.bfloat16


class Prog:
    def __init__(self):
        self.nc = bass.Bass("TRN2", target_bir_lowering=False,
                            detect_race_conditions=RACE_DETECT)
        self._stack = []
        self.sems = {}
        self.cnt = {}
        self.inputs = {}
        self.outputs = {}
        self.streams = {}
        self.wbf = {}
        self._names = set()
        self._scope = None
        for name in ("LD", "ST", "PE", "VE", "SC", "GP"):
            self._mksem(name)

    def uname(self, name):
        if name not in self._names:
            self._names.add(name)
            return name
        k = 2
        while f"{name}_{k}" in self._names:
            k += 1
        self._names.add(f"{name}_{k}")
        return f"{name}_{k}"

    def stream(self, name, shape, dtype=F32, out=False):
        if name in self.streams:
            return self.streams[name]
        return self.dram_out(name, shape, dtype) if out else self.dram_in(name, shape, dtype)

    def _mksem(self, name):
        cm = self.nc.semaphore(name)
        self.sems[name] = cm.__enter__()
        self._stack.append(cm)
        self.cnt[name] = 0

    def sem(self, name):
        if name not in self.sems:
            self._mksem(name)
        return self.sems[name]

    def inc(self, inst, name, dma=False):
        n = 16 if dma else 1
        self.sem(name)
        self.cnt[name] += n
        inst.then_inc(self.sems[name], n)
        return self.cnt[name]

    def wait(self, engine, name, value=None):
        self.sem(name)
        v = self.cnt[name] if value is None else value
        if v > 0:
            engine.wait_ge(self.sems[name], v)

    def snap(self, *names):
        names = names or tuple(self.cnt)
        return {n: self.cnt[n] for n in names}

    def wait_counts(self, engine, counts):
        for n, v in counts.items():
            if v > 0:
                engine.wait_ge(self.sems[n], v)

    def dram_in(self, name, shape, dtype=F32):
        if name in self.inputs:
            return self.inputs[name]
        ap = self.nc.dram_tensor(name, list(shape), dtype, kind="ExternalInput").ap()
        self.inputs[name] = ap
        return ap

    def dram_out(self, name, shape, dtype=F32):
        ap = self.nc.dram_tensor(name, list(shape), dtype, kind="ExternalOutput").ap()
        self.outputs[name] = ap
        return ap

    def dram_tmp(self, name, shape, dtype=F32):
        return self.nc.dram_tensor(name, list(shape), dtype, kind="Internal").ap()

    def sbuf(self, name, shape, dtype=F32):
        cm = self.nc.sbuf_tensor(self.uname(name), list(shape), dtype)
        t = cm.__enter__()
        (self._scope if self._scope is not None else self._stack).append(cm)
        return t

    def psum(self, name, shape, dtype=F32):
        cm = self.nc.psum_tensor(self.uname(name), list(shape), dtype)
        t = cm.__enter__()
        (self._scope if self._scope is not None else self._stack).append(cm)
        return t

    def scope_open(self):
        self._scope = []

    def scope_close(self):
        for name in list(self.cnt):
            if name.startswith("ST") or name == "DBG":
                self.wait(self.nc.sync, name)
        self.nc.all_engine_barrier()
        for cm in reversed(self._scope or []):
            cm.__exit__(None, None, None)
        self._scope = None

    def finish(self):
        if self._scope is not None:
            return
        for name in list(self.cnt):
            if name.startswith("ST") or name == "DBG":
                self.wait(self.nc.sync, name)
        for cm in reversed(self._stack):
            cm.__exit__(None, None, None)
        self._stack = []


def ch_major(dram_ap):
    return dram_ap.rearrange("(c p) t -> p c t", p=128)


class LN:

    def __init__(self, p: Prog, tag, sq=None):
        nc = p.nc
        self.p = p
        self.ones_col = p.sbuf(f"{tag}_onc", (128, 1), F32)
        self.ones_col_bf = p.sbuf(f"{tag}_oncb", (128, 1), BF16)
        self.ones_row = p.sbuf(f"{tag}_onr", (1, 128), F32)
        nc.vector.memset(self.ones_col[:, :], 1.0)
        nc.vector.memset(self.ones_col_bf[:, :], 1.0)
        p.inc(nc.vector.memset(self.ones_row[:, :], 1.0), "VE")
        self.sq = p.sbuf(f"{tag}_sq", (128, TB), F32) if sq is None else sq
        self.eps_t = p.sbuf(f"{tag}_eps", (1, 1), F32)
        nc.vector.memset(self.eps_t[:1, :1], EPS)
        self.rows = p.sbuf(f"{tag}_rows", (1, 4, TB), F32)
        self.ps_stat = p.psum(f"{tag}_pst", (1, 512), F32)
        self.ps_bc = p.psum(f"{tag}_psb", (128, 512), F32)
        self.hist = p.snap()

    def stats(self, x_tiles, ready):
        p, nc = self.p, self.p.nc
        nct = len(x_tiles)
        n_ch = nct * 128
        sx = self.ps_stat[:1, 0:TB]
        sq_ps = self.ps_stat[:1, 256:256 + TB]
        p.wait_counts(nc.tensor, {"VE": self.hist["VE"], "SC": self.hist["SC"]})
        p.wait_counts(nc.tensor, ready)
        onc = self.ones_col if x_tiles[0].dtype == F32 else self.ones_col_bf
        for c in range(nct):
            inst = nc.tensor.matmul(sx, onc[:, :], x_tiles[c],
                                    start=(c == 0), stop=(c == nct - 1))
        p.inc(inst, "PE")
        p.wait_counts(nc.scalar, ready)
        for c in range(nct):
            p.wait_counts(nc.scalar, {"PE": self.hist["PE"] if c == 0 else self.pe_sq_last})
            sc = p.inc(nc.scalar.activation(self.sq[:, :], x_tiles[c], AF.Square), "SC")
            p.wait(nc.tensor, "SC", sc)
            inst = nc.tensor.matmul(sq_ps, self.ones_col[:, :], self.sq[:, :],
                                    start=(c == 0), stop=(c == nct - 1))
            self.pe_sq_last = p.inc(inst, "PE")
        pe_stats = self.pe_sq_last
        mu = self.rows[:1, 0, :]
        ex2 = self.rows[:1, 1, :]
        var = self.rows[:1, 2, :]
        o = self.rows[:1, 3, :]
        p.wait(nc.vector, "PE", pe_stats)
        nc.vector.tensor_scalar(mu, sx, 1.0 / n_ch, None, ALU.mult)
        nc.vector.tensor_scalar(ex2, sq_ps, 1.0 / n_ch, None, ALU.mult)
        nc.vector.drain()
        nc.vector.scalar_tensor_tensor(var, mu, -1.0, mu, ALU.mult, ALU.mult)
        nc.vector.drain()
        ve_var = p.inc(nc.vector.tensor_add(var, var, ex2), "VE")
        inv = ex2
        p.wait(nc.scalar, "VE", ve_var)
        nc.scalar.activation(inv, var, AF.Ln, bias=self.eps_t[:1, :1])
        nc.scalar.drain()
        sc_inv = p.inc(nc.scalar.activation(inv, inv, AF.Exp, scale=-0.5), "SC")
        p.wait(nc.vector, "SC", sc_inv)
        ve_o = p.inc(nc.vector.scalar_tensor_tensor(o, mu, -1.0, inv, ALU.mult, ALU.mult), "VE")
        p.wait(nc.tensor, "VE", ve_o)
        nc.tensor.matmul(self.ps_bc[:, 0:TB], self.ones_row[:1, :], inv, start=True, stop=True)
        inst = nc.tensor.matmul(self.ps_bc[:, 256:256 + TB], self.ones_row[:1, :], o,
                                start=True, stop=True)
        pe_bc = p.inc(inst, "PE")
        return self.ps_bc[:, 0:TB], self.ps_bc[:, 256:256 + TB], pe_bc

    def mark_used(self):
        self.hist = self.p.snap()


def ln_apply(p, x_c, out_c, inv_bc, o_bc, scratch):
    nc = p.nc
    nc.vector.tensor_mul(scratch, x_c, inv_bc)
    nc.vector.drain()
    inst = nc.vector.tensor_add(out_c, scratch, o_bc)
    nc.vector.drain()
    return p.inc(inst, "VE")


def _f32(x):
    return np.ascontiguousarray(np.asarray(x, dtype=np.float32))


def _ch_tiles(vec, nct):
    return _f32(vec).reshape(nct, 128).T.copy()


def _lhsT_tiles(arr):
    ic, oc = arr.shape
    assert ic % 128 == 0
    return np.ascontiguousarray(arr.reshape(ic // 128, 128, oc).transpose(1, 0, 2))


def host_weights(inputs):
    w = {}
    w["L0_W_in"] = _lhsT_tiles(_f32(inputs["L0_attn_W_in"]))
    w["L0_W_out"] = _lhsT_tiles(_f32(inputs["L0_attn_W_out"]))
    w["L0f_W1"] = _lhsT_tiles(_f32(inputs["L0_ffn_W1"]))
    w["L0f_W2"] = _lhsT_tiles(_f32(inputs["L0_ffn_W2"]))
    for l in range(4):
        win = _f32(inputs["L1_attn_in_proj_w"][l]).T
        w[f"M{l}_Win"] = np.ascontiguousarray(
            win.reshape(8, 128, 32, 128).transpose(1, 2, 0, 3))
        wx = _f32(inputs["L1_attn_x_proj_w"][l]).T
        wxp = np.zeros((2048, 128), np.float32)
        wxp[:, 0] = wx[:, 0]
        wxp[:, 32:48] = wx[:, 1:17]
        wxp[:, 64:80] = wx[:, 17:33]
        w[f"M{l}_Wx"] = _lhsT_tiles(wxp)
        wo = _f32(inputs["L1_attn_out_proj_w"][l]).T
        w[f"M{l}_Wout"] = np.ascontiguousarray(
            wo.reshape(16, 128, 8, 128).transpose(1, 2, 0, 3))
    w["OP1"] = _lhsT_tiles(_f32(inputs["L1_attn_op1_w"]).T)
    w["OP2"] = _lhsT_tiles(_f32(inputs["L1_attn_op2_w"]).T)
    for i in range(6):
        w[f"F{i}_Wqkv"] = _lhsT_tiles(_f32(inputs["L1_ffn_Wqkv"][i]))
        w[f"F{i}_Wo"] = _lhsT_tiles(_f32(inputs["L1_ffn_Wo"][i]))
        w1 = _f32(inputs["L1_ffn_W1"][i])
        w1p = np.zeros((1024, 5632), np.float32)
        w1p[:, 0:2730] = w1[:, 0:2730]
        w1p[:, 2816:2816 + 2730] = w1[:, 2730:5460]
        w[f"F{i}_W1"] = _lhsT_tiles(w1p)
        w2 = _f32(inputs["L1_ffn_W2"][i])
        w2p = np.zeros((2816, 1024), np.float32)
        w2p[:2730] = w2
        w[f"F{i}_W2"] = _lhsT_tiles(w2p)
    return w


def host_smalls(inputs):
    s = {}
    s["L0_b_in_t"] = _ch_tiles(inputs["L0_attn_b_in"], 24)
    s["L0_short_w_t"] = _f32(inputs["L0_attn_short_w"]).reshape(24, 128, 3).transpose(1, 0, 2).copy()
    s["L0_short_b_t"] = _ch_tiles(inputs["L0_attn_short_b"], 24)
    s["L0_inner_w_t"] = _f32(inputs["L0_attn_inner_w"]).reshape(8, 128, 4).transpose(1, 0, 2).copy()
    s["L0_inner_b_t"] = _ch_tiles(inputs["L0_attn_inner_b"], 8)
    s["L0_b_out_t"] = _ch_tiles(inputs["L0_attn_b_out"], 8)
    s["L0f_b1_t"] = _ch_tiles(inputs["L0_ffn_b1"], 32)
    s["L0f_b2_t"] = _ch_tiles(inputs["L0_ffn_b2"], 8)
    s["ident128"] = np.eye(128, dtype=bfloat16)
    s["M_innorm_g"] = _ch_tiles(inputs["L1_attn_in_norm_g"], 8)
    s["M_innorm_b"] = _ch_tiles(inputs["L1_attn_in_norm_b"], 8)
    for l in range(4):
        s[f"M{l}_ln_g"] = _ch_tiles(inputs["L1_attn_ln_g"][l], 8)
        s[f"M{l}_ln_b"] = _ch_tiles(inputs["L1_attn_ln_b"][l], 8)
        cwv = _f32(inputs["L1_attn_conv_w"][l])
        s[f"M{l}_conv_w"] = np.ascontiguousarray(
            cwv.reshape(2, 16, 128, 4).transpose(2, 0, 1, 3))
        cbv = _f32(inputs["L1_attn_conv_b"][l])
        s[f"M{l}_conv_b"] = np.ascontiguousarray(
            cbv.reshape(2, 16, 128).transpose(2, 0, 1))
        s[f"M{l}_dt_w"] = _ch_tiles(inputs["L1_attn_dt_proj_w"][l][:, 0], 16)
        s[f"M{l}_dt_b"] = _ch_tiles(inputs["L1_attn_dt_proj_b"][l], 16)
        s[f"M{l}_D"] = _ch_tiles(inputs["L1_attn_D"][l], 16)
        s[f"M{l}_mln_g"] = _ch_tiles(inputs["L1_attn_mamba_ln_g"][l], 16)
        s[f"M{l}_mln_b"] = _ch_tiles(inputs["L1_attn_mamba_ln_b"][l], 16)
    sel = np.zeros((128, 33, 128), dtype=bfloat16)
    sel[0, 0, :] = 1.0
    for n in range(16):
        sel[32 + n, 1 + n, :] = 1.0
        sel[64 + n, 17 + n, :] = 1.0
    s["sel_rows"] = sel
    s["zeros96"] = np.zeros((128, 96), bfloat16)
    s["OP1_b"] = _ch_tiles(inputs["L1_attn_op1_b"], 8)
    s["OP2_b"] = _ch_tiles(inputs["L1_attn_op2_b"], 8)
    half = np.arange(0, 64, 2, dtype=np.float64)
    inv_freq = 1.0 / 10000.0 ** (half / 64.0)
    tt = np.arange(T, dtype=np.float64)
    fr = tt[:, None] * inv_freq[None, :]
    freqs = np.concatenate([fr, fr], axis=1)
    base_scale = (half + 0.4 * 64) / (1.4 * 64)
    power = (tt - T // 2) / (256 // 2)
    scale = base_scale[None, :] ** power[:, None]
    scale = np.concatenate([scale, scale], axis=1)
    cos_t = np.cos(freqs); sin_t = np.sin(freqs)
    qc = (cos_t * scale).T.astype(np.float32)
    qs_ = (sin_t * scale).T.astype(np.float32)
    kc = (cos_t / scale).T.astype(np.float32)
    ks_ = (sin_t / scale).T.astype(np.float32)
    s["rot_qc"] = np.ascontiguousarray(np.tile(qc, (2, 1)))
    s["rot_qs"] = np.ascontiguousarray(np.tile(qs_, (2, 1)))
    s["rot_kc"] = np.ascontiguousarray(np.tile(kc, (2, 1)))
    s["rot_ks"] = np.ascontiguousarray(np.tile(ks_, (2, 1)))
    R = np.zeros((64, 64), np.float32)
    for d in range(32):
        R[d, d + 32] = -1.0
        R[d + 32, d] = 1.0
    Rt = R.T
    Rbd = np.zeros((128, 128), np.float32)
    Rbd[:64, :64] = Rt
    Rbd[64:, 64:] = Rt
    s["RbdT"] = Rbd.astype(bfloat16)
    bd2 = np.zeros((128, 2), np.float32)
    bd2[:64, 0] = 1.0
    bd2[64:, 1] = 1.0
    s["bd2"] = bd2
    ones_top = np.zeros((1, 128), np.float32); ones_top[0, :64] = 1.0
    ones_bot = np.zeros((1, 128), np.float32); ones_bot[0, 64:] = 1.0
    s["ones_top"] = ones_top
    s["ones_bot"] = ones_bot
    bdT = np.zeros((2, 128), np.float32)
    bdT[0, :64] = 1.0
    bdT[1, 64:] = 1.0
    s["bdT"] = bdT
    pp = np.arange(128, dtype=np.float32)
    s["mask_st"] = np.stack([pp, 128 + pp], axis=1)
    s["mask_en"] = np.stack([pp + 257, 128 + pp + 257], axis=1)
    s["mask_en0"] = np.stack([pp + 1, 128 + pp + 1], axis=1)
    NEG = np.float32(-3.0e38)
    ii = np.arange(128)[:, None]
    jj = np.arange(512)[None, :]
    mk = np.zeros((2, 128, 512), np.float32)
    for m in range(2):
        ok = (jj >= m * 128 + ii) & (jj <= m * 128 + ii + 256)
        mk[m] = np.where(ok, 0.0, NEG)
    s["mask_add_t"] = np.ascontiguousarray(mk.transpose(1, 0, 2))
    jj0 = np.arange(256)[None, :]
    mk0 = np.zeros((2, 128, 256), np.float32)
    for m in range(2):
        ok = jj0 <= m * 128 + ii
        mk0[m] = np.where(ok, 0.0, NEG)
    s["mask_add0_t"] = np.ascontiguousarray(mk0.transpose(1, 0, 2))
    s["ident128f"] = np.eye(128, dtype=np.float32)
    for i in range(6):
        qsv = _f32(inputs["L1_ffn_q_scale"][i])
        ksv = _f32(inputs["L1_ffn_k_scale"][i])
        s[f"F{i}_qs"] = np.tile(qsv, 2).reshape(128, 1).copy()
        s[f"F{i}_ks"] = np.tile(ksv, 2).reshape(128, 1).copy()
        s[f"F{i}_qsp"] = np.tile(np.roll(qsv, -32), 2).reshape(128, 1).copy()
        s[f"F{i}_ksp"] = np.tile(np.roll(ksv, -32), 2).reshape(128, 1).copy()
        s[f"F{i}_ln_g"] = _ch_tiles(inputs["L1_ffn_ln_g"][i], 8)
        s[f"F{i}_ln_b"] = _ch_tiles(inputs["L1_ffn_ln_b"][i], 8)
        b1 = _f32(inputs["L1_ffn_b1"][i])
        b1a = np.zeros(2816, np.float32); b1a[:2730] = b1[:2730]
        b1g = np.zeros(2816, np.float32); b1g[:2730] = b1[2730:]
        s[f"F{i}_b1a"] = _ch_tiles(b1a, 22)
        s[f"F{i}_b1g"] = _ch_tiles(b1g, 22)
        s[f"F{i}_b2"] = _ch_tiles(inputs["L1_ffn_b2"][i], 8)
    return s


def build_wcast(weights):
    p = Prog()
    nc = p.nc
    CH = 16384
    stage = p.sbuf("wc_stage", (128, 2, CH), F32)
    out = p.sbuf("wc_out", (128, 2, CH), BF16)
    items = []
    for name, arr in weights.items():
        cols = arr.size // 128
        src = p.dram_in(name, (128, cols))
        dst = p.dram_out(name + "_bf", (128, cols), BF16)
        items.append((src, dst, cols))
    hist = {0: None, 1: None}
    k = 0
    for src, dst, cols in items:
        for c0 in range(0, cols, CH):
            w = min(CH, cols - c0)
            s = k % 2
            if hist[s] is not None:
                nc.sync.wait_ge(p.sem("SC"), hist[s][0])
                nc.sync.wait_ge(p.sem(f"ST{s}"), hist[s][1])
            ld = p.inc(nc.sync.dma_start(stage[:, s, :w], src[:, c0:c0 + w]), f"WL{s}", dma=True)
            p.wait(nc.scalar, f"WL{s}", ld)
            sc = p.inc(nc.scalar.copy(out[:, s, :w], stage[:, s, :w]), "SC")
            p.wait(nc.sync, "SC", sc)
            st = p.inc(nc.sync.dma_start(dst[:, c0:c0 + w], out[:, s, :w]), f"ST{s}", dma=True)
            hist[s] = (sc, st)
            k += 1
    p.finish()
    return p


def load_w_bf(p, name, n_ic, n_oc):
    nc = p.nc
    if name in p.wbf:
        d = p.wbf[name].rearrange("p (i o) -> p i o", i=n_ic)
    else:
        d = p.dram_in(name + "_bf", (128, n_ic * n_oc), BF16).rearrange(
            "p (i o) -> p i o", i=n_ic)
    s = p.sbuf(name + "_sb", (128, n_ic, n_oc), BF16)
    p.inc(nc.sync.dma_start(s[:, :, :], d[:, :, :]), "LD", dma=True)
    return s


def load_small(p, name, shape, dtype=F32):
    nc = p.nc
    d = p.dram_in(name, shape, dtype)
    s = p.sbuf(name + "_sb", shape, dtype)
    p.inc(nc.sync.dma_start(s[...], d[...]), "LD", dma=True)
    return s


def emit_l0_attn(p: Prog):
    nc = p.nc
    xin = ch_major(p.stream("xin", (D, T)))
    xout = ch_major(p.stream("xout", (D, T), out=True))
    w_in = load_w_bf(p, "L0_W_in", 8, 3072)
    w_out = load_w_bf(p, "L0_W_out", 8, 1024)
    b_in = load_small(p, "L0_b_in_t", (128, 24))
    sw = load_small(p, "L0_short_w_t", (128, 24, 3))
    sbv = load_small(p, "L0_short_b_t", (128, 24))
    iw = load_small(p, "L0_inner_w_t", (128, 8, 4))
    ib = load_small(p, "L0_inner_b_t", (128, 8))
    bo = load_small(p, "L0_b_out_t", (128, 8))
    ident = load_small(p, "ident128", (128, 128), BF16)
    ldw = p.cnt["LD"]

    dg_s = p.sbuf("dg_s", (128, 24, 3, 128), BF16)
    dg_i = p.sbuf("dg_i", (128, 8, 4, 128), BF16)
    p.wait(nc.vector, "LD", ldw)
    for c in range(24):
        for k in range(3):
            inst = nc.vector.tensor_scalar(dg_s[:, c, k, :], ident[:, :], sw[:, c, k:k + 1],
                                           None, ALU.mult)
    for c in range(8):
        for k in range(4):
            inst = nc.vector.tensor_scalar(dg_i[:, c, k, :], ident[:, :], iw[:, c, k:k + 1],
                                           None, ALU.mult)
    dgv = p.inc(inst, "VE")
    p.wait(nc.tensor, "VE", dgv)

    x_sb = p.sbuf("x_sb", (128, 2, 8, TB), F32)
    xn_sb = p.sbuf("xn_sb", (128, 8, TB), BF16)
    cin = p.sbuf("cin", (128, 24, 2 + TB), BF16)
    qk_sb = p.sbuf("qk_sb", (128, 16, TB), F32)
    vg = p.sbuf("vg", (128, 8, 3 + TB), BF16)
    y_sb = p.sbuf("y_sb", (128, 8, TB), BF16)
    o_sb = p.sbuf("o_sb", (128, 8, TB), F32)
    scr = p.sbuf("scr", (128, TB), F32)
    ln = LN(p, "l0a_ln")
    ps_mm = [p.psum(f"l0a_mm{i}", (128, 512), F32) for i in range(2)]

    nc.vector.memset(cin[:, :, 0:2], 0.0)
    p.inc(nc.vector.memset(vg[:, :, 0:3], 0.0), "VE")

    x_hist = [None, None]
    ps_guard = [{}, {}]
    blk_hist = None

    for tau in range(NT):
        s = tau % 2
        t0 = tau * TB
        if x_hist[s] is not None:
            p.wait_counts(nc.sync, x_hist[s])
        ld = p.inc(nc.sync.dma_start(x_sb[:, s, :, :], xin[:, :, t0:t0 + TB]), f"LX{s}", dma=True)
        xt = [x_sb[:, s, c, :] for c in range(8)]
        inv_bc, o_bc, pe_bc = ln.stats(xt, {f"LX{s}": ld})
        p.wait(nc.vector, "PE", pe_bc)
        if blk_hist is not None:
            p.wait_counts(nc.vector, {"PE": blk_hist["PE"]})
        for c in range(8):
            ln_apply(p, xt[c], xn_sb[:, c, :], inv_bc, o_bc, scr[:, :])
        xnv = p.cnt["VE"]
        ln.mark_used()
        p.wait(nc.tensor, "VE", xnv)
        for oc in range(24):
            bank = ps_mm[oc % 2]
            p.wait_counts(nc.tensor, ps_guard[oc % 2])
            for ic in range(8):
                inst = nc.tensor.matmul(bank[:, :TB], w_in[:, ic, oc * 128:(oc + 1) * 128],
                                        xn_sb[:, ic, :], start=(ic == 0), stop=(ic == 7))
            mm = p.inc(inst, "PE")
            p.wait(nc.scalar, "PE", mm)
            ps_guard[oc % 2] = {"SC": p.inc(
                nc.scalar.activation(cin[:, oc, 2:2 + TB], bank[:, :TB], AF.Identity,
                                     bias=b_in[:, oc:oc + 1]), "SC")}
        cin_sc = p.cnt["SC"]
        p.wait(nc.tensor, "SC", cin_sc)
        sc_k_done = 0
        for oc in list(range(8, 16)) + list(range(0, 8)) + list(range(16, 24)):
            bank = ps_mm[oc % 2]
            p.wait_counts(nc.tensor, ps_guard[oc % 2])
            for k in range(3):
                inst = nc.tensor.matmul(bank[:, :TB], dg_s[:, oc, k, :], cin[:, oc, k:k + TB],
                                        start=(k == 0), stop=(k == 2))
            mm = p.inc(inst, "PE")
            if oc < 16:
                p.wait(nc.scalar, "PE", mm)
                sc_k_done = p.inc(
                    nc.scalar.activation(qk_sb[:, oc, :], bank[:, :TB], AF.Identity,
                                         bias=sbv[:, oc:oc + 1]), "SC")
                ps_guard[oc % 2] = {"SC": sc_k_done}
            else:
                c = oc - 16
                p.wait(nc.vector, "PE", mm)
                p.wait(nc.vector, "SC", sc_k_done)
                ve = p.inc(nc.vector.scalar_tensor_tensor(
                    vg[:, c, 3:3 + TB], bank[:, :TB], sbv[:, oc:oc + 1], qk_sb[:, 8 + c, :],
                    ALU.add, ALU.mult), "VE")
                ps_guard[oc % 2] = {"VE": ve}
        ve_v = p.cnt["VE"]
        p.wait(nc.tensor, "VE", ve_v)
        for c in range(8):
            bank = ps_mm[c % 2]
            p.wait_counts(nc.tensor, ps_guard[c % 2])
            for k in range(4):
                inst = nc.tensor.matmul(bank[:, :TB], dg_i[:, c, k, :], vg[:, c, k:k + TB],
                                        start=(k == 0), stop=(k == 3))
            mm = p.inc(inst, "PE")
            p.wait(nc.vector, "PE", mm)
            ps_guard[c % 2] = {"VE": p.inc(nc.vector.scalar_tensor_tensor(
                y_sb[:, c, :], bank[:, :TB], ib[:, c:c + 1], qk_sb[:, c, :],
                ALU.add, ALU.mult), "VE")}
        yv = p.cnt["VE"]
        p.wait(nc.tensor, "VE", yv)
        if blk_hist is not None:
            p.wait(nc.scalar, "ST0", blk_hist.get("ST0", 0))
            p.wait(nc.scalar, "ST1", blk_hist.get("ST1", 0))
        sc_o = 0
        for oc in range(8):
            bank = ps_mm[oc % 2]
            p.wait_counts(nc.tensor, ps_guard[oc % 2])
            for ic in range(8):
                inst = nc.tensor.matmul(bank[:, :TB], w_out[:, ic, oc * 128:(oc + 1) * 128],
                                        y_sb[:, ic, :], start=(ic == 0), stop=(ic == 7))
            mm = p.inc(inst, "PE")
            p.wait(nc.scalar, "PE", mm)
            sc_o = p.inc(nc.scalar.activation(o_sb[:, oc, :], bank[:, :TB], AF.Identity,
                                              bias=bo[:, oc:oc + 1]), "SC")
            ps_guard[oc % 2] = {"SC": sc_o}
        ot = [o_sb[:, c, :] for c in range(8)]
        inv2, o2, pe_bc2 = ln.stats(ot, {"SC": sc_o})
        p.wait(nc.vector, "PE", pe_bc2)
        for c in range(8):
            nc.vector.tensor_mul(scr[:, :], ot[c], inv2)
            nc.vector.drain()
            nc.vector.tensor_add(scr[:, :], scr[:, :], o2)
            nc.vector.drain()
            inst = nc.vector.tensor_add(o_sb[:, c, :], scr[:, :], xt[c])
            nc.vector.drain()
        res_ve = p.inc(inst, "VE")
        ln.mark_used()
        p.wait(nc.sync, "VE", res_ve)
        p.inc(nc.sync.dma_start(xout[:, :, t0:t0 + TB], o_sb[:, :, :]), f"ST{s}", dma=True)
        x_hist[s] = {"VE": res_ve, "PE": p.cnt["PE"], "SC": p.cnt["SC"]}
        blk_hist = p.snap()
        if tau < NT - 1:
            p.wait(nc.scalar, "PE", p.cnt["PE"])
            nc.scalar.drain()
            nc.scalar.copy(cin[:, :, 0:2], cin[:, :, TB:TB + 2])
            p.inc(nc.scalar.copy(vg[:, :, 0:3], vg[:, :, TB:TB + 3]), "SC")
    p.finish()
    return p


def emit_l0_ffn(p: Prog):
    nc = p.nc
    xin = ch_major(p.stream("xin", (D, T)))
    xout = ch_major(p.stream("xout", (D, T), out=True))
    w1 = load_w_bf(p, "L0f_W1", 8, 4096)
    w2 = load_w_bf(p, "L0f_W2", 32, 1024)
    b1 = load_small(p, "L0f_b1_t", (128, 32))
    b2 = load_small(p, "L0f_b2_t", (128, 8))
    p.wait(nc.tensor, "LD")
    p.wait(nc.scalar, "LD")
    p.wait(nc.vector, "LD")

    x_sb = p.sbuf("x_sb", (128, 2, 8, TB), F32)
    xn_sb = p.sbuf("xn_sb", (128, 8, TB), BF16)
    h_sb = p.sbuf("h_sb", (128, 32, TB), BF16)
    o_sb = p.sbuf("o_sb", (128, 8, TB), F32)
    scr = p.sbuf("scr", (128, TB), F32)
    ln = LN(p, "l0f_ln")
    ps_mm = [p.psum(f"l0f_mm{i}", (128, 512), F32) for i in range(2)]

    x_hist = [None, None]
    ps_hist = [0, 0]
    blk_hist = None
    for tau in range(NT):
        s = tau % 2
        t0 = tau * TB
        nc.vector.drain()
        nc.scalar.drain()
        nc.gpsimd.drain()
        if x_hist[s] is not None:
            p.wait_counts(nc.sync, x_hist[s])
        ld = p.inc(nc.sync.dma_start(x_sb[:, s, :, :], xin[:, :, t0:t0 + TB]), f"LX{s}", dma=True)
        xt = [x_sb[:, s, c, :] for c in range(8)]
        inv_bc, o_bc, pe_bc = ln.stats(xt, {f"LX{s}": ld})
        p.wait(nc.vector, "PE", pe_bc)
        if blk_hist is not None:
            p.wait_counts(nc.vector, {"PE": blk_hist["PE"]})
        for c in range(8):
            ln_apply(p, xt[c], xn_sb[:, c, :], inv_bc, o_bc, scr[:, :])
        xnv = p.cnt["VE"]
        ln.mark_used()
        p.wait(nc.tensor, "VE", xnv)
        for oc in range(32):
            bank = ps_mm[oc % 2]
            p.wait(nc.tensor, "SC", ps_hist[oc % 2])
            for ic in range(8):
                inst = nc.tensor.matmul(bank[:, :TB], w1[:, ic, oc * 128:(oc + 1) * 128],
                                        xn_sb[:, ic, :], start=(ic == 0), stop=(ic == 7))
            mm = p.inc(inst, "PE")
            p.wait(nc.scalar, "PE", mm)
            ps_hist[oc % 2] = p.inc(
                nc.scalar.activation(h_sb[:, oc, :], bank[:, :TB], AF.Relu,
                                     bias=b1[:, oc:oc + 1]), "SC")
        hsc = p.cnt["SC"]
        p.wait(nc.tensor, "SC", hsc)
        if blk_hist is not None:
            p.wait(nc.scalar, "ST", blk_hist["ST"])
        sc_o = 0
        for oc in range(8):
            bank = ps_mm[oc % 2]
            p.wait(nc.tensor, "SC", ps_hist[oc % 2])
            for ic in range(32):
                inst = nc.tensor.matmul(bank[:, :TB], w2[:, ic, oc * 128:(oc + 1) * 128],
                                        h_sb[:, ic, :], start=(ic == 0), stop=(ic == 31))
            mm = p.inc(inst, "PE")
            p.wait(nc.scalar, "PE", mm)
            sc_o = p.inc(nc.scalar.activation(o_sb[:, oc, :], bank[:, :TB], AF.Identity,
                                              bias=b2[:, oc:oc + 1]), "SC")
            ps_hist[oc % 2] = sc_o
        ot = [o_sb[:, c, :] for c in range(8)]
        inv2, o2, pe_bc2 = ln.stats(ot, {"SC": sc_o})
        p.wait(nc.vector, "PE", pe_bc2)
        for c in range(8):
            nc.vector.tensor_mul(scr[:, :], ot[c], inv2)
            nc.vector.drain()
            nc.vector.tensor_add(scr[:, :], scr[:, :], o2)
            nc.vector.drain()
            inst = nc.vector.tensor_add(o_sb[:, c, :], scr[:, :], xt[c])
            nc.vector.drain()
        res_ve = p.inc(inst, "VE")
        ln.mark_used()
        p.wait(nc.sync, "VE", res_ve)
        p.inc(nc.sync.dma_start(xout[:, :, t0:t0 + TB], o_sb[:, :, :]), "ST", dma=True)
        x_hist[s] = {"VE": res_ve, "PE": p.cnt["PE"], "SC": p.cnt["SC"]}
        blk_hist = p.snap()
    p.finish()
    return p




def emit_l1_pre(p: Prog):
    nc = p.nc
    xin = ch_major(p.stream("xin", (D, T)))
    xout = ch_major(p.stream("xout", (D, T), out=True))
    g = load_small(p, "M_innorm_g", (128, 8))
    bb = load_small(p, "M_innorm_b", (128, 8))
    p.wait(nc.tensor, "LD")
    p.wait(nc.scalar, "LD")
    p.wait(nc.vector, "LD")
    x_sb = p.sbuf("x_sb", (128, 2, 8, TB), F32)
    o_sb = p.sbuf("o_sb", (128, 2, 8, TB), F32)
    scr = p.sbuf("scr", (128, TB), F32)
    ln = LN(p, "pre_ln")
    x_hist = [None, None]
    for tau in range(NT):
        sl = tau % 2
        t0 = tau * TB
        nc.vector.drain()
        nc.scalar.drain()
        nc.gpsimd.drain()
        if x_hist[sl] is not None:
            p.wait_counts(nc.sync, x_hist[sl])
        ld = p.inc(nc.sync.dma_start(x_sb[:, sl, :, :], xin[:, :, t0:t0 + TB]), f"LX{sl}", dma=True)
        xt = [x_sb[:, sl, c, :] for c in range(8)]
        inv_bc, o_bc, pe_bc = ln.stats(xt, {f"LX{sl}": ld})
        p.wait(nc.vector, "PE", pe_bc)
        for c in range(8):
            nc.vector.tensor_mul(scr[:, :], xt[c], inv_bc)
            nc.vector.drain()
            nc.vector.tensor_add(scr[:, :], scr[:, :], o_bc)
            nc.vector.drain()
            inst = nc.vector.tensor_scalar(o_sb[:, sl, c, :], scr[:, :], g[:, c:c + 1],
                                           bb[:, c:c + 1], ALU.mult, ALU.add)
            nc.vector.drain()
        ve = p.inc(inst, "VE")
        ln.mark_used()
        p.wait(nc.sync, "VE", ve)
        st = p.inc(nc.sync.dma_start(xout[:, :, t0:t0 + TB], o_sb[:, sl, :, :]), f"ST{sl}", dma=True)
        x_hist[sl] = {"VE": ve, "PE": p.cnt["PE"], "SC": p.cnt["SC"], f"ST{sl}": st}
    p.finish()
    return p


TBM = 512
NTM = T // TBM
MAMBA_DEBUG = False


def emit_mamba(p: Prog, l):
    nc = p.nc
    DI = 16
    NST = 16
    xin = ch_major(p.stream("xin", (D, T)))
    xout = ch_major(p.stream("xout", (D, T), out=True))
    if f"M{l}_Win" in p.wbf:
        win_d = p.wbf[f"M{l}_Win"].rearrange("p (o i c) -> p o i c", o=32, i=8)
        wout_d = p.wbf[f"M{l}_Wout"].rearrange("p (o i c) -> p o i c", o=8, i=16)
    else:
        win_d = p.dram_in(f"M{l}_Win_bf", (128, 32 * 8 * 128), BF16).rearrange(
            "p (o i c) -> p o i c", o=32, i=8)
        wout_d = p.dram_in(f"M{l}_Wout_bf", (128, 8 * 16 * 128), BF16).rearrange(
            "p (o i c) -> p o i c", o=8, i=16)
    z0h_early = [p.sbuf("z0h", (128, 16, 3), BF16), p.sbuf("z1h", (128, 16, 3), BF16)]
    wx = load_w_bf(p, f"M{l}_Wx", 16, 128)
    ln_g = load_small(p, f"M{l}_ln_g", (128, 8))
    ln_b = load_small(p, f"M{l}_ln_b", (128, 8))
    cw = load_small(p, f"M{l}_conv_w", (128, 2, 16, 4))
    cb = load_small(p, f"M{l}_conv_b", (128, 2, 16))
    dtw = load_small(p, f"M{l}_dt_w", (128, 16))
    dtb = load_small(p, f"M{l}_dt_b", (128, 16))
    Dv = load_small(p, f"M{l}_D", (128, 16))
    mg = load_small(p, f"M{l}_mln_g", (128, 16))
    mb = load_small(p, f"M{l}_mln_b", (128, 16))
    ident = load_small(p, "ident128", (128, 128), BF16)
    sel = load_small(p, "sel_rows", (128, 33, 128), BF16)
    zz = p.dram_in("zeros96", (128, 96), BF16)
    p.inc(nc.sync.dma_start(z0h_early[0][:, :, :], zz[:, 0:48].rearrange("p (a b) -> p a b", b=3)),
          "LD", dma=True)
    p.inc(nc.sync.dma_start(z0h_early[1][:, :, :], zz[:, 48:96].rearrange("p (a b) -> p a b", b=3)),
          "LD", dma=True)
    ldw = p.cnt["LD"]

    dg = p.sbuf("dg", (128, 2, 16, 4, 128), BF16)
    p.wait(nc.vector, "LD", ldw)
    for cv in range(2):
        for j in range(16):
            for k in range(4):
                inst = nc.vector.tensor_scalar(dg[:, cv, j, k, :], ident[:, :],
                                               cw[:, cv, j, k:k + 1], None, ALU.mult)
    dgv = p.inc(inst, "VE")
    p.wait(nc.tensor, "VE", dgv)

    x_sb = p.sbuf("x_sb", (128, 2, 8, TBM), F32)
    xn_sb = p.sbuf("xn_sb", (128, 8, TBM), BF16)
    win_sb = p.sbuf("win_sb", (128, 2, 8, 128), BF16)
    wout_sb = p.sbuf("wout_sb", (128, 2, 16, 128), BF16)
    z0 = p.sbuf("z0", (128, 3 + TBM), BF16)
    z1 = p.sbuf("z1", (128, 3 + TBM), BF16)
    z0h, z1h = z0h_early
    z2 = p.sbuf("z2", (128, 16, TBM), BF16)
    sig = p.sbuf("sig", (128, 16, TBM), BF16)
    praw = p.sbuf("praw", (128, TBM), F32)
    praw_b = p.sbuf("praw_b", (128, TBM), BF16)
    dt_sb = p.sbuf("dt_sb", (128, TBM), F32)
    u_sb = p.sbuf("u_sb", (128, TBM), F32)
    Bbc = p.sbuf("Bbc", (128, 16, TBM), BF16)
    a_sb = p.sbuf("a_sb", (128, 2, TBM), F32)
    b_sb = p.sbuf("b_sb", (128, 2, TBM), F32)
    h_sb = p.sbuf("h_sb", (128, 2, TBM), F32)
    yh_sb = p.sbuf("yh_sb", (128, 2, TBM), BF16)
    hc = p.sbuf("hc", (128, 16, 16), F32)
    y_sb = p.sbuf("y_sb", (128, 16, TBM), BF16)
    scr = p.sbuf("scr", (128, TBM), F32)
    scr2 = p.sbuf("scr2", (128, TBM), F32)
    ln = LN(p, "m_ln")

    ps_mm = [p.psum(f"m_mm{i}", (128, 512), F32) for i in range(2)]
    ps_dtr = p.psum("m_dtr", (128, 512), F32)
    ps_cbc = p.psum("m_cbc", (128, 512), F32)
    ps_y = p.psum("m_y", (128, 512), F32)

    if MAMBA_DEBUG:
        dbg = {
            "dbg_xn": ch_major(p.dram_out("dbg_xn", (1024, T), BF16)),
            "dbg_z2": ch_major(p.dram_out("dbg_z2", (2048, T), BF16)),
            "dbg_sig": ch_major(p.dram_out("dbg_sig", (2048, T), BF16)),
            "dbg_praw": p.dram_out("dbg_praw", (128, T), F32),
            "dbg_y": ch_major(p.dram_out("dbg_y", (2048, T), BF16)),
            "dbg_dt": p.dram_out("dbg_dt", (128, T), F32),
        }

    p.wait(nc.scalar, "LD")
    p.wait(nc.tensor, "LD")
    x_hist = [None, None]
    ps_guard = [{}, {}]
    dtr_guard = {}
    cbc_guard = {}
    y_guard = {}
    prev_ygv = 0
    prev_mln_sc = 0
    prev_yD_ve = 0
    w_guard = [{}, {}]
    wo_guard = [{}, {}]
    zh_guard = {}
    for tau in range(NTM):
        sl = tau % 2
        t0 = tau * TBM
        nc.vector.drain()
        nc.scalar.drain()
        nc.gpsimd.drain()
        if x_hist[sl] is not None:
            p.wait_counts(nc.sync, x_hist[sl])
        ld = p.inc(nc.sync.dma_start(x_sb[:, sl, :, :], xin[:, :, t0:t0 + TBM]), f"LX{sl}", dma=True)
        xt = [x_sb[:, sl, c, :] for c in range(8)]
        if tau > 0:
            p.wait_counts(nc.vector, {"PE": x_hist[1 - sl]["PE"]})
        for hh in range(2):
            xth = [x_sb[:, sl, c, hh * 256:(hh + 1) * 256] for c in range(8)]
            inv_bc, o_bc, pe_bc = ln.stats(xth, {f"LX{sl}": ld})
            p.wait(nc.vector, "PE", pe_bc)
            for c in range(8):
                ho = hh * 256
                nc.vector.tensor_mul(scr[:, :256], xth[c], inv_bc)
                nc.vector.drain()
                nc.vector.tensor_add(scr[:, :256], scr[:, :256], o_bc)
                nc.vector.drain()
                inst = nc.vector.tensor_scalar(xn_sb[:, c, ho:ho + 256], scr[:, :256],
                                               ln_g[:, c:c + 1], ln_b[:, c:c + 1],
                                               ALU.mult, ALU.add)
                nc.vector.drain()
            p.inc(inst, "VE")
            ln.mark_used()
        xnv = p.cnt["VE"]
        p.wait(nc.tensor, "VE", xnv)
        for oc in range(32):
            ws = oc % 2
            if x_hist[sl] is not None or True:
                p.wait_counts(nc.sync, w_guard[ws])
            wld = p.inc(nc.sync.dma_start(win_sb[:, ws, :, :], win_d[:, oc, :, :]), f"LW{ws}", dma=True)
            bank = ps_mm[oc % 2]
            p.wait_counts(nc.tensor, ps_guard[oc % 2])
            p.wait(nc.tensor, f"LW{ws}", wld)
            for ic in range(8):
                inst = nc.tensor.matmul(bank[:, :TBM], win_sb[:, ws, ic, :], xn_sb[:, ic, :],
                                        start=(ic == 0), stop=(ic == 7))
            mm = p.inc(inst, "PE")
            w_guard[ws] = {"PE": mm}
            if oc < 16:
                j = oc
                p.wait(nc.scalar, "PE", mm)
                p.wait_counts(nc.scalar, zh_guard)
                nc.scalar.copy(z0[:, 0:3], z0h[:, j, :])
                nc.scalar.drain()
                sc0 = p.inc(nc.scalar.activation(z0[:, 3:3 + TBM], bank[:, :TBM], AF.Silu), "SC")
                nc.scalar.drain()
                nc.scalar.copy(z0h[:, j, :], z0[:, TBM:TBM + 3])
                ps_guard[oc % 2] = {"SC": sc0}
                bank2 = ps_mm[(oc + 1) % 2]
                p.wait_counts(nc.tensor, ps_guard[(oc + 1) % 2])
                p.wait(nc.tensor, "SC", sc0)
                for k in range(4):
                    inst = nc.tensor.matmul(bank2[:, :TBM], dg[:, 0, j, k, :], z0[:, k:k + TBM],
                                            start=(k == 0), stop=(k == 3))
                mm2 = p.inc(inst, "PE")
                p.wait(nc.scalar, "PE", mm2)
                nc.scalar.copy(z1[:, 0:3], z1h[:, j, :])
                nc.scalar.drain()
                sc1 = p.inc(nc.scalar.activation(z1[:, 3:3 + TBM], bank2[:, :TBM], AF.Silu,
                                                 bias=cb[:, 0, j:j + 1]), "SC")
                nc.scalar.drain()
                nc.scalar.copy(z1h[:, j, :], z1[:, TBM:TBM + 3])
                zh_guard = {"SC": sc1}
                ps_guard[(oc + 1) % 2] = {"SC": sc1}
                bank3 = ps_mm[oc % 2]
                p.wait_counts(nc.tensor, ps_guard[oc % 2])
                p.wait(nc.tensor, "SC", sc1)
                for k in range(4):
                    inst = nc.tensor.matmul(bank3[:, :TBM], dg[:, 1, j, k, :], z1[:, k:k + TBM],
                                            start=(k == 0), stop=(k == 3))
                mm3 = p.inc(inst, "PE")
                p.wait(nc.scalar, "PE", mm3)
                p.wait(nc.scalar, "VE", prev_yD_ve)
                sc2 = p.inc(nc.scalar.activation(z2[:, j, :], bank3[:, :TBM], AF.Silu,
                                                 bias=cb[:, 1, j:j + 1]), "SC")
                ps_guard[oc % 2] = {"SC": sc2}
            else:
                j = oc - 16
                p.wait(nc.scalar, "PE", mm)
                p.wait(nc.scalar, "VE", prev_ygv)
                sc = p.inc(nc.scalar.activation(sig[:, j, :], bank[:, :TBM], AF.Sigmoid), "SC")
                ps_guard[oc % 2] = {"SC": sc}
        z2sc = p.cnt["SC"]
        if MAMBA_DEBUG:
            p.wait(nc.sync, "VE", xnv)
            p.inc(nc.sync.dma_start(dbg["dbg_xn"][:, :, t0:t0 + TBM], xn_sb[:, :, :]), "DBG", dma=True)
            p.wait(nc.sync, "SC", z2sc)
            p.inc(nc.sync.dma_start(dbg["dbg_z2"][:, :, t0:t0 + TBM], z2[:, :, :]), "DBG", dma=True)
            p.inc(nc.sync.dma_start(dbg["dbg_sig"][:, :, t0:t0 + TBM], sig[:, :, :]), "DBG", dma=True)
        bank = ps_mm[0]
        p.wait_counts(nc.tensor, ps_guard[0])
        p.wait(nc.tensor, "SC", z2sc)
        for ic in range(16):
            inst = nc.tensor.matmul(bank[:, :TBM], wx[:, ic, :], z2[:, ic, :],
                                    start=(ic == 0), stop=(ic == 15))
        mmx = p.inc(inst, "PE")
        p.wait(nc.scalar, "PE", mmx)
        scp = p.inc(nc.scalar.copy(praw[:, :], bank[:, :TBM]), "SC")
        scpb = p.inc(nc.scalar.copy(praw_b[:, :], bank[:, :TBM]), "SC")
        ps_guard[0] = {"SC": scpb}
        p.wait_counts(nc.tensor, dtr_guard)
        p.wait(nc.tensor, "SC", scpb)
        mmd = p.inc(nc.tensor.matmul(ps_dtr[:, :TBM], sel[:, 0, :], praw_b[:, :],
                                     start=True, stop=True), "PE")
        p.wait(nc.scalar, "GP", p.cnt["GP"])
        for n in range(NST):
            p.wait_counts(nc.tensor, cbc_guard)
            mmb = p.inc(nc.tensor.matmul(ps_cbc[:, :TBM], sel[:, 1 + n, :], praw_b[:, :],
                                         start=True, stop=True), "PE")
            p.wait(nc.scalar, "PE", mmb)
            cbc_guard = {"SC": p.inc(nc.scalar.copy(Bbc[:, n, :], ps_cbc[:, :TBM]), "SC")}
        bbc_sc = p.cnt["SC"]
        for j in range(16):
            p.wait(nc.scalar, "PE", mmd)
            p.wait_counts(nc.scalar, {"VE": p.cnt["VE"]})
            nc.scalar.activation(scr2[:, :], ps_dtr[:, :TBM], AF.Abs,
                                 scale=dtw[:, j:j + 1], bias=dtb[:, j:j + 1])
            nc.scalar.drain()
            nc.scalar.activation(scr2[:, :], scr2[:, :], AF.Exp, scale=-1.0)
            nc.scalar.drain()
            nc.scalar.activation(scr2[:, :], scr2[:, :], AF.Ln, bias=1.0)
            scdt = p.inc(nc.scalar.activation(dt_sb[:, :], ps_dtr[:, :TBM], AF.Relu,
                                 scale=dtw[:, j:j + 1], bias=dtb[:, j:j + 1]), "SC")
            p.wait(nc.vector, "SC", scdt)
            vedt = p.inc(nc.vector.tensor_add(dt_sb[:, :], dt_sb[:, :], scr2[:, :]), "VE")
            nc.vector.drain()
            p.wait(nc.vector, "GP", p.cnt["GP"])
            veu = p.inc(nc.vector.tensor_mul(u_sb[:, :], dt_sb[:, :], z2[:, j, :]), "VE")
            p.wait(nc.gpsimd, "VE", veu)
            p.wait(nc.gpsimd, "SC", bbc_sc)
            ab_guard = [0, 0]
            for n in range(NST):
                sn = n % 2
                p.wait(nc.scalar, "VE", max(vedt, ab_guard[sn]))
                sca = p.inc(nc.scalar.activation(a_sb[:, sn, :], dt_sb[:, :], AF.Identity,
                                                 scale=-float(n + 1), bias=1.0), "SC")
                if ab_guard[sn]:
                    p.wait(nc.gpsimd, "VE", ab_guard[sn])
                gpb = p.inc(nc.gpsimd.tensor_tensor(b_sb[:, sn, :], u_sb[:, :], Bbc[:, n, :],
                                                    ALU.mult), "GP")
                p.wait(nc.vector, "SC", sca)
                p.wait(nc.vector, "GP", gpb)
                init = 0.0 if tau == 0 else hc[:, j, n:n + 1]
                nc.vector.tensor_tensor_scan(h_sb[:, sn, :], a_sb[:, sn, :], b_sb[:, sn, :],
                                             init, ALU.mult, ALU.add)
                nc.vector.drain()
                nc.vector.tensor_copy(hc[:, j, n:n + 1], h_sb[:, sn, TBM - 1:TBM])
                p.wait_counts(nc.tensor, cbc_guard)
                mmc = p.inc(nc.tensor.matmul(ps_cbc[:, :TBM], sel[:, 17 + n, :], praw_b[:, :],
                                             start=True, stop=True), "PE")
                p.wait(nc.vector, "PE", mmc)
                veyh = p.inc(nc.vector.tensor_mul(yh_sb[:, sn, :], h_sb[:, sn, :],
                                                  ps_cbc[:, :TBM]), "VE")
                cbc_guard = {"VE": veyh}
                ab_guard[sn] = veyh
                if n == 0:
                    p.wait_counts(nc.tensor, y_guard)
                p.wait(nc.tensor, "VE", veyh)
                mmy = p.inc(nc.tensor.matmul(ps_y[:, :TBM], ident[:, :], yh_sb[:, sn, :],
                                             start=(n == 0), stop=(n == NST - 1)), "PE")
            p.wait(nc.vector, "PE", mmy)
            p.wait(nc.vector, "SC", prev_mln_sc)
            y_guard = {"VE": p.inc(nc.vector.scalar_tensor_tensor(
                y_sb[:, j, :], z2[:, j, :], Dv[:, j:j + 1], ps_y[:, :TBM],
                ALU.mult, ALU.add), "VE")}
        if MAMBA_DEBUG:
            p.wait(nc.sync, "SC", scp)
            p.inc(nc.sync.dma_start(dbg["dbg_praw"][:, t0:t0 + TBM], praw[:, :]), "DBG", dma=True)
            p.wait_counts(nc.sync, y_guard)
            yb = y_sb[:, :, :].rearrange("p c t -> p (c t)")
            p.inc(nc.sync.dma_start(dbg["dbg_y"][:, :, t0:t0 + TBM], y_sb[:, :, :]), "DBG", dma=True)
            p.inc(nc.sync.dma_start(dbg["dbg_dt"][:, t0:t0 + TBM], dt_sb[:, :]), "DBG", dma=True)
        dtr_guard = {"SC": p.cnt["SC"]}
        prev_yD_ve = p.cnt["VE"]
        yv = p.cnt["VE"]
        for hh in range(2):
            ho = hh * 256
            yth = [y_sb[:, c, ho:ho + 256] for c in range(16)]
            inv_bc, o_bc, pe_bc = ln.stats(yth, {"VE": yv})
            p.wait(nc.vector, "PE", pe_bc)
            for c in range(16):
                nc.vector.tensor_mul(scr[:, :256], yth[c], inv_bc)
                nc.vector.drain()
                nc.vector.tensor_add(scr[:, :256], scr[:, :256], o_bc)
                nc.vector.drain()
                nc.vector.tensor_scalar(scr[:, :256], scr[:, :256], mg[:, c:c + 1],
                                        mb[:, c:c + 1], ALU.mult, ALU.add)
                nc.vector.drain()
                inst = nc.vector.tensor_mul(y_sb[:, c, ho:ho + 256], scr[:, :256],
                                            sig[:, c, ho:ho + 256])
                nc.vector.drain()
            p.inc(inst, "VE")
            ln.mark_used()
        prev_mln_sc = p.cnt["SC"]
        ygv = p.cnt["VE"]
        prev_ygv = ygv
        p.wait(nc.tensor, "VE", ygv)
        if x_hist[1 - sl] is not None:
            p.wait(nc.scalar, f"ST{1 - sl}", x_hist[1 - sl].get(f"ST{1 - sl}", 0))
        resv = 0
        for oc in range(8):
            ws = oc % 2
            p.wait_counts(nc.sync, wo_guard[ws])
            wld = p.inc(nc.sync.dma_start(wout_sb[:, ws, :, :], wout_d[:, oc, :, :]), f"LO{ws}", dma=True)
            bank = ps_mm[oc % 2]
            p.wait_counts(nc.tensor, ps_guard[oc % 2])
            p.wait(nc.tensor, f"LO{ws}", wld)
            for ic in range(16):
                inst = nc.tensor.matmul(bank[:, :TBM], wout_sb[:, ws, ic, :], y_sb[:, ic, :],
                                        start=(ic == 0), stop=(ic == 15))
            mm = p.inc(inst, "PE")
            wo_guard[ws] = {"PE": mm}
            p.wait(nc.vector, "PE", mm)
            resv = p.inc(nc.vector.tensor_add(x_sb[:, sl, oc, :], x_sb[:, sl, oc, :],
                                              bank[:, :TBM]), "VE")
            ps_guard[oc % 2] = {"VE": resv}
        p.wait(nc.sync, "VE", resv)
        st = p.inc(nc.sync.dma_start(xout[:, :, t0:t0 + TBM], x_sb[:, sl, :, :]), f"ST{sl}", dma=True)
        x_hist[sl] = {"VE": resv, "PE": p.cnt["PE"], "SC": p.cnt["SC"], f"ST{sl}": st}
    p.finish()
    return p




def emit_opmlp(p: Prog):
    nc = p.nc
    xin = ch_major(p.stream("xin", (D, T)))
    tin = ch_major(p.stream("tin", (D, T)))
    rin = ch_major(p.stream("rin", (D, T)))
    xout = ch_major(p.stream("xout", (D, T), out=True))
    w1 = load_w_bf(p, "OP1", 8, 1024)
    w2 = load_w_bf(p, "OP2", 8, 1024)
    b1 = load_small(p, "OP1_b", (128, 8))
    b2 = load_small(p, "OP2_b", (128, 8))
    p.wait(nc.tensor, "LD")
    p.wait(nc.scalar, "LD")
    p.wait(nc.vector, "LD")

    x_sb = p.sbuf("x_sb", (128, 2, 8, TB), F32)
    t_sb = p.sbuf("t_sb", (128, 2, 8, TB), F32)
    r_sb = p.sbuf("r_sb", (128, 2, 8, TB), F32)
    xb = p.sbuf("xb", (128, 8, TB), BF16)
    h_sb = p.sbuf("h_sb", (128, 8, TB), BF16)
    o_sb = p.sbuf("o_sb", (128, 8, TB), F32)
    scr = p.sbuf("scr", (128, TB), F32)
    ln = LN(p, "op_ln")
    ps_mm = [p.psum(f"op_mm{i}", (128, 512), F32) for i in range(2)]

    x_hist = [None, None]
    ps_guard = [{}, {}]
    blk_hist = None
    for tau in range(NT):
        nc.vector.drain()
        nc.scalar.drain()
        sl = tau % 2
        t0 = tau * TB
        if x_hist[sl] is not None:
            p.wait_counts(nc.sync, x_hist[sl])
        ld = p.inc(nc.sync.dma_start(x_sb[:, sl, :, :], xin[:, :, t0:t0 + TB]), f"LX{sl}", dma=True)
        ld2 = p.inc(nc.sync.dma_start(t_sb[:, sl, :, :], tin[:, :, t0:t0 + TB]), f"LT{sl}", dma=True)
        ld3 = p.inc(nc.sync.dma_start(r_sb[:, sl, :, :], rin[:, :, t0:t0 + TB]), f"LR{sl}", dma=True)
        p.wait(nc.scalar, f"LX{sl}", ld)
        if blk_hist is not None:
            p.wait_counts(nc.scalar, {"PE": blk_hist["PE"]})
        for c in range(8):
            inst = nc.scalar.copy(xb[:, c, :], x_sb[:, sl, c, :])
        xbv = p.inc(inst, "SC")
        p.wait(nc.tensor, "SC", xbv)
        for oc in range(8):
            bank = ps_mm[oc % 2]
            p.wait_counts(nc.tensor, ps_guard[oc % 2])
            for ic in range(8):
                inst = nc.tensor.matmul(bank[:, :TB], w1[:, ic, oc * 128:(oc + 1) * 128],
                                        xb[:, ic, :], start=(ic == 0), stop=(ic == 7))
            mm = p.inc(inst, "PE")
            p.wait(nc.scalar, "PE", mm)
            ps_guard[oc % 2] = {"SC": p.inc(
                nc.scalar.activation(h_sb[:, oc, :], bank[:, :TB], AF.Gelu,
                                     bias=b1[:, oc:oc + 1]), "SC")}
        hsc = p.cnt["SC"]
        p.wait(nc.tensor, "SC", hsc)
        p.wait(nc.vector, f"LT{sl}", ld2)
        p.wait(nc.vector, "ST0", p.cnt.get("ST0", 0))
        p.wait(nc.vector, "ST1", p.cnt.get("ST1", 0))
        ve_o = 0
        for oc in range(8):
            bank = ps_mm[oc % 2]
            p.wait_counts(nc.tensor, ps_guard[oc % 2])
            for ic in range(8):
                inst = nc.tensor.matmul(bank[:, :TB], w2[:, ic, oc * 128:(oc + 1) * 128],
                                        h_sb[:, ic, :], start=(ic == 0), stop=(ic == 7))
            mm = p.inc(inst, "PE")
            p.wait(nc.vector, "PE", mm)
            ve_o = p.inc(nc.vector.scalar_tensor_tensor(
                o_sb[:, oc, :], bank[:, :TB], b2[:, oc:oc + 1], t_sb[:, sl, oc, :],
                ALU.add, ALU.add), "VE")
            nc.vector.drain()
            ps_guard[oc % 2] = {"VE": ve_o}
        ot = [o_sb[:, c, :] for c in range(8)]
        inv2, o2, pe_bc2 = ln.stats(ot, {"VE": ve_o})
        p.wait(nc.vector, "PE", pe_bc2)
        p.wait(nc.vector, f"LR{sl}", ld3)
        for c in range(8):
            nc.vector.tensor_mul(scr[:, :], ot[c], inv2)
            nc.vector.drain()
            nc.vector.tensor_add(scr[:, :], scr[:, :], o2)
            nc.vector.drain()
            inst = nc.vector.tensor_add(o_sb[:, c, :], scr[:, :], r_sb[:, sl, c, :])
            nc.vector.drain()
        res_ve = p.inc(inst, "VE")
        ln.mark_used()
        p.wait(nc.sync, "VE", res_ve)
        p.inc(nc.sync.dma_start(xout[:, :, t0:t0 + TB], o_sb[:, :, :]), f"ST{sl}", dma=True)
        x_hist[sl] = {"VE": res_ve, "PE": p.cnt["PE"], "SC": p.cnt["SC"]}
        blk_hist = p.snap()
    p.finish()
    return p


def emit_ln_only(p: Prog):
    nc = p.nc
    xin = ch_major(p.stream("xin", (D, T)))
    xout = ch_major(p.stream("xout", (D, T), out=True))
    x_sb = p.sbuf("x_sb", (128, 2, 8, TB), F32)
    o_sb = p.sbuf("o_sb", (128, 2, 8, TB), F32)
    ln = LN(p, "lno")
    x_hist = [None, None]
    for tau in range(NT):
        nc.vector.drain()
        nc.scalar.drain()
        sl = tau % 2
        t0 = tau * TB
        if x_hist[sl] is not None:
            p.wait_counts(nc.sync, x_hist[sl])
        ld = p.inc(nc.sync.dma_start(x_sb[:, sl, :, :], xin[:, :, t0:t0 + TB]), f"LX{sl}", dma=True)
        xt = [x_sb[:, sl, c, :] for c in range(8)]
        inv_bc, o_bc, pe_bc = ln.stats(xt, {f"LX{sl}": ld})
        p.wait(nc.vector, "PE", pe_bc)
        for c in range(8):
            ve = ln_apply(p, xt[c], o_sb[:, sl, c, :], inv_bc, o_bc, x_sb[:, sl, c, :] if False else o_sb[:, sl, c, :])
        ln.mark_used()
        p.wait(nc.sync, "VE", ve)
        st = p.inc(nc.sync.dma_start(xout[:, :, t0:t0 + TB], o_sb[:, sl, :, :]), f"ST{sl}", dma=True)
        x_hist[sl] = {"VE": ve, "PE": p.cnt["PE"], "SC": p.cnt["SC"], f"ST{sl}": st}
    p.finish()
    return p


def emit_final(p: Prog):
    nc = p.nc
    xin = ch_major(p.stream("xin", (D, T)))
    rin = ch_major(p.stream("rin", (D, T)))
    out = p.dram_out("out", (T, D))
    ident = load_small(p, "ident128f", (128, 128), F32)
    p.wait(nc.tensor, "LD")
    x_sb = p.sbuf("x_sb", (128, 2, 8, TB), F32)
    r_sb = p.sbuf("r_sb", (128, 2, 8, TB), F32)
    o_sb = p.sbuf("o_sb", (128, 8, TB), F32)
    ot_sb = p.sbuf("ot_sb", (128, 2, 8, 128), F32)
    scr = p.sbuf("scr", (128, TB), F32)
    ln = LN(p, "fin_ln")
    ps_t = [p.psum(f"fin_t{i}", (128, 512), F32) for i in range(2)]
    x_hist = [None, None]
    tr_guard = [{}, {}]
    ot_guard = [{}, {}]
    o_guard = {}
    for tau in range(NT):
        nc.vector.drain()
        nc.scalar.drain()
        sl = tau % 2
        t0 = tau * TB
        if x_hist[sl] is not None:
            p.wait_counts(nc.sync, x_hist[sl])
        ld = p.inc(nc.sync.dma_start(x_sb[:, sl, :, :], xin[:, :, t0:t0 + TB]), f"LX{sl}", dma=True)
        ld2 = p.inc(nc.sync.dma_start(r_sb[:, sl, :, :], rin[:, :, t0:t0 + TB]), f"LR{sl}", dma=True)
        xt = [x_sb[:, sl, c, :] for c in range(8)]
        inv_bc, o_bc, pe_bc = ln.stats(xt, {f"LX{sl}": ld})
        p.wait(nc.vector, "PE", pe_bc)
        p.wait(nc.vector, f"LR{sl}", ld2)
        p.wait_counts(nc.vector, o_guard)
        for c in range(8):
            nc.vector.tensor_mul(scr[:, :], xt[c], inv_bc)
            nc.vector.drain()
            nc.vector.tensor_add(scr[:, :], scr[:, :], o_bc)
            nc.vector.drain()
            inst = nc.vector.tensor_add(o_sb[:, c, :], scr[:, :], r_sb[:, sl, c, :])
            nc.vector.drain()
        res_ve = p.inc(inst, "VE")
        ln.mark_used()
        p.wait(nc.tensor, "VE", res_ve)
        o_guard = {"PE": None}
        for half in range(2):
            for grp in range(2):
                bank = ps_t[grp]
                p.wait_counts(nc.tensor, tr_guard[grp])
                for cc in range(4):
                    c = grp * 4 + cc
                    inst = nc.tensor.transpose(bank[:, cc * 128:(cc + 1) * 128],
                                               o_sb[:, c, half * 128:(half + 1) * 128],
                                               ident[:, :])
                mm = p.inc(inst, "PE")
                p.wait(nc.scalar, "PE", mm)
                p.wait_counts(nc.scalar, ot_guard[sl])
                sc = p.inc(nc.scalar.copy(
                    ot_sb[:, sl, grp * 4:(grp + 1) * 4, :].rearrange("p a b -> p (a b)"),
                    bank[:, :]), "SC")
                tr_guard[grp] = {"SC": sc}
            p.wait(nc.sync, "SC", p.cnt["SC"])
            st = p.inc(nc.sync.dma_start(
                out[t0 + half * 128: t0 + (half + 1) * 128, :],
                ot_sb[:, sl, :, :].rearrange("p a b -> p (a b)")), f"ST{sl}{half}", dma=True)
            ot_guard[sl] = {f"ST{sl}{half}": st}
        o_guard = {"PE": p.cnt["PE"]}
        x_hist[sl] = {"VE": res_ve, "PE": p.cnt["PE"], "SC": p.cnt["SC"]}
    p.finish()
    return p




def emit_ffn_attn(p: Prog, i):
    nc = p.nc
    xin = ch_major(p.stream("xin", (D, T)))
    xout = ch_major(p.stream("xout", (D, T), out=True))
    wqkv = load_w_bf(p, f"F{i}_Wqkv", 8, 3072)
    wo = load_w_bf(p, f"F{i}_Wo", 8, 1024)
    qs = load_small(p, f"F{i}_qs", (128, 1))
    ks = load_small(p, f"F{i}_ks", (128, 1))
    qsp = load_small(p, f"F{i}_qsp", (128, 1))
    ksp = load_small(p, f"F{i}_ksp", (128, 1))
    rbd = load_small(p, "RbdT", (128, 128), BF16)
    idb = load_small(p, "ident128", (128, 128), BF16)
    bd2 = load_small(p, "bd2", (128, 2))
    bdt = load_small(p, "bdT", (2, 128))
    mka = load_small(p, "mask_add_t", (128, 2, 512))
    mka0 = load_small(p, "mask_add0_t", (128, 2, 256))
    rot_d = {nm: p.dram_in(nm, (128, T)) for nm in ("rot_qc", "rot_qs", "rot_kc", "rot_ks")}
    p.wait(nc.tensor, "LD")
    p.wait(nc.scalar, "LD")
    p.wait(nc.vector, "LD")

    x_sb = p.sbuf("x_sb", (128, 2, 8, 1 + TB), F32)
    xb = p.sbuf("xb", (128, 8, 1 + TB), BF16)
    q_sb = p.sbuf("q_sb", (128, 8, TB), F32)
    k_sb = p.sbuf("k_sb", (128, 8, TB), F32)
    v_sb = p.sbuf("v_sb", (128, 2, 8, TB), BF16)
    qf_sb = p.sbuf("qf_sb", (128, 8, TB), BF16)
    kf_sb = p.sbuf("kf_sb", (128, 2, 8, TB), BF16)
    vt_sb = p.sbuf("vt_sb", (128, 2, 16, 2, 64), BF16)
    rt_sb = p.sbuf("rt_sb", (128, 4, TB), F32)
    tq_sb = p.sbuf("tq_sb", (128, 4, TB), F32)
    sqq = p.sbuf("sqq", (128, TB), F32)
    nrm = p.sbuf("nrm", (2, 2, 8, TB), F32)
    qn_b = p.sbuf("qn_b", (128, TB), BF16)
    scr = p.sbuf("scr", (128, TB), F32)
    scr2 = p.sbuf("scr2", (128, TB), F32)
    sim_m = p.sbuf("sim_m", (128, 512), F32)
    attn_b = p.sbuf("attn_b", (128, 512), BF16)
    attn_n = p.sbuf("attn_n", (128, 512), BF16)
    mx = p.sbuf("mx", (128, 4), F32)
    at_sb = p.sbuf("at_sb", (128, 4, 128), BF16)
    ob_sb = p.sbuf("ob_sb", (128, 8, TB), BF16)

    ps_mm = [p.psum(f"fa_mm{k}", (128, 512), F32) for k in range(2)]
    ps_sim = p.psum("fa_sim", (128, 512), F32)
    ps_at = p.psum("fa_at", (128, 512), BF16)
    ps_av = [p.psum(f"fa_av{k}", (128, 256), F32) for k in range(2)]
    ps_bc = p.psum("fa_bc", (128, 512), F32)

    x_hist = [None, None]
    ps_guard = [{}, {}]
    sim_guard = {}
    at_guard = {}
    av_guard = [{}, {}]
    bc_guard = {}
    prev = {}
    for tau in range(NT):
        nc.vector.drain()
        nc.scalar.drain()
        sl = tau % 2
        t0 = tau * TB
        if x_hist[sl] is not None:
            p.wait_counts(nc.sync, x_hist[sl])
        if tau == 0:
            ld = p.inc(nc.sync.dma_start(x_sb[:, sl, :, 1:], xin[:, :, 0:TB]), f"LX{sl}", dma=True)
            p.wait(nc.vector, f"LX{sl}", ld)
            ve0 = p.inc(nc.vector.memset(x_sb[:, sl, :, 0:1], 0.0), "VE")
            xld = {f"LX{sl}": ld, "VE": ve0}
        else:
            ld = p.inc(nc.sync.dma_start(x_sb[:, sl, :, :], xin[:, :, t0 - 1:t0 + TB]),
                       f"LX{sl}", dma=True)
            xld = {f"LX{sl}": ld}
        ldr = 0
        for r, nm in enumerate(("rot_qc", "rot_qs", "rot_kc", "rot_ks")):
            p.wait_counts(nc.sync, prev.get("rt", {}))
            ldr = p.inc(nc.sync.dma_start(rt_sb[:, r, :], rot_d[nm][:, t0:t0 + TB]),
                        f"LQ{sl}", dma=True)
        p.wait(nc.vector, f"LQ{sl}", ldr)
        p.wait_counts(nc.vector, prev.get("tq", {}))
        nc.vector.tensor_scalar(tq_sb[:, 0, :], rt_sb[:, 0, :], qs[:, 0:1], None, ALU.mult)
        nc.vector.tensor_scalar(tq_sb[:, 1, :], rt_sb[:, 1, :], qsp[:, 0:1], None, ALU.mult)
        nc.vector.tensor_scalar(tq_sb[:, 2, :], rt_sb[:, 2, :], ks[:, 0:1], None, ALU.mult)
        tqv = p.inc(nc.vector.tensor_scalar(tq_sb[:, 3, :], rt_sb[:, 3, :], ksp[:, 0:1],
                                            None, ALU.mult), "VE")
        prev["rt"] = {"VE": tqv}
        p.wait_counts(nc.scalar, xld)
        p.wait_counts(nc.scalar, prev.get("xb", {}))
        for c in range(8):
            inst = nc.scalar.copy(xb[:, c, :], x_sb[:, sl, c, :])
        xbv = p.inc(inst, "SC")
        prev["xb"] = {}
        p.wait(nc.tensor, "SC", xbv)
        for oc in range(24):
            bank = ps_mm[oc % 2]
            p.wait_counts(nc.tensor, ps_guard[oc % 2])
            for ic in range(8):
                off = 1 if ic < 4 else 0
                inst = nc.tensor.matmul(bank[:, :TB], wqkv[:, ic, oc * 128:(oc + 1) * 128],
                                        xb[:, ic, off:off + TB], start=(ic == 0), stop=(ic == 7))
            mm = p.inc(inst, "PE")
            p.wait(nc.scalar, "PE", mm)
            if oc < 8:
                p.wait_counts(nc.scalar, prev.get("q", {}))
                sc = p.inc(nc.scalar.copy(q_sb[:, oc, :], bank[:, :TB]), "SC")
            elif oc < 16:
                p.wait_counts(nc.scalar, prev.get("k", {}))
                sc = p.inc(nc.scalar.copy(k_sb[:, oc - 8, :], bank[:, :TB]), "SC")
            else:
                p.wait_counts(nc.scalar, prev.get("v", {}))
                sc = p.inc(nc.scalar.copy(v_sb[:, sl, oc - 16, :], bank[:, :TB]), "SC")
            ps_guard[oc % 2] = {"SC": sc}
        qkv_sc = p.cnt["SC"]
        p.wait(nc.scalar, "SC", qkv_sc)
        for qk, src in ((0, q_sb), (1, k_sb)):
            for c in range(8):
                p.wait_counts(nc.scalar, {"PE": p.cnt["PE"]})
                nc.scalar.drain()
                ssc = p.inc(nc.scalar.activation(sqq[:, :], src[:, c, :], AF.Square), "SC")
                p.wait(nc.tensor, "SC", ssc)
                p.wait_counts(nc.tensor, bc_guard)
                mm = p.inc(nc.tensor.matmul(ps_bc[0:2, 0:TB], bd2[:, :], sqq[:, :],
                                            start=True, stop=True), "PE")
                p.wait(nc.scalar, "PE", mm)
                p.wait_counts(nc.scalar, prev.get("nrm", {}))
                bc_guard = {"SC": p.inc(nc.scalar.copy(nrm[0:2, qk, c, :], ps_bc[0:2, 0:TB]),
                                        "SC")}
        nsc = p.cnt["SC"]
        nflat = nrm[0:2, :, :, :].rearrange("p a c t -> p (a c t)")
        p.wait(nc.vector, "SC", nsc)
        vem = p.inc(nc.vector.tensor_scalar(nflat, nflat, 1e-24, None, ALU.max), "VE")
        p.wait(nc.scalar, "VE", vem)
        nc.scalar.drain()
        nc.scalar.activation(nflat, nflat, AF.Ln)
        nc.scalar.drain()
        ninv = p.inc(nc.scalar.activation(nflat, nflat, AF.Exp, scale=-0.5), "SC")
        prev["nrm"] = {"VE": 0}
        p.wait(nc.tensor, "SC", ninv)
        for qk, src, dst, ci, si in ((0, q_sb, qf_sb, 0, 1), (1, k_sb, kf_sb, 2, 3)):
            for c in range(8):
                p.wait_counts(nc.tensor, bc_guard)
                mmb = p.inc(nc.tensor.matmul(ps_bc[:, 0:TB], bdt[:, :], nrm[0:2, qk, c, :],
                                             start=True, stop=True), "PE")
                p.wait(nc.vector, "PE", mmb)
                p.wait(nc.vector, "VE", tqv)
                p.wait_counts(nc.vector, prev.get("qnb", {}))
                veq = p.inc(nc.vector.tensor_mul(qn_b[:, :], src[:, c, :], ps_bc[:, 0:TB]), "VE")
                nc.vector.drain()
                p.wait(nc.tensor, "VE", veq)
                p.wait_counts(nc.tensor, prev.get("rot", {}))
                mmr = p.inc(nc.tensor.matmul(ps_sim[:, 0:TB], rbd[:, :], qn_b[:, :],
                                             start=True, stop=True), "PE")
                bc_guard = {"VE": veq}
                nc.vector.tensor_mul(scr[:, :], qn_b[:, :], tq_sb[:, ci, :])
                nc.vector.drain()
                p.wait(nc.vector, "PE", mmr)
                nc.vector.tensor_mul(scr2[:, :], ps_sim[:, 0:TB], tq_sb[:, si, :])
                nc.vector.drain()
                if qk == 0:
                    inst = nc.vector.tensor_add(dst[:, c, :], scr[:, :], scr2[:, :])
                else:
                    inst = nc.vector.tensor_add(dst[:, sl, c, :], scr[:, :], scr2[:, :])
                nc.vector.drain()
                ver = p.inc(inst, "VE")
                prev["rot"] = {"VE": ver}
                prev["qnb"] = {"VE": ver}
        rot_ve = p.cnt["VE"]
        sim_guard = {"VE": rot_ve}
        p.wait(nc.tensor, "SC", qkv_sc)
        for h in range(16):
            pb = (h % 2) * 64
            ct = h // 2
            for jc in range(2):
                p.wait_counts(nc.tensor, at_guard)
                inst = nc.tensor.transpose(ps_at[0:128, 0:64],
                                           v_sb[:, sl, ct, jc * 128:(jc + 1) * 128][pb:pb + 64, :],
                                           idb[:, :][pb:pb + 64, pb:pb + 64])
                mm = p.inc(inst, "PE")
                p.wait(nc.scalar, "PE", mm)
                p.wait_counts(nc.scalar, prev.get("vt", {}))
                at_guard = {"SC": p.inc(nc.scalar.copy(vt_sb[:, sl, h, jc, :], ps_at[0:128, 0:64]),
                                        "SC")}
        prev["vt"] = {}
        vt_sc = p.cnt["SC"]
        p.wait(nc.tensor, "VE", rot_ve)
        p.wait(nc.tensor, "SC", vt_sc)
        for ct in range(8):
            avb = ps_av[ct % 2]
            p.wait_counts(nc.tensor, av_guard[ct % 2])
            for hh in range(2):
                h = ct * 2 + hh
                pb = hh * 64
                for m in range(2):
                    qT = qf_sb[:, ct, m * 128:(m + 1) * 128][pb:pb + 64, :]
                    wid = 512 if tau > 0 else 256
                    p.wait_counts(nc.tensor, sim_guard)
                    if tau > 0:
                        nc.tensor.matmul(ps_sim[:, 0:256], qT,
                                         kf_sb[:, 1 - sl, ct, :][pb:pb + 64, :],
                                         start=True, stop=True)
                    mms = p.inc(nc.tensor.matmul(
                        ps_sim[:, (wid - 256):wid] if tau > 0 else ps_sim[:, 0:256],
                        qT, kf_sb[:, sl, ct, :][pb:pb + 64, :],
                        start=True, stop=True), "PE")
                    p.wait(nc.vector, "PE", mms)
                    p.wait_counts(nc.vector, prev.get("simm", {}))
                    msk = mka[:, m, 0:wid] if tau > 0 else mka0[:, m, 0:wid]
                    nc.vector.scalar_tensor_tensor(sim_m[:, 0:wid], ps_sim[:, 0:wid], 8.0,
                                                   msk, ALU.mult, ALU.add)
                    nc.vector.drain()
                    vet = p.inc(nc.vector.reduce_max(mx[:, 1:2], sim_m[:, 0:wid],
                                                     AX.X, negate=True), "VE")
                    nc.vector.drain()
                    ven = vet
                    p.wait(nc.scalar, "VE", ven)
                    p.wait_counts(nc.scalar, prev.get("attnb", {}))
                    sce = p.inc(nc.scalar.activation(attn_b[:, 0:wid], sim_m[:, 0:wid], AF.Exp,
                                                     bias=mx[:, 1:2], accum_out=mx[:, 2:3]),
                                "SC")
                    p.wait(nc.vector, "SC", sce)
                    nc.vector.reciprocal(mx[:, 3:4], mx[:, 2:3])
                    nc.vector.drain()
                    vesc = p.inc(nc.vector.tensor_scalar(attn_n[:, 0:wid], attn_b[:, 0:wid],
                                                         mx[:, 3:4], None, ALU.mult), "VE")
                    prev["simm"] = {"SC": sce}
                    prev["attnb"] = {"VE": vesc}
                    sim_guard = {"VE": vet}
                    p.wait(nc.tensor, "VE", vesc)
                    nch = wid // 128
                    for jc in range(nch):
                        p.wait_counts(nc.tensor, at_guard)
                        inst = nc.tensor.transpose(ps_at[:, jc * 128:(jc + 1) * 128],
                                                   attn_n[:, jc * 128:(jc + 1) * 128],
                                                   idb[:, :])
                    mmt = p.inc(inst, "PE")
                    p.wait(nc.scalar, "PE", mmt)
                    p.wait_counts(nc.scalar, prev.get("at", {}))
                    sca = p.inc(nc.scalar.copy(
                        at_sb[:, 0:nch, :].rearrange("p a b -> p (a b)"),
                        ps_at[:, 0:wid]), "SC")
                    at_guard = {"SC": sca}
                    prev["at"] = {"PE": 0}
                    p.wait(nc.tensor, "SC", sca)
                    chunks = []
                    if tau > 0:
                        chunks = [(1 - sl, 0, 0), (1 - sl, 1, 1), (sl, 0, 2), (sl, 1, 3)]
                    else:
                        chunks = [(sl, 0, 0), (sl, 1, 1)]
                    for (bp, vjc, ajc) in chunks:
                        inst = nc.tensor.matmul(
                            avb[pb:pb + 64, m * 128:(m + 1) * 128],
                            vt_sb[:, bp, h, vjc, :], at_sb[:, ajc, :],
                            start=(ajc == chunks[0][2]), stop=(ajc == chunks[-1][2]))
                    mmv = p.inc(inst, "PE")
                    at_guard = {"PE": mmv}
            p.wait(nc.scalar, "PE", p.cnt["PE"])
            p.wait_counts(nc.scalar, prev.get("ob", {}))
            av_guard[ct % 2] = {"SC": p.inc(nc.scalar.copy(ob_sb[:, ct, :], avb[:, :]), "SC")}
        ob_sc = p.cnt["SC"]
        prev["ob"] = {}
        p.wait(nc.tensor, "SC", ob_sc)
        res_ve = 0
        for oc in range(8):
            bank = ps_mm[oc % 2]
            p.wait_counts(nc.tensor, ps_guard[oc % 2])
            for ic in range(8):
                inst = nc.tensor.matmul(bank[:, :TB], wo[:, ic, oc * 128:(oc + 1) * 128],
                                        ob_sb[:, ic, :], start=(ic == 0), stop=(ic == 7))
            mm = p.inc(inst, "PE")
            p.wait(nc.vector, "PE", mm)
            p.wait_counts(nc.vector, x_hist[sl] or {})
            res_ve = p.inc(nc.vector.tensor_add(x_sb[:, sl, oc, 1:], x_sb[:, sl, oc, 1:],
                                                bank[:, :TB]), "VE")
            nc.vector.drain()
            ps_guard[oc % 2] = {"VE": res_ve}
        p.wait(nc.sync, "VE", res_ve)
        st = p.inc(nc.sync.dma_start(xout[:, :, t0:t0 + TB], x_sb[:, sl, :, 1:]),
                   f"ST{sl}", dma=True)
        x_hist[sl] = {"VE": res_ve, "PE": p.cnt["PE"], "SC": p.cnt["SC"], f"ST{sl}": st}
        prev["q"] = {"VE": rot_ve}
        prev["k"] = {"VE": rot_ve}
        prev["v"] = {"PE": p.cnt["PE"]}
        prev["xb"] = {"PE": p.cnt["PE"]}
        prev["tq"] = {"VE": rot_ve}
    p.finish()
    return p




def emit_ffn_ffn(p: Prog, i):
    nc = p.nc
    xin = ch_major(p.stream("xin", (D, T)))
    xout = ch_major(p.stream("xout", (D, T), out=True))
    w1 = load_w_bf(p, f"F{i}_W1", 8, 5632)
    w2 = load_w_bf(p, f"F{i}_W2", 22, 1024)
    lg = load_small(p, f"F{i}_ln_g", (128, 8))
    lb = load_small(p, f"F{i}_ln_b", (128, 8))
    b1a = load_small(p, f"F{i}_b1a", (128, 22))
    b1g = load_small(p, f"F{i}_b1g", (128, 22))
    b2 = load_small(p, f"F{i}_b2", (128, 8))
    p.wait(nc.tensor, "LD")
    p.wait(nc.scalar, "LD")
    p.wait(nc.vector, "LD")

    x_sb = p.sbuf("x_sb", (128, 2, 8, 1 + TB), F32)
    xn_sb = p.sbuf("xn_sb", (128, 8, TB), BF16)
    a_sb = p.sbuf("a_sb", (128, 22, TB), BF16)
    g_sb = p.sbuf("g_sb", (128, TB), BF16)
    hg_sb = p.sbuf("hg_sb", (128, 22, TB), BF16)
    scr = p.sbuf("scr", (128, TB), F32)
    ln = LN(p, "ff_ln")
    ps_mm = [p.psum(f"ff_mm{k}", (128, 512), F32) for k in range(2)]

    x_hist = [None, None]
    ps_guard = [{}, {}]
    prev = {}
    for tau in range(NT):
        nc.vector.drain()
        nc.scalar.drain()
        sl = tau % 2
        t0 = tau * TB
        if x_hist[sl] is not None:
            p.wait_counts(nc.sync, x_hist[sl])
        if tau == 0:
            ld = p.inc(nc.sync.dma_start(x_sb[:, sl, :, 1:], xin[:, :, 0:TB]), f"LX{sl}", dma=True)
            p.wait(nc.vector, f"LX{sl}", ld)
            ve0 = p.inc(nc.vector.memset(x_sb[:, sl, :, 0:1], 0.0), "VE")
            xld = {f"LX{sl}": ld, "VE": ve0}
        else:
            ld = p.inc(nc.sync.dma_start(x_sb[:, sl, :, :], xin[:, :, t0 - 1:t0 + TB]),
                       f"LX{sl}", dma=True)
            xld = {f"LX{sl}": ld}
        xsh = [x_sb[:, sl, c, (1 if c < 4 else 0):(1 if c < 4 else 0) + TB] for c in range(8)]
        inv_bc, o_bc, pe_bc = ln.stats(xsh, xld)
        p.wait(nc.vector, "PE", pe_bc)
        if prev:
            p.wait_counts(nc.vector, {"PE": prev["PE"]})
        for c in range(8):
            nc.vector.tensor_mul(scr[:, :], xsh[c], inv_bc)
            nc.vector.drain()
            nc.vector.tensor_add(scr[:, :], scr[:, :], o_bc)
            nc.vector.drain()
            inst = nc.vector.tensor_scalar(xn_sb[:, c, :], scr[:, :], lg[:, c:c + 1],
                                           lb[:, c:c + 1], ALU.mult, ALU.add)
            nc.vector.drain()
        xnv = p.inc(inst, "VE")
        ln.mark_used()
        p.wait(nc.tensor, "VE", xnv)
        for oc in range(22):
            bank = ps_mm[oc % 2]
            p.wait_counts(nc.tensor, ps_guard[oc % 2])
            for ic in range(8):
                inst = nc.tensor.matmul(bank[:, :TB], w1[:, ic, oc * 128:(oc + 1) * 128],
                                        xn_sb[:, ic, :], start=(ic == 0), stop=(ic == 7))
            mm = p.inc(inst, "PE")
            p.wait(nc.scalar, "PE", mm)
            p.wait_counts(nc.scalar, prev.get("a", {}))
            ps_guard[oc % 2] = {"SC": p.inc(
                nc.scalar.activation(a_sb[:, oc, :], bank[:, :TB], AF.Identity,
                                     bias=b1a[:, oc:oc + 1]), "SC")}
        for oc in range(22):
            bank = ps_mm[oc % 2]
            p.wait_counts(nc.tensor, ps_guard[oc % 2])
            for ic in range(8):
                inst = nc.tensor.matmul(bank[:, :TB], w1[:, ic, (22 + oc) * 128:(23 + oc) * 128],
                                        xn_sb[:, ic, :], start=(ic == 0), stop=(ic == 7))
            mm = p.inc(inst, "PE")
            p.wait(nc.scalar, "PE", mm)
            p.wait_counts(nc.scalar, prev.get("g", {}))
            scg = p.inc(nc.scalar.activation(g_sb[:, :], bank[:, :TB], AF.Gelu,
                                             bias=b1g[:, oc:oc + 1]), "SC")
            p.wait(nc.vector, "SC", scg)
            p.wait_counts(nc.vector, prev.get("hg", {}))
            ve = p.inc(nc.vector.tensor_mul(hg_sb[:, oc, :], a_sb[:, oc, :], g_sb[:, :]), "VE")
            nc.vector.drain()
            ps_guard[oc % 2] = {"VE": ve}
            prev["g"] = {"VE": ve}
        hgv = p.cnt["VE"]
        p.wait(nc.tensor, "VE", hgv)
        res_ve = 0
        for oc in range(8):
            bank = ps_mm[oc % 2]
            p.wait_counts(nc.tensor, ps_guard[oc % 2])
            for ic in range(22):
                inst = nc.tensor.matmul(bank[:, :TB], w2[:, ic, oc * 128:(oc + 1) * 128],
                                        hg_sb[:, ic, :], start=(ic == 0), stop=(ic == 21))
            mm = p.inc(inst, "PE")
            p.wait(nc.vector, "PE", mm)
            res_ve = p.inc(nc.vector.scalar_tensor_tensor(
                x_sb[:, sl, oc, 1:], bank[:, :TB], b2[:, oc:oc + 1], x_sb[:, sl, oc, 1:],
                ALU.add, ALU.add), "VE")
            nc.vector.drain()
            ps_guard[oc % 2] = {"VE": res_ve}
        p.wait(nc.sync, "VE", res_ve)
        st = p.inc(nc.sync.dma_start(xout[:, :, t0:t0 + TB], x_sb[:, sl, :, 1:]),
                   f"ST{sl}", dma=True)
        x_hist[sl] = {"VE": res_ve, "PE": p.cnt["PE"], "SC": p.cnt["SC"], f"ST{sl}": st}
        prev = {"PE": p.cnt["PE"], "a": {"PE": p.cnt["PE"]}, "hg": {"PE": p.cnt["PE"]},
                "g": prev.get("g", {})}
    p.finish()
    return p


def _run(p: Prog, in_map_per_core):
    res = bass_utils.run_bass_kernel_spmd(p.nc, in_map_per_core, core_ids=list(range(len(in_map_per_core))))
    return res.results




def emit_wcast_fused(p: Prog, weights_meta):
    nc = p.nc
    CH = 16384
    stage = p.sbuf("wc_stage", (128, 2, CH), F32)
    out = p.sbuf("wc_out", (128, 2, CH), BF16)
    hist = {0: None, 1: None}
    k = 0
    for name, cols in weights_meta:
        src = p.dram_in(name, (128, cols))
        dst = p.dram_tmp(name + "_bfi", (128, cols), BF16)
        p.wbf[name] = dst
        for c0 in range(0, cols, CH):
            w = min(CH, cols - c0)
            sl = k % 2
            if hist[sl] is not None:
                nc.sync.wait_ge(p.sem("SC"), hist[sl][0])
                nc.sync.wait_ge(p.sem(f"WS{sl}"), hist[sl][1])
            ld = p.inc(nc.sync.dma_start(stage[:, sl, :w], src[:, c0:c0 + w]),
                       f"WL{sl}", dma=True)
            p.wait(nc.scalar, f"WL{sl}", ld)
            sc = p.inc(nc.scalar.copy(out[:, sl, :w], stage[:, sl, :w]), "SC")
            p.wait(nc.sync, "SC", sc)
            st = p.inc(nc.sync.dma_start(dst[:, c0:c0 + w], out[:, sl, :w]),
                       f"WS{sl}", dma=True)
            hist[sl] = (sc, st)
            k += 1
    for sl in range(2):
        if hist[sl] is not None:
            p.wait(nc.sync, f"WS{sl}")


def build_fused(weights_meta):
    p = Prog()
    nc = p.nc
    p.scope_open()
    emit_wcast_fused(p, weights_meta)
    p.scope_close()
    sA = p.dram_tmp("strA", (D, T))
    sB = p.dram_tmp("strB", (D, T))
    sX2 = p.dram_tmp("strX2", (D, T))
    sT0 = p.dram_tmp("strT0", (D, T))
    sX3 = p.dram_tmp("strX3", (D, T))

    def run(emit, streams, args=()):
        p.scope_open()
        p.streams = streams
        emit(p, *args)
        p.scope_close()
        p.streams = {}

    run(emit_l0_attn, {"xin": p.dram_in("xin_full", (D, T)), "xout": sA})
    run(emit_l0_ffn, {"xin": sA, "xout": sX2})
    run(emit_l1_pre, {"xin": sX2, "xout": sT0})
    cur, nxt = sA, sB
    src = sT0
    for l in range(4):
        run(emit_mamba, {"xin": src, "xout": cur}, (l,))
        src = cur
        cur, nxt = nxt, cur
    run(emit_opmlp, {"xin": src, "tin": sT0, "rin": sX2, "xout": sX3})
    cur, nxt = sA, sB
    run(emit_ln_only, {"xin": sX3, "xout": cur})
    src = cur
    cur, nxt = nxt, cur
    for i in range(6):
        run(emit_ffn_attn, {"xin": src, "xout": cur}, (i,))
        src, cur, nxt = cur, nxt, cur
        run(emit_ffn_ffn, {"xin": src, "xout": cur}, (i,))
        src, cur, nxt = cur, nxt, cur
    p.scope_open()
    p.streams = {"xin": src, "rin": sX3}
    emit_final(p)
    p.streams = {}
    for name in list(p.cnt):
        if name.startswith("ST") or name == "DBG":
            p.wait(p.nc.sync, name)
    for cm in reversed(p._scope or []):
        cm.__exit__(None, None, None)
    p._scope = None
    for cm in reversed(p._stack):
        cm.__exit__(None, None, None)
    p._stack = []
    return p


WEIGHTS_META = None


def _launch(build, in_maps, n_cores):
    p = build()
    res = bass_utils.run_bass_kernel_spmd(p.nc, in_maps, core_ids=list(range(n_cores)))
    return p, res.results


def kernel(**inputs):
    W = host_weights(inputs)
    S = host_smalls(inputs)
    u = np.asarray(inputs["L0_attn_u"], dtype=np.float32)
    xs = [np.ascontiguousarray(u[c].T) for c in range(B)]
    Wflat = {k: np.ascontiguousarray(np.asarray(v, np.float32).reshape(128, -1))
             for k, v in W.items()}
    meta = [(k, v.shape[1]) for k, v in Wflat.items()]
    p = build_fused(meta)
    in_maps = []
    for c in range(B):
        m = {}
        for name in p.inputs:
            if name == "xin_full":
                m[name] = xs[c]
            elif name in Wflat:
                m[name] = Wflat[name]
            elif name in S:
                m[name] = S[name]
            else:
                raise KeyError(name)
        in_maps.append(m)
    res = bass_utils.run_bass_kernel_spmd(p.nc, in_maps, core_ids=list(range(B)))
    out = np.stack([np.asarray(r["out"]) for r in res.results], axis=0)
    return out.astype(np.float32)


# version 05
# speedup vs baseline: 1.0274x; 1.0274x over previous
import numpy as np
import ml_dtypes
import concourse.bass as bass
import concourse.mybir as mybir
import concourse.bass_utils as bass_utils

F32 = mybir.dt.float32
BF16 = mybir.dt.bfloat16
AF = mybir.ActivationFunctionType
ALU = mybir.AluOpType
AX = mybir.AxisListType

RACE_DETECT = False
DRAINS_VE = False
DRAINS_SC = True
DRAINS_GP = True
B, T, D = 8, 8192, 1024


def dr(engine):
    et = engine.engine if hasattr(engine, "engine") else None
    name = str(et)
    if "DVE" in name:
        if DRAINS_VE:
            engine.drain()
    elif "Activation" in name:
        if DRAINS_SC:
            engine.drain()
    else:
        if DRAINS_GP:
            engine.drain()

TB = 256
NT = T // TB
EPS = 1e-5
bfloat16 = ml_dtypes.bfloat16


class Prog:
    def __init__(self):
        self.nc = bass.Bass("TRN2", target_bir_lowering=False,
                            detect_race_conditions=RACE_DETECT)
        self._stack = []
        self.sems = {}
        self.cnt = {}
        self.inputs = {}
        self.outputs = {}
        self.streams = {}
        self.wbf = {}
        self._names = set()
        self._scope = None
        for name in ("LD", "ST", "PE", "VE", "SC", "GP"):
            self._mksem(name)

    def uname(self, name):
        if name not in self._names:
            self._names.add(name)
            return name
        k = 2
        while f"{name}_{k}" in self._names:
            k += 1
        self._names.add(f"{name}_{k}")
        return f"{name}_{k}"

    def stream(self, name, shape, dtype=F32, out=False):
        if name in self.streams:
            return self.streams[name]
        return self.dram_out(name, shape, dtype) if out else self.dram_in(name, shape, dtype)

    def _mksem(self, name):
        cm = self.nc.semaphore(name)
        self.sems[name] = cm.__enter__()
        self._stack.append(cm)
        self.cnt[name] = 0

    def sem(self, name):
        if name not in self.sems:
            self._mksem(name)
        return self.sems[name]

    def inc(self, inst, name, dma=False):
        n = 16 if dma else 1
        self.sem(name)
        self.cnt[name] += n
        inst.then_inc(self.sems[name], n)
        return self.cnt[name]

    def wait(self, engine, name, value=None):
        self.sem(name)
        v = self.cnt[name] if value is None else value
        if v > 0:
            engine.wait_ge(self.sems[name], v)

    def snap(self, *names):
        names = names or tuple(self.cnt)
        return {n: self.cnt[n] for n in names}

    def wait_counts(self, engine, counts):
        for n, v in counts.items():
            if v > 0:
                engine.wait_ge(self.sems[n], v)

    def dram_in(self, name, shape, dtype=F32):
        if name in self.inputs:
            return self.inputs[name]
        ap = self.nc.dram_tensor(name, list(shape), dtype, kind="ExternalInput").ap()
        self.inputs[name] = ap
        return ap

    def dram_out(self, name, shape, dtype=F32):
        ap = self.nc.dram_tensor(name, list(shape), dtype, kind="ExternalOutput").ap()
        self.outputs[name] = ap
        return ap

    def dram_tmp(self, name, shape, dtype=F32):
        return self.nc.dram_tensor(name, list(shape), dtype, kind="Internal").ap()

    def sbuf(self, name, shape, dtype=F32):
        cm = self.nc.sbuf_tensor(self.uname(name), list(shape), dtype)
        t = cm.__enter__()
        (self._scope if self._scope is not None else self._stack).append(cm)
        return t

    def psum(self, name, shape, dtype=F32):
        cm = self.nc.psum_tensor(self.uname(name), list(shape), dtype)
        t = cm.__enter__()
        (self._scope if self._scope is not None else self._stack).append(cm)
        return t

    def scope_open(self):
        self._scope = []

    def scope_close(self):
        for name in list(self.cnt):
            if name.startswith("ST") or name == "DBG":
                self.wait(self.nc.sync, name)
        self.nc.all_engine_barrier()
        for cm in reversed(self._scope or []):
            cm.__exit__(None, None, None)
        self._scope = None

    def finish(self):
        if self._scope is not None:
            return
        for name in list(self.cnt):
            if name.startswith("ST") or name == "DBG":
                self.wait(self.nc.sync, name)
        for cm in reversed(self._stack):
            cm.__exit__(None, None, None)
        self._stack = []


def ch_major(dram_ap):
    return dram_ap.rearrange("(c p) t -> p c t", p=128)


class LN:

    def __init__(self, p: Prog, tag, sq=None):
        nc = p.nc
        self.p = p
        self.ones_col = p.sbuf(f"{tag}_onc", (128, 1), F32)
        self.ones_col_bf = p.sbuf(f"{tag}_oncb", (128, 1), BF16)
        self.ones_row = p.sbuf(f"{tag}_onr", (1, 128), F32)
        nc.vector.memset(self.ones_col[:, :], 1.0)
        nc.vector.memset(self.ones_col_bf[:, :], 1.0)
        p.inc(nc.vector.memset(self.ones_row[:, :], 1.0), "VE")
        self.sq = p.sbuf(f"{tag}_sq", (128, TB), F32) if sq is None else sq
        self.eps_t = p.sbuf(f"{tag}_eps", (1, 1), F32)
        nc.vector.memset(self.eps_t[:1, :1], EPS)
        self.rows = p.sbuf(f"{tag}_rows", (1, 4, TB), F32)
        self.ps_stat = p.psum(f"{tag}_pst", (1, 512), F32)
        self.ps_bc = p.psum(f"{tag}_psb", (128, 512), F32)
        self.bc_sb = p.sbuf(f"{tag}_bcs", (128, 2, TB), F32)
        self.hist = p.snap()

    def stats(self, x_tiles, ready):
        p, nc = self.p, self.p.nc
        nct = len(x_tiles)
        n_ch = nct * 128
        sx = self.ps_stat[:1, 0:TB]
        sq_ps = self.ps_stat[:1, 256:256 + TB]
        p.wait_counts(nc.tensor, {"VE": self.hist["VE"], "SC": self.hist["SC"]})
        p.wait_counts(nc.tensor, ready)
        onc = self.ones_col if x_tiles[0].dtype == F32 else self.ones_col_bf
        for c in range(nct):
            inst = nc.tensor.matmul(sx, onc[:, :], x_tiles[c],
                                    start=(c == 0), stop=(c == nct - 1))
        p.inc(inst, "PE")
        p.wait_counts(nc.scalar, ready)
        for c in range(nct):
            p.wait_counts(nc.scalar, {"PE": self.hist["PE"] if c == 0 else self.pe_sq_last})
            sc = p.inc(nc.scalar.activation(self.sq[:, :], x_tiles[c], AF.Square), "SC")
            p.wait(nc.tensor, "SC", sc)
            inst = nc.tensor.matmul(sq_ps, self.ones_col[:, :], self.sq[:, :],
                                    start=(c == 0), stop=(c == nct - 1))
            self.pe_sq_last = p.inc(inst, "PE")
        pe_stats = self.pe_sq_last
        mu = self.rows[:1, 0, :]
        ex2 = self.rows[:1, 1, :]
        var = self.rows[:1, 2, :]
        o = self.rows[:1, 3, :]
        p.wait(nc.vector, "PE", pe_stats)
        nc.vector.tensor_scalar(mu, sx, 1.0 / n_ch, None, ALU.mult)
        nc.vector.tensor_scalar(ex2, sq_ps, 1.0 / n_ch, None, ALU.mult)
        nc.vector.drain()
        nc.vector.scalar_tensor_tensor(var, mu, -1.0, mu, ALU.mult, ALU.mult)
        nc.vector.drain()
        ve_var = p.inc(nc.vector.tensor_add(var, var, ex2), "VE")
        inv = ex2
        p.wait(nc.scalar, "VE", ve_var)
        nc.scalar.activation(inv, var, AF.Ln, bias=self.eps_t[:1, :1])
        dr(nc.scalar)
        sc_inv = p.inc(nc.scalar.activation(inv, inv, AF.Exp, scale=-0.5), "SC")
        p.wait(nc.vector, "SC", sc_inv)
        nc.vector.drain()
        ve_o = p.inc(nc.vector.scalar_tensor_tensor(o, mu, -1.0, inv, ALU.mult, ALU.mult), "VE")
        p.wait(nc.tensor, "VE", ve_o)
        p.wait_counts(nc.tensor, {"SC": self.hist.get("SC", 0)})
        nc.tensor.matmul(self.ps_bc[:, 0:TB], self.ones_row[:1, :], inv, start=True, stop=True)
        inst = nc.tensor.matmul(self.ps_bc[:, 256:256 + TB], self.ones_row[:1, :], o,
                                start=True, stop=True)
        pe_bc = p.inc(inst, "PE")
        p.wait(nc.scalar, "PE", pe_bc)
        p.wait_counts(nc.scalar, {"VE": self.hist.get("VE", 0)})
        nc.scalar.copy(self.bc_sb[:, 0, :], self.ps_bc[:, 0:TB])
        nc.scalar.drain()
        sc_bc = p.inc(nc.scalar.copy(self.bc_sb[:, 1, :], self.ps_bc[:, 256:256 + TB]), "SC")
        return self.bc_sb[:, 0, :], self.bc_sb[:, 1, :], sc_bc

    def mark_used(self):
        self.hist = self.p.snap()


def ln_apply(p, x_c, out_c, inv_bc, o_bc, scratch):
    nc = p.nc
    nc.vector.tensor_mul(scratch, x_c, inv_bc)
    dr(nc.vector)
    inst = nc.vector.tensor_add(out_c, scratch, o_bc)
    dr(nc.vector)
    return p.inc(inst, "VE")


def _f32(x):
    return np.ascontiguousarray(np.asarray(x, dtype=np.float32))


def _ch_tiles(vec, nct):
    return _f32(vec).reshape(nct, 128).T.copy()


def _lhsT_tiles(arr):
    ic, oc = arr.shape
    assert ic % 128 == 0
    return np.ascontiguousarray(arr.reshape(ic // 128, 128, oc).transpose(1, 0, 2))


def host_weights(inputs):
    w = {}
    w["L0_W_in"] = _lhsT_tiles(_f32(inputs["L0_attn_W_in"]))
    w["L0_W_out"] = _lhsT_tiles(_f32(inputs["L0_attn_W_out"]))
    w["L0f_W1"] = _lhsT_tiles(_f32(inputs["L0_ffn_W1"]))
    w["L0f_W2"] = _lhsT_tiles(_f32(inputs["L0_ffn_W2"]))
    for l in range(4):
        win = _f32(inputs["L1_attn_in_proj_w"][l]).T
        w[f"M{l}_Win"] = np.ascontiguousarray(
            win.reshape(8, 128, 32, 128).transpose(1, 2, 0, 3))
        wx = _f32(inputs["L1_attn_x_proj_w"][l]).T
        wxp = np.zeros((2048, 128), np.float32)
        wxp[:, 0] = wx[:, 0]
        wxp[:, 32:48] = wx[:, 1:17]
        wxp[:, 64:80] = wx[:, 17:33]
        w[f"M{l}_Wx"] = _lhsT_tiles(wxp)
        wo = _f32(inputs["L1_attn_out_proj_w"][l]).T
        w[f"M{l}_Wout"] = np.ascontiguousarray(
            wo.reshape(16, 128, 8, 128).transpose(1, 2, 0, 3))
    w["OP1"] = _lhsT_tiles(_f32(inputs["L1_attn_op1_w"]).T)
    w["OP2"] = _lhsT_tiles(_f32(inputs["L1_attn_op2_w"]).T)
    for i in range(6):
        w[f"F{i}_Wqkv"] = _lhsT_tiles(_f32(inputs["L1_ffn_Wqkv"][i]))
        w[f"F{i}_Wo"] = _lhsT_tiles(_f32(inputs["L1_ffn_Wo"][i]))
        w1 = _f32(inputs["L1_ffn_W1"][i])
        w1p = np.zeros((1024, 5632), np.float32)
        w1p[:, 0:2730] = w1[:, 0:2730]
        w1p[:, 2816:2816 + 2730] = w1[:, 2730:5460]
        w[f"F{i}_W1"] = _lhsT_tiles(w1p)
        w2 = _f32(inputs["L1_ffn_W2"][i])
        w2p = np.zeros((2816, 1024), np.float32)
        w2p[:2730] = w2
        w[f"F{i}_W2"] = _lhsT_tiles(w2p)
    return w


def host_smalls(inputs):
    s = {}
    s["L0_b_in_t"] = _ch_tiles(inputs["L0_attn_b_in"], 24)
    s["L0_short_w_t"] = _f32(inputs["L0_attn_short_w"]).reshape(24, 128, 3).transpose(1, 0, 2).copy()
    s["L0_short_b_t"] = _ch_tiles(inputs["L0_attn_short_b"], 24)
    s["L0_inner_w_t"] = _f32(inputs["L0_attn_inner_w"]).reshape(8, 128, 4).transpose(1, 0, 2).copy()
    s["L0_inner_b_t"] = _ch_tiles(inputs["L0_attn_inner_b"], 8)
    s["L0_b_out_t"] = _ch_tiles(inputs["L0_attn_b_out"], 8)
    s["L0f_b1_t"] = _ch_tiles(inputs["L0_ffn_b1"], 32)
    s["L0f_b2_t"] = _ch_tiles(inputs["L0_ffn_b2"], 8)
    s["ident128"] = np.eye(128, dtype=bfloat16)
    s["M_innorm_g"] = _ch_tiles(inputs["L1_attn_in_norm_g"], 8)
    s["M_innorm_b"] = _ch_tiles(inputs["L1_attn_in_norm_b"], 8)
    for l in range(4):
        s[f"M{l}_ln_g"] = _ch_tiles(inputs["L1_attn_ln_g"][l], 8)
        s[f"M{l}_ln_b"] = _ch_tiles(inputs["L1_attn_ln_b"][l], 8)
        cwv = _f32(inputs["L1_attn_conv_w"][l])
        s[f"M{l}_conv_w"] = np.ascontiguousarray(
            cwv.reshape(2, 16, 128, 4).transpose(2, 0, 1, 3))
        cbv = _f32(inputs["L1_attn_conv_b"][l])
        s[f"M{l}_conv_b"] = np.ascontiguousarray(
            cbv.reshape(2, 16, 128).transpose(2, 0, 1))
        s[f"M{l}_dt_w"] = _ch_tiles(inputs["L1_attn_dt_proj_w"][l][:, 0], 16)
        s[f"M{l}_dt_b"] = _ch_tiles(inputs["L1_attn_dt_proj_b"][l], 16)
        s[f"M{l}_D"] = _ch_tiles(inputs["L1_attn_D"][l], 16)
        s[f"M{l}_mln_g"] = _ch_tiles(inputs["L1_attn_mamba_ln_g"][l], 16)
        s[f"M{l}_mln_b"] = _ch_tiles(inputs["L1_attn_mamba_ln_b"][l], 16)
    sel = np.zeros((128, 33, 128), dtype=bfloat16)
    sel[0, 0, :] = 1.0
    for n in range(16):
        sel[32 + n, 1 + n, :] = 1.0
        sel[64 + n, 17 + n, :] = 1.0
    s["sel_rows"] = sel
    s["zeros96"] = np.zeros((128, 96), bfloat16)
    s["OP1_b"] = _ch_tiles(inputs["L1_attn_op1_b"], 8)
    s["OP2_b"] = _ch_tiles(inputs["L1_attn_op2_b"], 8)
    half = np.arange(0, 64, 2, dtype=np.float64)
    inv_freq = 1.0 / 10000.0 ** (half / 64.0)
    tt = np.arange(T, dtype=np.float64)
    fr = tt[:, None] * inv_freq[None, :]
    freqs = np.concatenate([fr, fr], axis=1)
    base_scale = (half + 0.4 * 64) / (1.4 * 64)
    power = (tt - T // 2) / (256 // 2)
    scale = base_scale[None, :] ** power[:, None]
    scale = np.concatenate([scale, scale], axis=1)
    cos_t = np.cos(freqs); sin_t = np.sin(freqs)
    qc = (cos_t * scale).T.astype(np.float32)
    qs_ = (sin_t * scale).T.astype(np.float32)
    kc = (cos_t / scale).T.astype(np.float32)
    ks_ = (sin_t / scale).T.astype(np.float32)
    s["rot_qc"] = np.ascontiguousarray(np.tile(qc, (2, 1)))
    s["rot_qs"] = np.ascontiguousarray(np.tile(qs_, (2, 1)))
    s["rot_kc"] = np.ascontiguousarray(np.tile(kc, (2, 1)))
    s["rot_ks"] = np.ascontiguousarray(np.tile(ks_, (2, 1)))
    R = np.zeros((64, 64), np.float32)
    for d in range(32):
        R[d, d + 32] = -1.0
        R[d + 32, d] = 1.0
    Rt = R.T
    Rbd = np.zeros((128, 128), np.float32)
    Rbd[:64, :64] = Rt
    Rbd[64:, 64:] = Rt
    s["RbdT"] = Rbd.astype(bfloat16)
    bd2 = np.zeros((128, 2), np.float32)
    bd2[:64, 0] = 1.0
    bd2[64:, 1] = 1.0
    s["bd2"] = bd2
    ones_top = np.zeros((1, 128), np.float32); ones_top[0, :64] = 1.0
    ones_bot = np.zeros((1, 128), np.float32); ones_bot[0, 64:] = 1.0
    s["ones_top"] = ones_top
    s["ones_bot"] = ones_bot
    bdT = np.zeros((2, 128), np.float32)
    bdT[0, :64] = 1.0
    bdT[1, 64:] = 1.0
    s["bdT"] = bdT
    pp = np.arange(128, dtype=np.float32)
    s["mask_st"] = np.stack([pp, 128 + pp], axis=1)
    s["mask_en"] = np.stack([pp + 257, 128 + pp + 257], axis=1)
    s["mask_en0"] = np.stack([pp + 1, 128 + pp + 1], axis=1)
    NEG = np.float32(-3.0e38)
    ii = np.arange(128)[:, None]
    jj = np.arange(512)[None, :]
    mk = np.zeros((2, 128, 512), np.float32)
    for m in range(2):
        ok = (jj >= m * 128 + ii) & (jj <= m * 128 + ii + 256)
        mk[m] = np.where(ok, 0.0, NEG)
    s["mask_add_t"] = np.ascontiguousarray(mk.transpose(1, 0, 2))
    jj0 = np.arange(256)[None, :]
    mk0 = np.zeros((2, 128, 256), np.float32)
    for m in range(2):
        ok = jj0 <= m * 128 + ii
        mk0[m] = np.where(ok, 0.0, NEG)
    s["mask_add0_t"] = np.ascontiguousarray(mk0.transpose(1, 0, 2))
    s["ident128f"] = np.eye(128, dtype=np.float32)
    for i in range(6):
        qsv = _f32(inputs["L1_ffn_q_scale"][i])
        ksv = _f32(inputs["L1_ffn_k_scale"][i])
        s[f"F{i}_qs"] = np.tile(qsv, 2).reshape(128, 1).copy()
        s[f"F{i}_ks"] = np.tile(ksv, 2).reshape(128, 1).copy()
        s[f"F{i}_qsp"] = np.tile(np.roll(qsv, -32), 2).reshape(128, 1).copy()
        s[f"F{i}_ksp"] = np.tile(np.roll(ksv, -32), 2).reshape(128, 1).copy()
        s[f"F{i}_ln_g"] = _ch_tiles(inputs["L1_ffn_ln_g"][i], 8)
        s[f"F{i}_ln_b"] = _ch_tiles(inputs["L1_ffn_ln_b"][i], 8)
        b1 = _f32(inputs["L1_ffn_b1"][i])
        b1a = np.zeros(2816, np.float32); b1a[:2730] = b1[:2730]
        b1g = np.zeros(2816, np.float32); b1g[:2730] = b1[2730:]
        s[f"F{i}_b1a"] = _ch_tiles(b1a, 22)
        s[f"F{i}_b1g"] = _ch_tiles(b1g, 22)
        s[f"F{i}_b2"] = _ch_tiles(inputs["L1_ffn_b2"][i], 8)
    return s


def build_wcast(weights):
    p = Prog()
    nc = p.nc
    CH = 16384
    stage = p.sbuf("wc_stage", (128, 2, CH), F32)
    out = p.sbuf("wc_out", (128, 2, CH), BF16)
    items = []
    for name, arr in weights.items():
        cols = arr.size // 128
        src = p.dram_in(name, (128, cols))
        dst = p.dram_out(name + "_bf", (128, cols), BF16)
        items.append((src, dst, cols))
    hist = {0: None, 1: None}
    k = 0
    for src, dst, cols in items:
        for c0 in range(0, cols, CH):
            w = min(CH, cols - c0)
            s = k % 2
            if hist[s] is not None:
                nc.sync.wait_ge(p.sem("SC"), hist[s][0])
                nc.sync.wait_ge(p.sem(f"ST{s}"), hist[s][1])
            ld = p.inc(nc.sync.dma_start(stage[:, s, :w], src[:, c0:c0 + w]), f"WL{s}", dma=True)
            p.wait(nc.scalar, f"WL{s}", ld)
            sc = p.inc(nc.scalar.copy(out[:, s, :w], stage[:, s, :w]), "SC")
            p.wait(nc.sync, "SC", sc)
            st = p.inc(nc.sync.dma_start(dst[:, c0:c0 + w], out[:, s, :w]), f"ST{s}", dma=True)
            hist[s] = (sc, st)
            k += 1
    p.finish()
    return p


def load_w_bf(p, name, n_ic, n_oc):
    nc = p.nc
    if name in p.wbf:
        d = p.wbf[name].rearrange("p (i o) -> p i o", i=n_ic)
    else:
        d = p.dram_in(name + "_bf", (128, n_ic * n_oc), BF16).rearrange(
            "p (i o) -> p i o", i=n_ic)
    s = p.sbuf(name + "_sb", (128, n_ic, n_oc), BF16)
    p.inc(nc.sync.dma_start(s[:, :, :], d[:, :, :]), "LD", dma=True)
    return s


def load_small(p, name, shape, dtype=F32):
    nc = p.nc
    d = p.dram_in(name, shape, dtype)
    s = p.sbuf(name + "_sb", shape, dtype)
    p.inc(nc.sync.dma_start(s[...], d[...]), "LD", dma=True)
    return s


def emit_l0_attn(p: Prog):
    nc = p.nc
    xin = ch_major(p.stream("xin", (D, T)))
    xout = ch_major(p.stream("xout", (D, T), out=True))
    w_in = load_w_bf(p, "L0_W_in", 8, 3072)
    w_out = load_w_bf(p, "L0_W_out", 8, 1024)
    b_in = load_small(p, "L0_b_in_t", (128, 24))
    sw = load_small(p, "L0_short_w_t", (128, 24, 3))
    sbv = load_small(p, "L0_short_b_t", (128, 24))
    iw = load_small(p, "L0_inner_w_t", (128, 8, 4))
    ib = load_small(p, "L0_inner_b_t", (128, 8))
    bo = load_small(p, "L0_b_out_t", (128, 8))
    ident = load_small(p, "ident128", (128, 128), BF16)
    ldw = p.cnt["LD"]

    dg_s = p.sbuf("dg_s", (128, 24, 3, 128), BF16)
    dg_i = p.sbuf("dg_i", (128, 8, 4, 128), BF16)
    p.wait(nc.vector, "LD", ldw)
    for c in range(24):
        for k in range(3):
            inst = nc.vector.tensor_scalar(dg_s[:, c, k, :], ident[:, :], sw[:, c, k:k + 1],
                                           None, ALU.mult)
    for c in range(8):
        for k in range(4):
            inst = nc.vector.tensor_scalar(dg_i[:, c, k, :], ident[:, :], iw[:, c, k:k + 1],
                                           None, ALU.mult)
    dgv = p.inc(inst, "VE")
    p.wait(nc.tensor, "VE", dgv)

    x_sb = p.sbuf("x_sb", (128, 2, 8, TB), F32)
    xn_sb = p.sbuf("xn_sb", (128, 8, TB), BF16)
    cin = p.sbuf("cin", (128, 24, 2 + TB), BF16)
    qk_sb = p.sbuf("qk_sb", (128, 16, TB), F32)
    vg = p.sbuf("vg", (128, 8, 3 + TB), BF16)
    y_sb = p.sbuf("y_sb", (128, 8, TB), BF16)
    o_sb = p.sbuf("o_sb", (128, 8, TB), F32)
    scr = p.sbuf("scr", (128, TB), F32)
    ln = LN(p, "l0a_ln")
    ps_mm = [p.psum(f"l0a_mm{i}", (128, 512), F32) for i in range(2)]

    nc.vector.memset(cin[:, :, 0:2], 0.0)
    p.inc(nc.vector.memset(vg[:, :, 0:3], 0.0), "VE")

    x_hist = [None, None]
    ps_guard = [{}, {}]
    blk_hist = None

    for tau in range(NT):
        s = tau % 2
        t0 = tau * TB
        if x_hist[s] is not None:
            p.wait_counts(nc.sync, x_hist[s])
        ld = p.inc(nc.sync.dma_start(x_sb[:, s, :, :], xin[:, :, t0:t0 + TB]), f"LX{s}", dma=True)
        xt = [x_sb[:, s, c, :] for c in range(8)]
        inv_bc, o_bc, pe_bc = ln.stats(xt, {f"LX{s}": ld})
        p.wait(nc.vector, "SC", pe_bc)
        if blk_hist is not None:
            p.wait_counts(nc.vector, {"PE": blk_hist["PE"]})
        for c in range(8):
            ln_apply(p, xt[c], xn_sb[:, c, :], inv_bc, o_bc, scr[:, :])
        xnv = p.cnt["VE"]
        ln.mark_used()
        p.wait(nc.tensor, "VE", xnv)
        for oc in range(24):
            bank = ps_mm[oc % 2]
            p.wait_counts(nc.tensor, ps_guard[oc % 2])
            for ic in range(8):
                inst = nc.tensor.matmul(bank[:, :TB], w_in[:, ic, oc * 128:(oc + 1) * 128],
                                        xn_sb[:, ic, :], start=(ic == 0), stop=(ic == 7))
            mm = p.inc(inst, "PE")
            p.wait(nc.scalar, "PE", mm)
            ps_guard[oc % 2] = {"SC": p.inc(
                nc.scalar.activation(cin[:, oc, 2:2 + TB], bank[:, :TB], AF.Identity,
                                     bias=b_in[:, oc:oc + 1]), "SC")}
        cin_sc = p.cnt["SC"]
        p.wait(nc.tensor, "SC", cin_sc)
        sc_k_done = 0
        for oc in list(range(8, 16)) + list(range(0, 8)) + list(range(16, 24)):
            bank = ps_mm[oc % 2]
            p.wait_counts(nc.tensor, ps_guard[oc % 2])
            for k in range(3):
                inst = nc.tensor.matmul(bank[:, :TB], dg_s[:, oc, k, :], cin[:, oc, k:k + TB],
                                        start=(k == 0), stop=(k == 2))
            mm = p.inc(inst, "PE")
            if oc < 16:
                p.wait(nc.scalar, "PE", mm)
                sc_k_done = p.inc(
                    nc.scalar.activation(qk_sb[:, oc, :], bank[:, :TB], AF.Identity,
                                         bias=sbv[:, oc:oc + 1]), "SC")
                ps_guard[oc % 2] = {"SC": sc_k_done}
            else:
                c = oc - 16
                p.wait(nc.vector, "PE", mm)
                p.wait(nc.vector, "SC", sc_k_done)
                ve = p.inc(nc.vector.scalar_tensor_tensor(
                    vg[:, c, 3:3 + TB], bank[:, :TB], sbv[:, oc:oc + 1], qk_sb[:, 8 + c, :],
                    ALU.add, ALU.mult), "VE")
                ps_guard[oc % 2] = {"VE": ve}
        ve_v = p.cnt["VE"]
        p.wait(nc.tensor, "VE", ve_v)
        for c in range(8):
            bank = ps_mm[c % 2]
            p.wait_counts(nc.tensor, ps_guard[c % 2])
            for k in range(4):
                inst = nc.tensor.matmul(bank[:, :TB], dg_i[:, c, k, :], vg[:, c, k:k + TB],
                                        start=(k == 0), stop=(k == 3))
            mm = p.inc(inst, "PE")
            p.wait(nc.vector, "PE", mm)
            ps_guard[c % 2] = {"VE": p.inc(nc.vector.scalar_tensor_tensor(
                y_sb[:, c, :], bank[:, :TB], ib[:, c:c + 1], qk_sb[:, c, :],
                ALU.add, ALU.mult), "VE")}
        yv = p.cnt["VE"]
        p.wait(nc.tensor, "VE", yv)
        if blk_hist is not None:
            p.wait(nc.scalar, "ST0", blk_hist.get("ST0", 0))
            p.wait(nc.scalar, "ST1", blk_hist.get("ST1", 0))
        sc_o = 0
        for oc in range(8):
            bank = ps_mm[oc % 2]
            p.wait_counts(nc.tensor, ps_guard[oc % 2])
            for ic in range(8):
                inst = nc.tensor.matmul(bank[:, :TB], w_out[:, ic, oc * 128:(oc + 1) * 128],
                                        y_sb[:, ic, :], start=(ic == 0), stop=(ic == 7))
            mm = p.inc(inst, "PE")
            p.wait(nc.scalar, "PE", mm)
            sc_o = p.inc(nc.scalar.activation(o_sb[:, oc, :], bank[:, :TB], AF.Identity,
                                              bias=bo[:, oc:oc + 1]), "SC")
            ps_guard[oc % 2] = {"SC": sc_o}
        ot = [o_sb[:, c, :] for c in range(8)]
        inv2, o2, pe_bc2 = ln.stats(ot, {"SC": sc_o})
        p.wait(nc.vector, "SC", pe_bc2)
        for c in range(8):
            nc.vector.tensor_mul(scr[:, :], ot[c], inv2)
            dr(nc.vector)
            nc.vector.tensor_add(scr[:, :], scr[:, :], o2)
            dr(nc.vector)
            inst = nc.vector.tensor_add(o_sb[:, c, :], scr[:, :], xt[c])
            dr(nc.vector)
        res_ve = p.inc(inst, "VE")
        ln.mark_used()
        p.wait(nc.sync, "VE", res_ve)
        p.inc(nc.sync.dma_start(xout[:, :, t0:t0 + TB], o_sb[:, :, :]), f"ST{s}", dma=True)
        x_hist[s] = {"VE": res_ve, "PE": p.cnt["PE"], "SC": p.cnt["SC"]}
        blk_hist = p.snap()
        if tau < NT - 1:
            p.wait(nc.scalar, "PE", p.cnt["PE"])
            dr(nc.scalar)
            nc.scalar.copy(cin[:, :, 0:2], cin[:, :, TB:TB + 2])
            p.inc(nc.scalar.copy(vg[:, :, 0:3], vg[:, :, TB:TB + 3]), "SC")
    p.finish()
    return p


def emit_l0_ffn(p: Prog):
    nc = p.nc
    xin = ch_major(p.stream("xin", (D, T)))
    xout = ch_major(p.stream("xout", (D, T), out=True))
    w1 = load_w_bf(p, "L0f_W1", 8, 4096)
    w2 = load_w_bf(p, "L0f_W2", 32, 1024)
    b1 = load_small(p, "L0f_b1_t", (128, 32))
    b2 = load_small(p, "L0f_b2_t", (128, 8))
    p.wait(nc.tensor, "LD")
    p.wait(nc.scalar, "LD")
    p.wait(nc.vector, "LD")

    x_sb = p.sbuf("x_sb", (128, 2, 8, TB), F32)
    xn_sb = p.sbuf("xn_sb", (128, 8, TB), BF16)
    h_sb = p.sbuf("h_sb", (128, 32, TB), BF16)
    o_sb = p.sbuf("o_sb", (128, 8, TB), F32)
    scr = p.sbuf("scr", (128, TB), F32)
    ln = LN(p, "l0f_ln")
    ps_mm = [p.psum(f"l0f_mm{i}", (128, 512), F32) for i in range(2)]

    x_hist = [None, None]
    ps_hist = [0, 0]
    blk_hist = None
    for tau in range(NT):
        s = tau % 2
        t0 = tau * TB
        dr(nc.vector)
        dr(nc.scalar)
        dr(nc.gpsimd)
        if x_hist[s] is not None:
            p.wait_counts(nc.sync, x_hist[s])
        ld = p.inc(nc.sync.dma_start(x_sb[:, s, :, :], xin[:, :, t0:t0 + TB]), f"LX{s}", dma=True)
        xt = [x_sb[:, s, c, :] for c in range(8)]
        inv_bc, o_bc, pe_bc = ln.stats(xt, {f"LX{s}": ld})
        p.wait(nc.vector, "SC", pe_bc)
        if blk_hist is not None:
            p.wait_counts(nc.vector, {"PE": blk_hist["PE"]})
        for c in range(8):
            ln_apply(p, xt[c], xn_sb[:, c, :], inv_bc, o_bc, scr[:, :])
        xnv = p.cnt["VE"]
        ln.mark_used()
        p.wait(nc.tensor, "VE", xnv)
        for oc in range(32):
            bank = ps_mm[oc % 2]
            p.wait(nc.tensor, "SC", ps_hist[oc % 2])
            for ic in range(8):
                inst = nc.tensor.matmul(bank[:, :TB], w1[:, ic, oc * 128:(oc + 1) * 128],
                                        xn_sb[:, ic, :], start=(ic == 0), stop=(ic == 7))
            mm = p.inc(inst, "PE")
            p.wait(nc.scalar, "PE", mm)
            ps_hist[oc % 2] = p.inc(
                nc.scalar.activation(h_sb[:, oc, :], bank[:, :TB], AF.Relu,
                                     bias=b1[:, oc:oc + 1]), "SC")
        hsc = p.cnt["SC"]
        p.wait(nc.tensor, "SC", hsc)
        if blk_hist is not None:
            p.wait(nc.scalar, "ST", blk_hist["ST"])
        sc_o = 0
        for oc in range(8):
            bank = ps_mm[oc % 2]
            p.wait(nc.tensor, "SC", ps_hist[oc % 2])
            for ic in range(32):
                inst = nc.tensor.matmul(bank[:, :TB], w2[:, ic, oc * 128:(oc + 1) * 128],
                                        h_sb[:, ic, :], start=(ic == 0), stop=(ic == 31))
            mm = p.inc(inst, "PE")
            p.wait(nc.scalar, "PE", mm)
            sc_o = p.inc(nc.scalar.activation(o_sb[:, oc, :], bank[:, :TB], AF.Identity,
                                              bias=b2[:, oc:oc + 1]), "SC")
            ps_hist[oc % 2] = sc_o
        ot = [o_sb[:, c, :] for c in range(8)]
        inv2, o2, pe_bc2 = ln.stats(ot, {"SC": sc_o})
        p.wait(nc.vector, "SC", pe_bc2)
        for c in range(8):
            nc.vector.tensor_mul(scr[:, :], ot[c], inv2)
            dr(nc.vector)
            nc.vector.tensor_add(scr[:, :], scr[:, :], o2)
            dr(nc.vector)
            inst = nc.vector.tensor_add(o_sb[:, c, :], scr[:, :], xt[c])
            dr(nc.vector)
        res_ve = p.inc(inst, "VE")
        ln.mark_used()
        p.wait(nc.sync, "VE", res_ve)
        p.inc(nc.sync.dma_start(xout[:, :, t0:t0 + TB], o_sb[:, :, :]), "ST", dma=True)
        x_hist[s] = {"VE": res_ve, "PE": p.cnt["PE"], "SC": p.cnt["SC"]}
        blk_hist = p.snap()
    p.finish()
    return p




def emit_l1_pre(p: Prog):
    nc = p.nc
    xin = ch_major(p.stream("xin", (D, T)))
    xout = ch_major(p.stream("xout", (D, T), out=True))
    g = load_small(p, "M_innorm_g", (128, 8))
    bb = load_small(p, "M_innorm_b", (128, 8))
    p.wait(nc.tensor, "LD")
    p.wait(nc.scalar, "LD")
    p.wait(nc.vector, "LD")
    x_sb = p.sbuf("x_sb", (128, 2, 8, TB), F32)
    o_sb = p.sbuf("o_sb", (128, 2, 8, TB), F32)
    scr = p.sbuf("scr", (128, TB), F32)
    ln = LN(p, "pre_ln")
    x_hist = [None, None]
    for tau in range(NT):
        sl = tau % 2
        t0 = tau * TB
        dr(nc.vector)
        dr(nc.scalar)
        dr(nc.gpsimd)
        if x_hist[sl] is not None:
            p.wait_counts(nc.sync, x_hist[sl])
        ld = p.inc(nc.sync.dma_start(x_sb[:, sl, :, :], xin[:, :, t0:t0 + TB]), f"LX{sl}", dma=True)
        xt = [x_sb[:, sl, c, :] for c in range(8)]
        inv_bc, o_bc, pe_bc = ln.stats(xt, {f"LX{sl}": ld})
        p.wait(nc.vector, "SC", pe_bc)
        for c in range(8):
            nc.vector.tensor_mul(scr[:, :], xt[c], inv_bc)
            dr(nc.vector)
            nc.vector.tensor_add(scr[:, :], scr[:, :], o_bc)
            dr(nc.vector)
            inst = nc.vector.tensor_scalar(o_sb[:, sl, c, :], scr[:, :], g[:, c:c + 1],
                                           bb[:, c:c + 1], ALU.mult, ALU.add)
            dr(nc.vector)
        ve = p.inc(inst, "VE")
        ln.mark_used()
        p.wait(nc.sync, "VE", ve)
        st = p.inc(nc.sync.dma_start(xout[:, :, t0:t0 + TB], o_sb[:, sl, :, :]), f"ST{sl}", dma=True)
        x_hist[sl] = {"VE": ve, "PE": p.cnt["PE"], "SC": p.cnt["SC"], f"ST{sl}": st}
    p.finish()
    return p


TBM = 512
NTM = T // TBM
MAMBA_DEBUG = False


def emit_mamba(p: Prog, l):
    nc = p.nc
    DI = 16
    NST = 16
    xin = ch_major(p.stream("xin", (D, T)))
    xout = ch_major(p.stream("xout", (D, T), out=True))
    if f"M{l}_Win" in p.wbf:
        win_d = p.wbf[f"M{l}_Win"].rearrange("p (o i c) -> p o i c", o=32, i=8)
        wout_d = p.wbf[f"M{l}_Wout"].rearrange("p (o i c) -> p o i c", o=8, i=16)
    else:
        win_d = p.dram_in(f"M{l}_Win_bf", (128, 32 * 8 * 128), BF16).rearrange(
            "p (o i c) -> p o i c", o=32, i=8)
        wout_d = p.dram_in(f"M{l}_Wout_bf", (128, 8 * 16 * 128), BF16).rearrange(
            "p (o i c) -> p o i c", o=8, i=16)
    z0h_early = [p.sbuf("z0h", (128, 16, 3), BF16), p.sbuf("z1h", (128, 16, 3), BF16)]
    wx = load_w_bf(p, f"M{l}_Wx", 16, 128)
    ln_g = load_small(p, f"M{l}_ln_g", (128, 8))
    ln_b = load_small(p, f"M{l}_ln_b", (128, 8))
    cw = load_small(p, f"M{l}_conv_w", (128, 2, 16, 4))
    cb = load_small(p, f"M{l}_conv_b", (128, 2, 16))
    dtw = load_small(p, f"M{l}_dt_w", (128, 16))
    dtb = load_small(p, f"M{l}_dt_b", (128, 16))
    Dv = load_small(p, f"M{l}_D", (128, 16))
    mg = load_small(p, f"M{l}_mln_g", (128, 16))
    mb = load_small(p, f"M{l}_mln_b", (128, 16))
    ident = load_small(p, "ident128", (128, 128), BF16)
    sel = load_small(p, "sel_rows", (128, 33, 128), BF16)
    zz = p.dram_in("zeros96", (128, 96), BF16)
    p.inc(nc.sync.dma_start(z0h_early[0][:, :, :], zz[:, 0:48].rearrange("p (a b) -> p a b", b=3)),
          "LD", dma=True)
    p.inc(nc.sync.dma_start(z0h_early[1][:, :, :], zz[:, 48:96].rearrange("p (a b) -> p a b", b=3)),
          "LD", dma=True)
    ldw = p.cnt["LD"]

    dg = p.sbuf("dg", (128, 2, 16, 4, 128), BF16)
    p.wait(nc.vector, "LD", ldw)
    for cv in range(2):
        for j in range(16):
            for k in range(4):
                inst = nc.vector.tensor_scalar(dg[:, cv, j, k, :], ident[:, :],
                                               cw[:, cv, j, k:k + 1], None, ALU.mult)
    dgv = p.inc(inst, "VE")
    p.wait(nc.tensor, "VE", dgv)

    x_sb = p.sbuf("x_sb", (128, 2, 8, TBM), F32)
    xn_sb = p.sbuf("xn_sb", (128, 8, TBM), BF16)
    win_sb = p.sbuf("win_sb", (128, 2, 8, 128), BF16)
    wout_sb = p.sbuf("wout_sb", (128, 2, 16, 128), BF16)
    z0 = p.sbuf("z0", (128, 3 + TBM), BF16)
    z1 = p.sbuf("z1", (128, 3 + TBM), BF16)
    z0h, z1h = z0h_early
    z2 = p.sbuf("z2", (128, 16, TBM), BF16)
    sig = p.sbuf("sig", (128, 16, TBM), BF16)
    praw = p.sbuf("praw", (128, TBM), F32)
    praw_b = p.sbuf("praw_b", (128, TBM), BF16)
    dt_sb = p.sbuf("dt_sb", (128, TBM), F32)
    u_sb = p.sbuf("u_sb", (128, TBM), F32)
    Bbc = p.sbuf("Bbc", (128, 16, TBM), BF16)
    a_sb = p.sbuf("a_sb", (128, 2, TBM), F32)
    b_sb = p.sbuf("b_sb", (128, 2, TBM), F32)
    h_sb = p.sbuf("h_sb", (128, 2, TBM), F32)
    yh_sb = p.sbuf("yh_sb", (128, 2, TBM), BF16)
    hc = p.sbuf("hc", (128, 16, 16), F32)
    y_sb = p.sbuf("y_sb", (128, 16, TBM), BF16)
    scr = p.sbuf("scr", (128, TBM), F32)
    scr2 = p.sbuf("scr2", (128, TBM), F32)
    ln = LN(p, "m_ln")

    ps_mm = [p.psum(f"m_mm{i}", (128, 512), F32) for i in range(2)]
    ps_dtr = p.psum("m_dtr", (128, 512), F32)
    ps_cbc = p.psum("m_cbc", (128, 512), F32)
    ps_y = p.psum("m_y", (128, 512), F32)

    if MAMBA_DEBUG:
        dbg = {
            "dbg_xn": ch_major(p.dram_out("dbg_xn", (1024, T), BF16)),
            "dbg_z2": ch_major(p.dram_out("dbg_z2", (2048, T), BF16)),
            "dbg_sig": ch_major(p.dram_out("dbg_sig", (2048, T), BF16)),
            "dbg_praw": p.dram_out("dbg_praw", (128, T), F32),
            "dbg_y": ch_major(p.dram_out("dbg_y", (2048, T), BF16)),
            "dbg_dt": p.dram_out("dbg_dt", (128, T), F32),
        }

    p.wait(nc.scalar, "LD")
    p.wait(nc.tensor, "LD")
    x_hist = [None, None]
    ps_guard = [{}, {}]
    dtr_guard = {}
    cbc_guard = {}
    y_guard = {}
    prev_ygv = 0
    prev_mln_sc = 0
    prev_yD_ve = 0
    w_guard = [{}, {}]
    wo_guard = [{}, {}]
    zh_guard = {}
    for tau in range(NTM):
        sl = tau % 2
        t0 = tau * TBM
        dr(nc.vector)
        dr(nc.scalar)
        dr(nc.gpsimd)
        if x_hist[sl] is not None:
            p.wait_counts(nc.sync, x_hist[sl])
        ld = p.inc(nc.sync.dma_start(x_sb[:, sl, :, :], xin[:, :, t0:t0 + TBM]), f"LX{sl}", dma=True)
        xt = [x_sb[:, sl, c, :] for c in range(8)]
        if tau > 0:
            p.wait_counts(nc.vector, {"PE": x_hist[1 - sl]["PE"]})
        for hh in range(2):
            xth = [x_sb[:, sl, c, hh * 256:(hh + 1) * 256] for c in range(8)]
            inv_bc, o_bc, pe_bc = ln.stats(xth, {f"LX{sl}": ld})
            p.wait(nc.vector, "SC", pe_bc)
            for c in range(8):
                ho = hh * 256
                nc.vector.tensor_mul(scr[:, :256], xth[c], inv_bc)
                dr(nc.vector)
                nc.vector.tensor_add(scr[:, :256], scr[:, :256], o_bc)
                dr(nc.vector)
                inst = nc.vector.tensor_scalar(xn_sb[:, c, ho:ho + 256], scr[:, :256],
                                               ln_g[:, c:c + 1], ln_b[:, c:c + 1],
                                               ALU.mult, ALU.add)
                dr(nc.vector)
            p.inc(inst, "VE")
            ln.mark_used()
        xnv = p.cnt["VE"]
        p.wait(nc.tensor, "VE", xnv)
        for oc in range(32):
            ws = oc % 2
            if x_hist[sl] is not None or True:
                p.wait_counts(nc.sync, w_guard[ws])
            wld = p.inc(nc.sync.dma_start(win_sb[:, ws, :, :], win_d[:, oc, :, :]), f"LW{ws}", dma=True)
            bank = ps_mm[oc % 2]
            p.wait_counts(nc.tensor, ps_guard[oc % 2])
            p.wait(nc.tensor, f"LW{ws}", wld)
            for ic in range(8):
                inst = nc.tensor.matmul(bank[:, :TBM], win_sb[:, ws, ic, :], xn_sb[:, ic, :],
                                        start=(ic == 0), stop=(ic == 7))
            mm = p.inc(inst, "PE")
            w_guard[ws] = {"PE": mm}
            if oc < 16:
                j = oc
                p.wait(nc.scalar, "PE", mm)
                p.wait_counts(nc.scalar, zh_guard)
                nc.scalar.copy(z0[:, 0:3], z0h[:, j, :])
                dr(nc.scalar)
                sc0 = p.inc(nc.scalar.activation(z0[:, 3:3 + TBM], bank[:, :TBM], AF.Silu), "SC")
                dr(nc.scalar)
                nc.scalar.copy(z0h[:, j, :], z0[:, TBM:TBM + 3])
                ps_guard[oc % 2] = {"SC": sc0}
                bank2 = ps_mm[(oc + 1) % 2]
                p.wait_counts(nc.tensor, ps_guard[(oc + 1) % 2])
                p.wait(nc.tensor, "SC", sc0)
                for k in range(4):
                    inst = nc.tensor.matmul(bank2[:, :TBM], dg[:, 0, j, k, :], z0[:, k:k + TBM],
                                            start=(k == 0), stop=(k == 3))
                mm2 = p.inc(inst, "PE")
                p.wait(nc.scalar, "PE", mm2)
                nc.scalar.copy(z1[:, 0:3], z1h[:, j, :])
                dr(nc.scalar)
                sc1 = p.inc(nc.scalar.activation(z1[:, 3:3 + TBM], bank2[:, :TBM], AF.Silu,
                                                 bias=cb[:, 0, j:j + 1]), "SC")
                dr(nc.scalar)
                nc.scalar.copy(z1h[:, j, :], z1[:, TBM:TBM + 3])
                zh_guard = {"SC": sc1}
                ps_guard[(oc + 1) % 2] = {"SC": sc1}
                bank3 = ps_mm[oc % 2]
                p.wait_counts(nc.tensor, ps_guard[oc % 2])
                p.wait(nc.tensor, "SC", sc1)
                for k in range(4):
                    inst = nc.tensor.matmul(bank3[:, :TBM], dg[:, 1, j, k, :], z1[:, k:k + TBM],
                                            start=(k == 0), stop=(k == 3))
                mm3 = p.inc(inst, "PE")
                p.wait(nc.scalar, "PE", mm3)
                p.wait(nc.scalar, "VE", prev_yD_ve)
                sc2 = p.inc(nc.scalar.activation(z2[:, j, :], bank3[:, :TBM], AF.Silu,
                                                 bias=cb[:, 1, j:j + 1]), "SC")
                ps_guard[oc % 2] = {"SC": sc2}
            else:
                j = oc - 16
                p.wait(nc.scalar, "PE", mm)
                p.wait(nc.scalar, "VE", prev_ygv)
                sc = p.inc(nc.scalar.activation(sig[:, j, :], bank[:, :TBM], AF.Sigmoid), "SC")
                ps_guard[oc % 2] = {"SC": sc}
        z2sc = p.cnt["SC"]
        if MAMBA_DEBUG:
            p.wait(nc.sync, "VE", xnv)
            p.inc(nc.sync.dma_start(dbg["dbg_xn"][:, :, t0:t0 + TBM], xn_sb[:, :, :]), "DBG", dma=True)
            p.wait(nc.sync, "SC", z2sc)
            p.inc(nc.sync.dma_start(dbg["dbg_z2"][:, :, t0:t0 + TBM], z2[:, :, :]), "DBG", dma=True)
            p.inc(nc.sync.dma_start(dbg["dbg_sig"][:, :, t0:t0 + TBM], sig[:, :, :]), "DBG", dma=True)
        bank = ps_mm[0]
        p.wait_counts(nc.tensor, ps_guard[0])
        p.wait(nc.tensor, "SC", z2sc)
        for ic in range(16):
            inst = nc.tensor.matmul(bank[:, :TBM], wx[:, ic, :], z2[:, ic, :],
                                    start=(ic == 0), stop=(ic == 15))
        mmx = p.inc(inst, "PE")
        p.wait(nc.scalar, "PE", mmx)
        scp = p.inc(nc.scalar.copy(praw[:, :], bank[:, :TBM]), "SC")
        scpb = p.inc(nc.scalar.copy(praw_b[:, :], bank[:, :TBM]), "SC")
        ps_guard[0] = {"SC": scpb}
        p.wait_counts(nc.tensor, dtr_guard)
        p.wait(nc.tensor, "SC", scpb)
        mmd = p.inc(nc.tensor.matmul(ps_dtr[:, :TBM], sel[:, 0, :], praw_b[:, :],
                                     start=True, stop=True), "PE")
        p.wait(nc.scalar, "GP", p.cnt["GP"])
        for n in range(NST):
            p.wait_counts(nc.tensor, cbc_guard)
            mmb = p.inc(nc.tensor.matmul(ps_cbc[:, :TBM], sel[:, 1 + n, :], praw_b[:, :],
                                         start=True, stop=True), "PE")
            p.wait(nc.scalar, "PE", mmb)
            cbc_guard = {"SC": p.inc(nc.scalar.copy(Bbc[:, n, :], ps_cbc[:, :TBM]), "SC")}
        bbc_sc = p.cnt["SC"]
        for j in range(16):
            p.wait(nc.scalar, "PE", mmd)
            p.wait_counts(nc.scalar, {"VE": p.cnt["VE"]})
            nc.scalar.activation(scr2[:, :], ps_dtr[:, :TBM], AF.Abs,
                                 scale=dtw[:, j:j + 1], bias=dtb[:, j:j + 1])
            dr(nc.scalar)
            nc.scalar.activation(scr2[:, :], scr2[:, :], AF.Exp, scale=-1.0)
            dr(nc.scalar)
            nc.scalar.activation(scr2[:, :], scr2[:, :], AF.Ln, bias=1.0)
            scdt = p.inc(nc.scalar.activation(dt_sb[:, :], ps_dtr[:, :TBM], AF.Relu,
                                 scale=dtw[:, j:j + 1], bias=dtb[:, j:j + 1]), "SC")
            p.wait(nc.vector, "SC", scdt)
            vedt = p.inc(nc.vector.tensor_add(dt_sb[:, :], dt_sb[:, :], scr2[:, :]), "VE")
            dr(nc.vector)
            p.wait(nc.vector, "GP", p.cnt["GP"])
            veu = p.inc(nc.vector.tensor_mul(u_sb[:, :], dt_sb[:, :], z2[:, j, :]), "VE")
            p.wait(nc.gpsimd, "VE", veu)
            p.wait(nc.gpsimd, "SC", bbc_sc)
            ab_guard = [0, 0]
            for n in range(NST):
                sn = n % 2
                p.wait(nc.scalar, "VE", max(vedt, ab_guard[sn]))
                sca = p.inc(nc.scalar.activation(a_sb[:, sn, :], dt_sb[:, :], AF.Identity,
                                                 scale=-float(n + 1), bias=1.0), "SC")
                if ab_guard[sn]:
                    p.wait(nc.gpsimd, "VE", ab_guard[sn])
                gpb = p.inc(nc.gpsimd.tensor_tensor(b_sb[:, sn, :], u_sb[:, :], Bbc[:, n, :],
                                                    ALU.mult), "GP")
                p.wait(nc.vector, "SC", sca)
                p.wait(nc.vector, "GP", gpb)
                init = 0.0 if tau == 0 else hc[:, j, n:n + 1]
                nc.vector.tensor_tensor_scan(h_sb[:, sn, :], a_sb[:, sn, :], b_sb[:, sn, :],
                                             init, ALU.mult, ALU.add)
                p.wait_counts(nc.tensor, cbc_guard)
                mmc = p.inc(nc.tensor.matmul(ps_cbc[:, :TBM], sel[:, 17 + n, :], praw_b[:, :],
                                             start=True, stop=True), "PE")
                p.wait(nc.vector, "PE", mmc)
                veyh = p.inc(nc.vector.tensor_mul(yh_sb[:, sn, :], h_sb[:, sn, :],
                                                  ps_cbc[:, :TBM]), "VE")
                nc.vector.tensor_copy(hc[:, j, n:n + 1], h_sb[:, sn, TBM - 1:TBM])
                cbc_guard = {"VE": veyh}
                ab_guard[sn] = veyh
                if n == 0:
                    p.wait_counts(nc.tensor, y_guard)
                p.wait(nc.tensor, "VE", veyh)
                mmy = p.inc(nc.tensor.matmul(ps_y[:, :TBM], ident[:, :], yh_sb[:, sn, :],
                                             start=(n == 0), stop=(n == NST - 1)), "PE")
            p.wait(nc.vector, "PE", mmy)
            p.wait(nc.vector, "SC", prev_mln_sc)
            y_guard = {"VE": p.inc(nc.vector.scalar_tensor_tensor(
                y_sb[:, j, :], z2[:, j, :], Dv[:, j:j + 1], ps_y[:, :TBM],
                ALU.mult, ALU.add), "VE")}
        if MAMBA_DEBUG:
            p.wait(nc.sync, "SC", scp)
            p.inc(nc.sync.dma_start(dbg["dbg_praw"][:, t0:t0 + TBM], praw[:, :]), "DBG", dma=True)
            p.wait_counts(nc.sync, y_guard)
            yb = y_sb[:, :, :].rearrange("p c t -> p (c t)")
            p.inc(nc.sync.dma_start(dbg["dbg_y"][:, :, t0:t0 + TBM], y_sb[:, :, :]), "DBG", dma=True)
            p.inc(nc.sync.dma_start(dbg["dbg_dt"][:, t0:t0 + TBM], dt_sb[:, :]), "DBG", dma=True)
        dtr_guard = {"SC": p.cnt["SC"]}
        prev_yD_ve = p.cnt["VE"]
        yv = p.cnt["VE"]
        for hh in range(2):
            ho = hh * 256
            yth = [y_sb[:, c, ho:ho + 256] for c in range(16)]
            inv_bc, o_bc, pe_bc = ln.stats(yth, {"VE": yv})
            p.wait(nc.vector, "SC", pe_bc)
            for c in range(16):
                nc.vector.tensor_mul(scr[:, :256], yth[c], inv_bc)
                dr(nc.vector)
                nc.vector.tensor_add(scr[:, :256], scr[:, :256], o_bc)
                dr(nc.vector)
                nc.vector.tensor_scalar(scr[:, :256], scr[:, :256], mg[:, c:c + 1],
                                        mb[:, c:c + 1], ALU.mult, ALU.add)
                dr(nc.vector)
                inst = nc.vector.tensor_mul(y_sb[:, c, ho:ho + 256], scr[:, :256],
                                            sig[:, c, ho:ho + 256])
                dr(nc.vector)
            p.inc(inst, "VE")
            ln.mark_used()
        prev_mln_sc = p.cnt["SC"]
        ygv = p.cnt["VE"]
        prev_ygv = ygv
        p.wait(nc.tensor, "VE", ygv)
        if x_hist[1 - sl] is not None:
            p.wait(nc.scalar, f"ST{1 - sl}", x_hist[1 - sl].get(f"ST{1 - sl}", 0))
        resv = 0
        for oc in range(8):
            ws = oc % 2
            p.wait_counts(nc.sync, wo_guard[ws])
            wld = p.inc(nc.sync.dma_start(wout_sb[:, ws, :, :], wout_d[:, oc, :, :]), f"LO{ws}", dma=True)
            bank = ps_mm[oc % 2]
            p.wait_counts(nc.tensor, ps_guard[oc % 2])
            p.wait(nc.tensor, f"LO{ws}", wld)
            for ic in range(16):
                inst = nc.tensor.matmul(bank[:, :TBM], wout_sb[:, ws, ic, :], y_sb[:, ic, :],
                                        start=(ic == 0), stop=(ic == 15))
            mm = p.inc(inst, "PE")
            wo_guard[ws] = {"PE": mm}
            p.wait(nc.vector, "PE", mm)
            resv = p.inc(nc.vector.tensor_add(x_sb[:, sl, oc, :], x_sb[:, sl, oc, :],
                                              bank[:, :TBM]), "VE")
            ps_guard[oc % 2] = {"VE": resv}
        p.wait(nc.sync, "VE", resv)
        st = p.inc(nc.sync.dma_start(xout[:, :, t0:t0 + TBM], x_sb[:, sl, :, :]), f"ST{sl}", dma=True)
        x_hist[sl] = {"VE": resv, "PE": p.cnt["PE"], "SC": p.cnt["SC"], f"ST{sl}": st}
    p.finish()
    return p




def emit_opmlp(p: Prog):
    nc = p.nc
    xin = ch_major(p.stream("xin", (D, T)))
    tin = ch_major(p.stream("tin", (D, T)))
    rin = ch_major(p.stream("rin", (D, T)))
    xout = ch_major(p.stream("xout", (D, T), out=True))
    w1 = load_w_bf(p, "OP1", 8, 1024)
    w2 = load_w_bf(p, "OP2", 8, 1024)
    b1 = load_small(p, "OP1_b", (128, 8))
    b2 = load_small(p, "OP2_b", (128, 8))
    p.wait(nc.tensor, "LD")
    p.wait(nc.scalar, "LD")
    p.wait(nc.vector, "LD")

    x_sb = p.sbuf("x_sb", (128, 2, 8, TB), F32)
    t_sb = p.sbuf("t_sb", (128, 2, 8, TB), F32)
    r_sb = p.sbuf("r_sb", (128, 2, 8, TB), F32)
    xb = p.sbuf("xb", (128, 8, TB), BF16)
    h_sb = p.sbuf("h_sb", (128, 8, TB), BF16)
    o_sb = p.sbuf("o_sb", (128, 8, TB), F32)
    scr = p.sbuf("scr", (128, TB), F32)
    ln = LN(p, "op_ln")
    ps_mm = [p.psum(f"op_mm{i}", (128, 512), F32) for i in range(2)]

    x_hist = [None, None]
    ps_guard = [{}, {}]
    blk_hist = None
    for tau in range(NT):
        dr(nc.vector)
        dr(nc.scalar)
        sl = tau % 2
        t0 = tau * TB
        if x_hist[sl] is not None:
            p.wait_counts(nc.sync, x_hist[sl])
        ld = p.inc(nc.sync.dma_start(x_sb[:, sl, :, :], xin[:, :, t0:t0 + TB]), f"LX{sl}", dma=True)
        ld2 = p.inc(nc.sync.dma_start(t_sb[:, sl, :, :], tin[:, :, t0:t0 + TB]), f"LT{sl}", dma=True)
        ld3 = p.inc(nc.sync.dma_start(r_sb[:, sl, :, :], rin[:, :, t0:t0 + TB]), f"LR{sl}", dma=True)
        p.wait(nc.scalar, f"LX{sl}", ld)
        if blk_hist is not None:
            p.wait_counts(nc.scalar, {"PE": blk_hist["PE"]})
        for c in range(8):
            inst = nc.scalar.copy(xb[:, c, :], x_sb[:, sl, c, :])
        xbv = p.inc(inst, "SC")
        p.wait(nc.tensor, "SC", xbv)
        for oc in range(8):
            bank = ps_mm[oc % 2]
            p.wait_counts(nc.tensor, ps_guard[oc % 2])
            for ic in range(8):
                inst = nc.tensor.matmul(bank[:, :TB], w1[:, ic, oc * 128:(oc + 1) * 128],
                                        xb[:, ic, :], start=(ic == 0), stop=(ic == 7))
            mm = p.inc(inst, "PE")
            p.wait(nc.scalar, "PE", mm)
            ps_guard[oc % 2] = {"SC": p.inc(
                nc.scalar.activation(h_sb[:, oc, :], bank[:, :TB], AF.Gelu,
                                     bias=b1[:, oc:oc + 1]), "SC")}
        hsc = p.cnt["SC"]
        p.wait(nc.tensor, "SC", hsc)
        p.wait(nc.vector, f"LT{sl}", ld2)
        p.wait(nc.vector, "ST0", p.cnt.get("ST0", 0))
        p.wait(nc.vector, "ST1", p.cnt.get("ST1", 0))
        ve_o = 0
        for oc in range(8):
            bank = ps_mm[oc % 2]
            p.wait_counts(nc.tensor, ps_guard[oc % 2])
            for ic in range(8):
                inst = nc.tensor.matmul(bank[:, :TB], w2[:, ic, oc * 128:(oc + 1) * 128],
                                        h_sb[:, ic, :], start=(ic == 0), stop=(ic == 7))
            mm = p.inc(inst, "PE")
            p.wait(nc.vector, "PE", mm)
            ve_o = p.inc(nc.vector.scalar_tensor_tensor(
                o_sb[:, oc, :], bank[:, :TB], b2[:, oc:oc + 1], t_sb[:, sl, oc, :],
                ALU.add, ALU.add), "VE")
            dr(nc.vector)
            ps_guard[oc % 2] = {"VE": ve_o}
        ot = [o_sb[:, c, :] for c in range(8)]
        inv2, o2, pe_bc2 = ln.stats(ot, {"VE": ve_o})
        p.wait(nc.vector, "SC", pe_bc2)
        p.wait(nc.vector, f"LR{sl}", ld3)
        for c in range(8):
            nc.vector.tensor_mul(scr[:, :], ot[c], inv2)
            dr(nc.vector)
            nc.vector.tensor_add(scr[:, :], scr[:, :], o2)
            dr(nc.vector)
            inst = nc.vector.tensor_add(o_sb[:, c, :], scr[:, :], r_sb[:, sl, c, :])
            dr(nc.vector)
        res_ve = p.inc(inst, "VE")
        ln.mark_used()
        p.wait(nc.sync, "VE", res_ve)
        p.inc(nc.sync.dma_start(xout[:, :, t0:t0 + TB], o_sb[:, :, :]), f"ST{sl}", dma=True)
        x_hist[sl] = {"VE": res_ve, "PE": p.cnt["PE"], "SC": p.cnt["SC"]}
        blk_hist = p.snap()
    p.finish()
    return p


def emit_ln_only(p: Prog):
    nc = p.nc
    xin = ch_major(p.stream("xin", (D, T)))
    xout = ch_major(p.stream("xout", (D, T), out=True))
    x_sb = p.sbuf("x_sb", (128, 2, 8, TB), F32)
    o_sb = p.sbuf("o_sb", (128, 2, 8, TB), F32)
    ln = LN(p, "lno")
    x_hist = [None, None]
    for tau in range(NT):
        dr(nc.vector)
        dr(nc.scalar)
        sl = tau % 2
        t0 = tau * TB
        if x_hist[sl] is not None:
            p.wait_counts(nc.sync, x_hist[sl])
        ld = p.inc(nc.sync.dma_start(x_sb[:, sl, :, :], xin[:, :, t0:t0 + TB]), f"LX{sl}", dma=True)
        xt = [x_sb[:, sl, c, :] for c in range(8)]
        inv_bc, o_bc, pe_bc = ln.stats(xt, {f"LX{sl}": ld})
        p.wait(nc.vector, "SC", pe_bc)
        for c in range(8):
            ve = ln_apply(p, xt[c], o_sb[:, sl, c, :], inv_bc, o_bc, x_sb[:, sl, c, :] if False else o_sb[:, sl, c, :])
        ln.mark_used()
        p.wait(nc.sync, "VE", ve)
        st = p.inc(nc.sync.dma_start(xout[:, :, t0:t0 + TB], o_sb[:, sl, :, :]), f"ST{sl}", dma=True)
        x_hist[sl] = {"VE": ve, "PE": p.cnt["PE"], "SC": p.cnt["SC"], f"ST{sl}": st}
    p.finish()
    return p


def emit_final(p: Prog):
    nc = p.nc
    xin = ch_major(p.stream("xin", (D, T)))
    rin = ch_major(p.stream("rin", (D, T)))
    out = p.dram_out("out", (T, D))
    ident = load_small(p, "ident128f", (128, 128), F32)
    p.wait(nc.tensor, "LD")
    x_sb = p.sbuf("x_sb", (128, 2, 8, TB), F32)
    r_sb = p.sbuf("r_sb", (128, 2, 8, TB), F32)
    o_sb = p.sbuf("o_sb", (128, 8, TB), F32)
    ot_sb = p.sbuf("ot_sb", (128, 2, 8, 128), F32)
    scr = p.sbuf("scr", (128, TB), F32)
    ln = LN(p, "fin_ln")
    ps_t = [p.psum(f"fin_t{i}", (128, 512), F32) for i in range(2)]
    x_hist = [None, None]
    tr_guard = [{}, {}]
    ot_guard = [{}, {}]
    o_guard = {}
    for tau in range(NT):
        dr(nc.vector)
        dr(nc.scalar)
        sl = tau % 2
        t0 = tau * TB
        if x_hist[sl] is not None:
            p.wait_counts(nc.sync, x_hist[sl])
        ld = p.inc(nc.sync.dma_start(x_sb[:, sl, :, :], xin[:, :, t0:t0 + TB]), f"LX{sl}", dma=True)
        ld2 = p.inc(nc.sync.dma_start(r_sb[:, sl, :, :], rin[:, :, t0:t0 + TB]), f"LR{sl}", dma=True)
        xt = [x_sb[:, sl, c, :] for c in range(8)]
        inv_bc, o_bc, pe_bc = ln.stats(xt, {f"LX{sl}": ld})
        p.wait(nc.vector, "SC", pe_bc)
        p.wait(nc.vector, f"LR{sl}", ld2)
        p.wait_counts(nc.vector, o_guard)
        for c in range(8):
            nc.vector.tensor_mul(scr[:, :], xt[c], inv_bc)
            dr(nc.vector)
            nc.vector.tensor_add(scr[:, :], scr[:, :], o_bc)
            dr(nc.vector)
            inst = nc.vector.tensor_add(o_sb[:, c, :], scr[:, :], r_sb[:, sl, c, :])
            dr(nc.vector)
        res_ve = p.inc(inst, "VE")
        ln.mark_used()
        p.wait(nc.tensor, "VE", res_ve)
        o_guard = {"PE": None}
        for half in range(2):
            for grp in range(2):
                bank = ps_t[grp]
                p.wait_counts(nc.tensor, tr_guard[grp])
                for cc in range(4):
                    c = grp * 4 + cc
                    inst = nc.tensor.transpose(bank[:, cc * 128:(cc + 1) * 128],
                                               o_sb[:, c, half * 128:(half + 1) * 128],
                                               ident[:, :])
                mm = p.inc(inst, "PE")
                p.wait(nc.scalar, "PE", mm)
                p.wait_counts(nc.scalar, ot_guard[sl])
                sc = p.inc(nc.scalar.copy(
                    ot_sb[:, sl, grp * 4:(grp + 1) * 4, :].rearrange("p a b -> p (a b)"),
                    bank[:, :]), "SC")
                tr_guard[grp] = {"SC": sc}
            p.wait(nc.sync, "SC", p.cnt["SC"])
            st = p.inc(nc.sync.dma_start(
                out[t0 + half * 128: t0 + (half + 1) * 128, :],
                ot_sb[:, sl, :, :].rearrange("p a b -> p (a b)")), f"ST{sl}{half}", dma=True)
            ot_guard[sl] = {f"ST{sl}{half}": st}
        o_guard = {"PE": p.cnt["PE"]}
        x_hist[sl] = {"VE": res_ve, "PE": p.cnt["PE"], "SC": p.cnt["SC"]}
    p.finish()
    return p




def emit_ffn_attn(p: Prog, i):
    nc = p.nc
    xin = ch_major(p.stream("xin", (D, T)))
    xout = ch_major(p.stream("xout", (D, T), out=True))
    wqkv = load_w_bf(p, f"F{i}_Wqkv", 8, 3072)
    wo = load_w_bf(p, f"F{i}_Wo", 8, 1024)
    qs = load_small(p, f"F{i}_qs", (128, 1))
    ks = load_small(p, f"F{i}_ks", (128, 1))
    qsp = load_small(p, f"F{i}_qsp", (128, 1))
    ksp = load_small(p, f"F{i}_ksp", (128, 1))
    rbd = load_small(p, "RbdT", (128, 128), BF16)
    idb = load_small(p, "ident128", (128, 128), BF16)
    bd2 = load_small(p, "bd2", (128, 2))
    bdt = load_small(p, "bdT", (2, 128))
    mka = load_small(p, "mask_add_t", (128, 2, 512))
    mka0 = load_small(p, "mask_add0_t", (128, 2, 256))
    rot_d = {nm: p.dram_in(nm, (128, T)) for nm in ("rot_qc", "rot_qs", "rot_kc", "rot_ks")}
    p.wait(nc.tensor, "LD")
    p.wait(nc.scalar, "LD")
    p.wait(nc.vector, "LD")

    x_sb = p.sbuf("x_sb", (128, 2, 8, 1 + TB), F32)
    xb = p.sbuf("xb", (128, 8, 1 + TB), BF16)
    q_sb = p.sbuf("q_sb", (128, 8, TB), F32)
    k_sb = p.sbuf("k_sb", (128, 8, TB), F32)
    v_sb = p.sbuf("v_sb", (128, 2, 8, TB), BF16)
    qf_sb = p.sbuf("qf_sb", (128, 8, TB), BF16)
    kf_sb = p.sbuf("kf_sb", (128, 2, 8, TB), BF16)
    vt_sb = p.sbuf("vt_sb", (128, 2, 16, 2, 64), BF16)
    rt_sb = p.sbuf("rt_sb", (128, 4, TB), F32)
    tq_sb = p.sbuf("tq_sb", (128, 4, TB), F32)
    sqq = p.sbuf("sqq", (128, TB), F32)
    nrm = p.sbuf("nrm", (2, 2, 8, TB), F32)
    qn_b = p.sbuf("qn_b", (128, TB), BF16)
    invq_sb = p.sbuf("invq_sb", (128, TB), F32)
    rot_sb = p.sbuf("rot_sb", (128, TB), F32)
    scr = p.sbuf("scr", (128, TB), F32)
    scr2 = p.sbuf("scr2", (128, TB), F32)
    sim_m = p.sbuf("sim_m", (128, 512), F32)
    attn_b = p.sbuf("attn_b", (128, 512), BF16)
    attn_n = p.sbuf("attn_n", (128, 512), BF16)
    mx = p.sbuf("mx", (128, 4), F32)
    at_sb = p.sbuf("at_sb", (128, 4, 128), BF16)
    ob_sb = p.sbuf("ob_sb", (128, 8, TB), BF16)

    ps_mm = [p.psum(f"fa_mm{k}", (128, 512), F32) for k in range(2)]
    ps_sim = p.psum("fa_sim", (128, 512), F32)
    ps_at = p.psum("fa_at", (128, 512), BF16)
    ps_av = [p.psum(f"fa_av{k}", (128, 256), F32) for k in range(2)]
    ps_bc = p.psum("fa_bc", (128, 512), F32)

    x_hist = [None, None]
    ps_guard = [{}, {}]
    sim_guard = {}
    at_guard = {}
    av_guard = [{}, {}]
    bc_guard = {}
    prev = {}
    for tau in range(NT):
        dr(nc.vector)
        dr(nc.scalar)
        sl = tau % 2
        t0 = tau * TB
        if x_hist[sl] is not None:
            p.wait_counts(nc.sync, x_hist[sl])
        if tau == 0:
            ld = p.inc(nc.sync.dma_start(x_sb[:, sl, :, 1:], xin[:, :, 0:TB]), f"LX{sl}", dma=True)
            p.wait(nc.vector, f"LX{sl}", ld)
            ve0 = p.inc(nc.vector.memset(x_sb[:, sl, :, 0:1], 0.0), "VE")
            xld = {f"LX{sl}": ld, "VE": ve0}
        else:
            ld = p.inc(nc.sync.dma_start(x_sb[:, sl, :, :], xin[:, :, t0 - 1:t0 + TB]),
                       f"LX{sl}", dma=True)
            xld = {f"LX{sl}": ld}
        ldr = 0
        for r, nm in enumerate(("rot_qc", "rot_qs", "rot_kc", "rot_ks")):
            p.wait_counts(nc.sync, prev.get("rt", {}))
            ldr = p.inc(nc.sync.dma_start(rt_sb[:, r, :], rot_d[nm][:, t0:t0 + TB]),
                        f"LQ{sl}", dma=True)
        p.wait(nc.vector, f"LQ{sl}", ldr)
        p.wait_counts(nc.vector, prev.get("tq", {}))
        nc.vector.tensor_scalar(tq_sb[:, 0, :], rt_sb[:, 0, :], qs[:, 0:1], None, ALU.mult)
        nc.vector.tensor_scalar(tq_sb[:, 1, :], rt_sb[:, 1, :], qsp[:, 0:1], None, ALU.mult)
        nc.vector.tensor_scalar(tq_sb[:, 2, :], rt_sb[:, 2, :], ks[:, 0:1], None, ALU.mult)
        tqv = p.inc(nc.vector.tensor_scalar(tq_sb[:, 3, :], rt_sb[:, 3, :], ksp[:, 0:1],
                                            None, ALU.mult), "VE")
        prev["rt"] = {"VE": tqv}
        p.wait_counts(nc.scalar, xld)
        p.wait_counts(nc.scalar, prev.get("xb", {}))
        for c in range(8):
            inst = nc.scalar.copy(xb[:, c, :], x_sb[:, sl, c, :])
        xbv = p.inc(inst, "SC")
        prev["xb"] = {}
        p.wait(nc.tensor, "SC", xbv)
        for oc in range(24):
            bank = ps_mm[oc % 2]
            p.wait_counts(nc.tensor, ps_guard[oc % 2])
            for ic in range(8):
                off = 1 if ic < 4 else 0
                inst = nc.tensor.matmul(bank[:, :TB], wqkv[:, ic, oc * 128:(oc + 1) * 128],
                                        xb[:, ic, off:off + TB], start=(ic == 0), stop=(ic == 7))
            mm = p.inc(inst, "PE")
            p.wait(nc.scalar, "PE", mm)
            if oc < 8:
                p.wait_counts(nc.scalar, prev.get("q", {}))
                sc = p.inc(nc.scalar.copy(q_sb[:, oc, :], bank[:, :TB]), "SC")
            elif oc < 16:
                p.wait_counts(nc.scalar, prev.get("k", {}))
                sc = p.inc(nc.scalar.copy(k_sb[:, oc - 8, :], bank[:, :TB]), "SC")
            else:
                p.wait_counts(nc.scalar, prev.get("v", {}))
                sc = p.inc(nc.scalar.copy(v_sb[:, sl, oc - 16, :], bank[:, :TB]), "SC")
            ps_guard[oc % 2] = {"SC": sc}
        qkv_sc = p.cnt["SC"]
        p.wait(nc.scalar, "SC", qkv_sc)
        for qk, src in ((0, q_sb), (1, k_sb)):
            for c in range(8):
                p.wait_counts(nc.scalar, {"PE": p.cnt["PE"]})
                dr(nc.scalar)
                ssc = p.inc(nc.scalar.activation(sqq[:, :], src[:, c, :], AF.Square), "SC")
                p.wait(nc.tensor, "SC", ssc)
                p.wait_counts(nc.tensor, bc_guard)
                mm = p.inc(nc.tensor.matmul(ps_bc[0:2, 0:TB], bd2[:, :], sqq[:, :],
                                            start=True, stop=True), "PE")
                p.wait(nc.scalar, "PE", mm)
                p.wait_counts(nc.scalar, prev.get("nrm", {}))
                bc_guard = {"SC": p.inc(nc.scalar.copy(nrm[0:2, qk, c, :], ps_bc[0:2, 0:TB]),
                                        "SC")}
        nsc = p.cnt["SC"]
        nflat = nrm[0:2, :, :, :].rearrange("p a c t -> p (a c t)")
        p.wait(nc.vector, "SC", nsc)
        vem = p.inc(nc.vector.tensor_scalar(nflat, nflat, 1e-24, None, ALU.max), "VE")
        p.wait(nc.scalar, "VE", vem)
        dr(nc.scalar)
        nc.scalar.activation(nflat, nflat, AF.Ln)
        dr(nc.scalar)
        ninv = p.inc(nc.scalar.activation(nflat, nflat, AF.Exp, scale=-0.5), "SC")
        prev["nrm"] = {"VE": 0}
        p.wait(nc.tensor, "SC", ninv)
        for qk, src, dst, ci, si in ((0, q_sb, qf_sb, 0, 1), (1, k_sb, kf_sb, 2, 3)):
            for c in range(8):
                p.wait_counts(nc.tensor, bc_guard)
                mmb = p.inc(nc.tensor.matmul(ps_bc[:, 0:TB], bdt[:, :], nrm[0:2, qk, c, :],
                                             start=True, stop=True), "PE")
                p.wait(nc.scalar, "PE", mmb)
                p.wait_counts(nc.scalar, prev.get("invq", {}))
                scb = p.inc(nc.scalar.copy(invq_sb[:, :], ps_bc[:, 0:TB]), "SC")
                bc_guard = {"SC": scb}
                p.wait(nc.vector, "SC", scb)
                p.wait(nc.vector, "VE", tqv)
                p.wait_counts(nc.vector, prev.get("qnb", {}))
                veq = p.inc(nc.vector.tensor_mul(qn_b[:, :], src[:, c, :], invq_sb[:, :]), "VE")
                prev["invq"] = {"VE": veq}
                p.wait(nc.tensor, "VE", veq)
                p.wait_counts(nc.tensor, prev.get("rot", {}))
                mmr = p.inc(nc.tensor.matmul(ps_sim[:, 0:TB], rbd[:, :], qn_b[:, :],
                                             start=True, stop=True), "PE")
                p.wait(nc.scalar, "PE", mmr)
                p.wait_counts(nc.scalar, prev.get("rotsb", {}))
                scr_ = p.inc(nc.scalar.copy(rot_sb[:, :], ps_sim[:, 0:TB]), "SC")
                prev["rot"] = {"SC": scr_}
                nc.vector.tensor_mul(scr[:, :], qn_b[:, :], tq_sb[:, ci, :])
                p.wait(nc.vector, "SC", scr_)
                nc.vector.tensor_mul(scr2[:, :], rot_sb[:, :], tq_sb[:, si, :])
                if qk == 0:
                    inst = nc.vector.tensor_add(dst[:, c, :], scr[:, :], scr2[:, :])
                else:
                    inst = nc.vector.tensor_add(dst[:, sl, c, :], scr[:, :], scr2[:, :])
                ver = p.inc(inst, "VE")
                prev["rotsb"] = {"VE": ver}
                prev["qnb"] = {"VE": ver}
        rot_ve = p.cnt["VE"]
        sim_guard = {"VE": rot_ve}
        p.wait(nc.tensor, "SC", qkv_sc)
        for h in range(16):
            pb = (h % 2) * 64
            ct = h // 2
            for jc in range(2):
                p.wait_counts(nc.tensor, at_guard)
                inst = nc.tensor.transpose(ps_at[0:128, 0:64],
                                           v_sb[:, sl, ct, jc * 128:(jc + 1) * 128][pb:pb + 64, :],
                                           idb[:, :][pb:pb + 64, pb:pb + 64])
                mm = p.inc(inst, "PE")
                p.wait(nc.scalar, "PE", mm)
                p.wait_counts(nc.scalar, prev.get("vt", {}))
                at_guard = {"SC": p.inc(nc.scalar.copy(vt_sb[:, sl, h, jc, :], ps_at[0:128, 0:64]),
                                        "SC")}
        prev["vt"] = {}
        vt_sc = p.cnt["SC"]
        p.wait(nc.tensor, "VE", rot_ve)
        p.wait(nc.tensor, "SC", vt_sc)
        for ct in range(8):
            avb = ps_av[ct % 2]
            p.wait_counts(nc.tensor, av_guard[ct % 2])
            for hh in range(2):
                h = ct * 2 + hh
                pb = hh * 64
                for m in range(2):
                    qT = qf_sb[:, ct, m * 128:(m + 1) * 128][pb:pb + 64, :]
                    wid = 512 if tau > 0 else 256
                    p.wait_counts(nc.tensor, sim_guard)
                    if tau > 0:
                        nc.tensor.matmul(ps_sim[:, 0:256], qT,
                                         kf_sb[:, 1 - sl, ct, :][pb:pb + 64, :],
                                         start=True, stop=True)
                    mms = p.inc(nc.tensor.matmul(
                        ps_sim[:, (wid - 256):wid] if tau > 0 else ps_sim[:, 0:256],
                        qT, kf_sb[:, sl, ct, :][pb:pb + 64, :],
                        start=True, stop=True), "PE")
                    p.wait(nc.vector, "PE", mms)
                    p.wait_counts(nc.vector, prev.get("simm", {}))
                    msk = mka[:, m, 0:wid] if tau > 0 else mka0[:, m, 0:wid]
                    nc.vector.scalar_tensor_tensor(sim_m[:, 0:wid], ps_sim[:, 0:wid], 8.0,
                                                   msk, ALU.mult, ALU.add)
                    nc.vector.drain()
                    vet = p.inc(nc.vector.reduce_max(mx[:, 1:2], sim_m[:, 0:wid],
                                                     AX.X, negate=True), "VE")
                    ven = vet
                    p.wait(nc.scalar, "VE", ven)
                    p.wait_counts(nc.scalar, prev.get("attnb", {}))
                    sce = p.inc(nc.scalar.activation(attn_b[:, 0:wid], sim_m[:, 0:wid], AF.Exp,
                                                     bias=mx[:, 1:2], accum_out=mx[:, 2:3]),
                                "SC")
                    p.wait(nc.vector, "SC", sce)
                    nc.vector.reciprocal(mx[:, 3:4], mx[:, 2:3])
                    nc.vector.drain()
                    vesc = p.inc(nc.vector.tensor_scalar(attn_n[:, 0:wid], attn_b[:, 0:wid],
                                                         mx[:, 3:4], None, ALU.mult), "VE")
                    prev["simm"] = {"SC": sce}
                    prev["attnb"] = {"VE": vesc}
                    sim_guard = {"VE": vet}
                    p.wait(nc.tensor, "VE", vesc)
                    nch = wid // 128
                    for jc in range(nch):
                        p.wait_counts(nc.tensor, at_guard)
                        inst = nc.tensor.transpose(ps_at[:, jc * 128:(jc + 1) * 128],
                                                   attn_n[:, jc * 128:(jc + 1) * 128],
                                                   idb[:, :])
                    mmt = p.inc(inst, "PE")
                    p.wait(nc.scalar, "PE", mmt)
                    p.wait_counts(nc.scalar, prev.get("at", {}))
                    sca = p.inc(nc.scalar.copy(
                        at_sb[:, 0:nch, :].rearrange("p a b -> p (a b)"),
                        ps_at[:, 0:wid]), "SC")
                    at_guard = {"SC": sca}
                    prev["at"] = {"PE": 0}
                    p.wait(nc.tensor, "SC", sca)
                    chunks = []
                    if tau > 0:
                        chunks = [(1 - sl, 0, 0), (1 - sl, 1, 1), (sl, 0, 2), (sl, 1, 3)]
                    else:
                        chunks = [(sl, 0, 0), (sl, 1, 1)]
                    for (bp, vjc, ajc) in chunks:
                        inst = nc.tensor.matmul(
                            avb[pb:pb + 64, m * 128:(m + 1) * 128],
                            vt_sb[:, bp, h, vjc, :], at_sb[:, ajc, :],
                            start=(ajc == chunks[0][2]), stop=(ajc == chunks[-1][2]))
                    mmv = p.inc(inst, "PE")
                    at_guard = {"PE": mmv}
            p.wait(nc.scalar, "PE", p.cnt["PE"])
            p.wait_counts(nc.scalar, prev.get("ob", {}))
            av_guard[ct % 2] = {"SC": p.inc(nc.scalar.copy(ob_sb[:, ct, :], avb[:, :]), "SC")}
        ob_sc = p.cnt["SC"]
        prev["ob"] = {}
        p.wait(nc.tensor, "SC", ob_sc)
        res_ve = 0
        for oc in range(8):
            bank = ps_mm[oc % 2]
            p.wait_counts(nc.tensor, ps_guard[oc % 2])
            for ic in range(8):
                inst = nc.tensor.matmul(bank[:, :TB], wo[:, ic, oc * 128:(oc + 1) * 128],
                                        ob_sb[:, ic, :], start=(ic == 0), stop=(ic == 7))
            mm = p.inc(inst, "PE")
            p.wait(nc.vector, "PE", mm)
            p.wait_counts(nc.vector, x_hist[sl] or {})
            res_ve = p.inc(nc.vector.tensor_add(x_sb[:, sl, oc, 1:], x_sb[:, sl, oc, 1:],
                                                bank[:, :TB]), "VE")
            dr(nc.vector)
            ps_guard[oc % 2] = {"VE": res_ve}
        p.wait(nc.sync, "VE", res_ve)
        st = p.inc(nc.sync.dma_start(xout[:, :, t0:t0 + TB], x_sb[:, sl, :, 1:]),
                   f"ST{sl}", dma=True)
        x_hist[sl] = {"VE": res_ve, "PE": p.cnt["PE"], "SC": p.cnt["SC"], f"ST{sl}": st}
        prev["q"] = {"VE": rot_ve}
        prev["k"] = {"VE": rot_ve}
        prev["v"] = {"PE": p.cnt["PE"]}
        prev["xb"] = {"PE": p.cnt["PE"]}
        prev["tq"] = {"VE": rot_ve}
    p.finish()
    return p




def emit_ffn_ffn(p: Prog, i):
    nc = p.nc
    xin = ch_major(p.stream("xin", (D, T)))
    xout = ch_major(p.stream("xout", (D, T), out=True))
    w1 = load_w_bf(p, f"F{i}_W1", 8, 5632)
    w2 = load_w_bf(p, f"F{i}_W2", 22, 1024)
    lg = load_small(p, f"F{i}_ln_g", (128, 8))
    lb = load_small(p, f"F{i}_ln_b", (128, 8))
    b1a = load_small(p, f"F{i}_b1a", (128, 22))
    b1g = load_small(p, f"F{i}_b1g", (128, 22))
    b2 = load_small(p, f"F{i}_b2", (128, 8))
    p.wait(nc.tensor, "LD")
    p.wait(nc.scalar, "LD")
    p.wait(nc.vector, "LD")

    x_sb = p.sbuf("x_sb", (128, 2, 8, 1 + TB), F32)
    xn_sb = p.sbuf("xn_sb", (128, 8, TB), BF16)
    a_sb = p.sbuf("a_sb", (128, 22, TB), BF16)
    g_sb = p.sbuf("g_sb", (128, TB), BF16)
    hg_sb = p.sbuf("hg_sb", (128, 22, TB), BF16)
    scr = p.sbuf("scr", (128, TB), F32)
    ln = LN(p, "ff_ln")
    ps_mm = [p.psum(f"ff_mm{k}", (128, 512), F32) for k in range(2)]

    x_hist = [None, None]
    ps_guard = [{}, {}]
    prev = {}
    for tau in range(NT):
        dr(nc.vector)
        dr(nc.scalar)
        sl = tau % 2
        t0 = tau * TB
        if x_hist[sl] is not None:
            p.wait_counts(nc.sync, x_hist[sl])
        if tau == 0:
            ld = p.inc(nc.sync.dma_start(x_sb[:, sl, :, 1:], xin[:, :, 0:TB]), f"LX{sl}", dma=True)
            p.wait(nc.vector, f"LX{sl}", ld)
            ve0 = p.inc(nc.vector.memset(x_sb[:, sl, :, 0:1], 0.0), "VE")
            xld = {f"LX{sl}": ld, "VE": ve0}
        else:
            ld = p.inc(nc.sync.dma_start(x_sb[:, sl, :, :], xin[:, :, t0 - 1:t0 + TB]),
                       f"LX{sl}", dma=True)
            xld = {f"LX{sl}": ld}
        xsh = [x_sb[:, sl, c, (1 if c < 4 else 0):(1 if c < 4 else 0) + TB] for c in range(8)]
        inv_bc, o_bc, pe_bc = ln.stats(xsh, xld)
        p.wait(nc.vector, "SC", pe_bc)
        if prev:
            p.wait_counts(nc.vector, {"PE": prev["PE"]})
        for c in range(8):
            nc.vector.tensor_mul(scr[:, :], xsh[c], inv_bc)
            dr(nc.vector)
            nc.vector.tensor_add(scr[:, :], scr[:, :], o_bc)
            dr(nc.vector)
            inst = nc.vector.tensor_scalar(xn_sb[:, c, :], scr[:, :], lg[:, c:c + 1],
                                           lb[:, c:c + 1], ALU.mult, ALU.add)
            dr(nc.vector)
        xnv = p.inc(inst, "VE")
        ln.mark_used()
        p.wait(nc.tensor, "VE", xnv)
        for oc in range(22):
            bank = ps_mm[oc % 2]
            p.wait_counts(nc.tensor, ps_guard[oc % 2])
            for ic in range(8):
                inst = nc.tensor.matmul(bank[:, :TB], w1[:, ic, oc * 128:(oc + 1) * 128],
                                        xn_sb[:, ic, :], start=(ic == 0), stop=(ic == 7))
            mm = p.inc(inst, "PE")
            p.wait(nc.scalar, "PE", mm)
            p.wait_counts(nc.scalar, prev.get("a", {}))
            ps_guard[oc % 2] = {"SC": p.inc(
                nc.scalar.activation(a_sb[:, oc, :], bank[:, :TB], AF.Identity,
                                     bias=b1a[:, oc:oc + 1]), "SC")}
        for oc in range(22):
            bank = ps_mm[oc % 2]
            p.wait_counts(nc.tensor, ps_guard[oc % 2])
            for ic in range(8):
                inst = nc.tensor.matmul(bank[:, :TB], w1[:, ic, (22 + oc) * 128:(23 + oc) * 128],
                                        xn_sb[:, ic, :], start=(ic == 0), stop=(ic == 7))
            mm = p.inc(inst, "PE")
            p.wait(nc.scalar, "PE", mm)
            p.wait_counts(nc.scalar, prev.get("g", {}))
            scg = p.inc(nc.scalar.activation(g_sb[:, :], bank[:, :TB], AF.Gelu,
                                             bias=b1g[:, oc:oc + 1]), "SC")
            p.wait(nc.vector, "SC", scg)
            p.wait_counts(nc.vector, prev.get("hg", {}))
            ve = p.inc(nc.vector.tensor_mul(hg_sb[:, oc, :], a_sb[:, oc, :], g_sb[:, :]), "VE")
            dr(nc.vector)
            ps_guard[oc % 2] = {"VE": ve}
            prev["g"] = {"VE": ve}
        hgv = p.cnt["VE"]
        p.wait(nc.tensor, "VE", hgv)
        res_ve = 0
        for oc in range(8):
            bank = ps_mm[oc % 2]
            p.wait_counts(nc.tensor, ps_guard[oc % 2])
            for ic in range(22):
                inst = nc.tensor.matmul(bank[:, :TB], w2[:, ic, oc * 128:(oc + 1) * 128],
                                        hg_sb[:, ic, :], start=(ic == 0), stop=(ic == 21))
            mm = p.inc(inst, "PE")
            p.wait(nc.vector, "PE", mm)
            res_ve = p.inc(nc.vector.scalar_tensor_tensor(
                x_sb[:, sl, oc, 1:], bank[:, :TB], b2[:, oc:oc + 1], x_sb[:, sl, oc, 1:],
                ALU.add, ALU.add), "VE")
            dr(nc.vector)
            ps_guard[oc % 2] = {"VE": res_ve}
        p.wait(nc.sync, "VE", res_ve)
        st = p.inc(nc.sync.dma_start(xout[:, :, t0:t0 + TB], x_sb[:, sl, :, 1:]),
                   f"ST{sl}", dma=True)
        x_hist[sl] = {"VE": res_ve, "PE": p.cnt["PE"], "SC": p.cnt["SC"], f"ST{sl}": st}
        prev = {"PE": p.cnt["PE"], "a": {"PE": p.cnt["PE"]}, "hg": {"PE": p.cnt["PE"]},
                "g": prev.get("g", {})}
    p.finish()
    return p


def _run(p: Prog, in_map_per_core):
    res = bass_utils.run_bass_kernel_spmd(p.nc, in_map_per_core, core_ids=list(range(len(in_map_per_core))))
    return res.results




def emit_wcast_fused(p: Prog, weights_meta):
    nc = p.nc
    CH = 16384
    stage = p.sbuf("wc_stage", (128, 2, CH), F32)
    out = p.sbuf("wc_out", (128, 2, CH), BF16)
    hist = {0: None, 1: None}
    k = 0
    for name, cols in weights_meta:
        src = p.dram_in(name, (128, cols))
        dst = p.dram_tmp(name + "_bfi", (128, cols), BF16)
        p.wbf[name] = dst
        for c0 in range(0, cols, CH):
            w = min(CH, cols - c0)
            sl = k % 2
            if hist[sl] is not None:
                nc.sync.wait_ge(p.sem("SC"), hist[sl][0])
                nc.sync.wait_ge(p.sem(f"WS{sl}"), hist[sl][1])
            ld = p.inc(nc.sync.dma_start(stage[:, sl, :w], src[:, c0:c0 + w]),
                       f"WL{sl}", dma=True)
            p.wait(nc.scalar, f"WL{sl}", ld)
            sc = p.inc(nc.scalar.copy(out[:, sl, :w], stage[:, sl, :w]), "SC")
            p.wait(nc.sync, "SC", sc)
            st = p.inc(nc.sync.dma_start(dst[:, c0:c0 + w], out[:, sl, :w]),
                       f"WS{sl}", dma=True)
            hist[sl] = (sc, st)
            k += 1
    for sl in range(2):
        if hist[sl] is not None:
            p.wait(nc.sync, f"WS{sl}")


def build_fused(weights_meta):
    p = Prog()
    nc = p.nc
    p.scope_open()
    emit_wcast_fused(p, weights_meta)
    p.scope_close()
    sA = p.dram_tmp("strA", (D, T))
    sB = p.dram_tmp("strB", (D, T))
    sX2 = p.dram_tmp("strX2", (D, T))
    sT0 = p.dram_tmp("strT0", (D, T))
    sX3 = p.dram_tmp("strX3", (D, T))

    def run(emit, streams, args=()):
        p.scope_open()
        p.streams = streams
        emit(p, *args)
        p.scope_close()
        p.streams = {}

    run(emit_l0_attn, {"xin": p.dram_in("xin_full", (D, T)), "xout": sA})
    run(emit_l0_ffn, {"xin": sA, "xout": sX2})
    run(emit_l1_pre, {"xin": sX2, "xout": sT0})
    cur, nxt = sA, sB
    src = sT0
    for l in range(4):
        run(emit_mamba, {"xin": src, "xout": cur}, (l,))
        src = cur
        cur, nxt = nxt, cur
    run(emit_opmlp, {"xin": src, "tin": sT0, "rin": sX2, "xout": sX3})
    cur, nxt = sA, sB
    run(emit_ln_only, {"xin": sX3, "xout": cur})
    src = cur
    cur, nxt = nxt, cur
    for i in range(6):
        run(emit_ffn_attn, {"xin": src, "xout": cur}, (i,))
        src, cur, nxt = cur, nxt, cur
        run(emit_ffn_ffn, {"xin": src, "xout": cur}, (i,))
        src, cur, nxt = cur, nxt, cur
    p.scope_open()
    p.streams = {"xin": src, "rin": sX3}
    emit_final(p)
    p.streams = {}
    for name in list(p.cnt):
        if name.startswith("ST") or name == "DBG":
            p.wait(p.nc.sync, name)
    for cm in reversed(p._scope or []):
        cm.__exit__(None, None, None)
    p._scope = None
    for cm in reversed(p._stack):
        cm.__exit__(None, None, None)
    p._stack = []
    return p


WEIGHTS_META = None


def _launch(build, in_maps, n_cores):
    p = build()
    res = bass_utils.run_bass_kernel_spmd(p.nc, in_maps, core_ids=list(range(n_cores)))
    return p, res.results


def kernel(**inputs):
    W = host_weights(inputs)
    S = host_smalls(inputs)
    u = np.asarray(inputs["L0_attn_u"], dtype=np.float32)
    xs = [np.ascontiguousarray(u[c].T) for c in range(B)]
    Wflat = {k: np.ascontiguousarray(np.asarray(v, np.float32).reshape(128, -1))
             for k, v in W.items()}
    meta = [(k, v.shape[1]) for k, v in Wflat.items()]
    p = build_fused(meta)
    in_maps = []
    for c in range(B):
        m = {}
        for name in p.inputs:
            if name == "xin_full":
                m[name] = xs[c]
            elif name in Wflat:
                m[name] = Wflat[name]
            elif name in S:
                m[name] = S[name]
            else:
                raise KeyError(name)
        in_maps.append(m)
    res = bass_utils.run_bass_kernel_spmd(p.nc, in_maps, core_ids=list(range(B)))
    out = np.stack([np.asarray(r["out"]) for r in res.results], axis=0)
    return out.astype(np.float32)


# version 08 (final)
# speedup vs baseline: 1.0460x; 1.0180x over previous
import numpy as np
import ml_dtypes
import concourse.bass as bass
import concourse.mybir as mybir
import concourse.bass_utils as bass_utils

F32 = mybir.dt.float32
BF16 = mybir.dt.bfloat16
AF = mybir.ActivationFunctionType
ALU = mybir.AluOpType
AX = mybir.AxisListType

RACE_DETECT = False
DRAINS_VE = False
DRAINS_SC = True
DRAINS_GP = True
B, T, D = 8, 8192, 1024


def dr(engine):
    et = engine.engine if hasattr(engine, "engine") else None
    name = str(et)
    if "DVE" in name:
        if DRAINS_VE:
            engine.drain()
    elif "Activation" in name:
        if DRAINS_SC:
            engine.drain()
    else:
        if DRAINS_GP:
            engine.drain()

TB = 256
NT = T // TB
EPS = 1e-5
bfloat16 = ml_dtypes.bfloat16


class Prog:
    def __init__(self):
        self.nc = bass.Bass("TRN2", target_bir_lowering=False,
                            detect_race_conditions=RACE_DETECT)
        self._stack = []
        self.sems = {}
        self.cnt = {}
        self.inputs = {}
        self.outputs = {}
        self.streams = {}
        self.wbf = {}
        self._names = set()
        self._scope = None
        for name in ("LD", "ST", "PE", "VE", "SC", "GP"):
            self._mksem(name)

    def uname(self, name):
        if name not in self._names:
            self._names.add(name)
            return name
        k = 2
        while f"{name}_{k}" in self._names:
            k += 1
        self._names.add(f"{name}_{k}")
        return f"{name}_{k}"

    def stream(self, name, shape, dtype=F32, out=False):
        if name in self.streams:
            return self.streams[name]
        return self.dram_out(name, shape, dtype) if out else self.dram_in(name, shape, dtype)

    def _mksem(self, name):
        cm = self.nc.semaphore(name)
        self.sems[name] = cm.__enter__()
        self._stack.append(cm)
        self.cnt[name] = 0

    def sem(self, name):
        if name not in self.sems:
            self._mksem(name)
        return self.sems[name]

    def inc(self, inst, name, dma=False):
        n = 16 if dma else 1
        self.sem(name)
        self.cnt[name] += n
        inst.then_inc(self.sems[name], n)
        return self.cnt[name]

    def wait(self, engine, name, value=None):
        self.sem(name)
        v = self.cnt[name] if value is None else value
        if v > 0:
            engine.wait_ge(self.sems[name], v)

    def snap(self, *names):
        names = names or tuple(self.cnt)
        return {n: self.cnt[n] for n in names}

    def wait_counts(self, engine, counts):
        for n, v in counts.items():
            if v > 0:
                engine.wait_ge(self.sems[n], v)

    def dram_in(self, name, shape, dtype=F32):
        if name in self.inputs:
            return self.inputs[name]
        ap = self.nc.dram_tensor(name, list(shape), dtype, kind="ExternalInput").ap()
        self.inputs[name] = ap
        return ap

    def dram_out(self, name, shape, dtype=F32):
        ap = self.nc.dram_tensor(name, list(shape), dtype, kind="ExternalOutput").ap()
        self.outputs[name] = ap
        return ap

    def dram_tmp(self, name, shape, dtype=F32):
        return self.nc.dram_tensor(name, list(shape), dtype, kind="Internal").ap()

    def sbuf(self, name, shape, dtype=F32):
        cm = self.nc.sbuf_tensor(self.uname(name), list(shape), dtype)
        t = cm.__enter__()
        (self._scope if self._scope is not None else self._stack).append(cm)
        return t

    def psum(self, name, shape, dtype=F32):
        cm = self.nc.psum_tensor(self.uname(name), list(shape), dtype)
        t = cm.__enter__()
        (self._scope if self._scope is not None else self._stack).append(cm)
        return t

    def scope_open(self):
        self._scope = []

    def scope_close(self):
        for name in list(self.cnt):
            if name.startswith("ST") or name == "DBG":
                self.wait(self.nc.sync, name)
        self.nc.all_engine_barrier()
        for cm in reversed(self._scope or []):
            cm.__exit__(None, None, None)
        self._scope = None

    def finish(self):
        if self._scope is not None:
            return
        for name in list(self.cnt):
            if name.startswith("ST") or name == "DBG":
                self.wait(self.nc.sync, name)
        for cm in reversed(self._stack):
            cm.__exit__(None, None, None)
        self._stack = []


def ch_major(dram_ap):
    return dram_ap.rearrange("(c p) t -> p c t", p=128)


class LN:

    def __init__(self, p: Prog, tag, sq=None):
        nc = p.nc
        self.p = p
        self.ones_col = p.sbuf(f"{tag}_onc", (128, 1), F32)
        self.ones_col_bf = p.sbuf(f"{tag}_oncb", (128, 1), BF16)
        self.ones_row = p.sbuf(f"{tag}_onr", (1, 128), F32)
        nc.vector.memset(self.ones_col[:, :], 1.0)
        nc.vector.memset(self.ones_col_bf[:, :], 1.0)
        p.inc(nc.vector.memset(self.ones_row[:, :], 1.0), "VE")
        self.sq = p.sbuf(f"{tag}_sq", (128, TB), F32) if sq is None else sq
        self.eps_t = p.sbuf(f"{tag}_eps", (1, 1), F32)
        nc.vector.memset(self.eps_t[:1, :1], EPS)
        self.rows = p.sbuf(f"{tag}_rows", (1, 4, TB), F32)
        self.ps_stat = p.psum(f"{tag}_pst", (1, 512), F32)
        self.ps_bc = p.psum(f"{tag}_psb", (128, 512), F32)
        self.bc_sb = p.sbuf(f"{tag}_bcs", (128, 2, TB), F32)
        self.hist = p.snap()

    def stats(self, x_tiles, ready):
        p, nc = self.p, self.p.nc
        nct = len(x_tiles)
        n_ch = nct * 128
        sx = self.ps_stat[:1, 0:TB]
        sq_ps = self.ps_stat[:1, 256:256 + TB]
        p.wait_counts(nc.tensor, {"VE": self.hist["VE"], "SC": self.hist["SC"]})
        p.wait_counts(nc.tensor, ready)
        onc = self.ones_col if x_tiles[0].dtype == F32 else self.ones_col_bf
        for c in range(nct):
            inst = nc.tensor.matmul(sx, onc[:, :], x_tiles[c],
                                    start=(c == 0), stop=(c == nct - 1))
        p.inc(inst, "PE")
        p.wait_counts(nc.scalar, ready)
        for c in range(nct):
            p.wait_counts(nc.scalar, {"PE": self.hist["PE"] if c == 0 else self.pe_sq_last})
            sc = p.inc(nc.scalar.activation(self.sq[:, :], x_tiles[c], AF.Square), "SC")
            p.wait(nc.tensor, "SC", sc)
            inst = nc.tensor.matmul(sq_ps, self.ones_col[:, :], self.sq[:, :],
                                    start=(c == 0), stop=(c == nct - 1))
            self.pe_sq_last = p.inc(inst, "PE")
        pe_stats = self.pe_sq_last
        mu = self.rows[:1, 0, :]
        ex2 = self.rows[:1, 1, :]
        var = self.rows[:1, 2, :]
        o = self.rows[:1, 3, :]
        p.wait(nc.vector, "PE", pe_stats)
        nc.vector.tensor_scalar(mu, sx, 1.0 / n_ch, None, ALU.mult)
        nc.vector.tensor_scalar(ex2, sq_ps, 1.0 / n_ch, None, ALU.mult)
        nc.vector.drain()
        nc.vector.scalar_tensor_tensor(var, mu, -1.0, mu, ALU.mult, ALU.mult)
        nc.vector.drain()
        ve_var = p.inc(nc.vector.tensor_add(var, var, ex2), "VE")
        inv = ex2
        p.wait(nc.scalar, "VE", ve_var)
        nc.scalar.activation(inv, var, AF.Ln, bias=self.eps_t[:1, :1])
        dr(nc.scalar)
        sc_inv = p.inc(nc.scalar.activation(inv, inv, AF.Exp, scale=-0.5), "SC")
        p.wait(nc.vector, "SC", sc_inv)
        nc.vector.drain()
        ve_o = p.inc(nc.vector.scalar_tensor_tensor(o, mu, -1.0, inv, ALU.mult, ALU.mult), "VE")
        p.wait(nc.tensor, "VE", ve_o)
        p.wait_counts(nc.tensor, {"SC": self.hist.get("SC", 0)})
        nc.tensor.matmul(self.ps_bc[:, 0:TB], self.ones_row[:1, :], inv, start=True, stop=True)
        inst = nc.tensor.matmul(self.ps_bc[:, 256:256 + TB], self.ones_row[:1, :], o,
                                start=True, stop=True)
        pe_bc = p.inc(inst, "PE")
        p.wait(nc.scalar, "PE", pe_bc)
        p.wait_counts(nc.scalar, {"VE": self.hist.get("VE", 0)})
        nc.scalar.copy(self.bc_sb[:, 0, :], self.ps_bc[:, 0:TB])
        nc.scalar.drain()
        sc_bc = p.inc(nc.scalar.copy(self.bc_sb[:, 1, :], self.ps_bc[:, 256:256 + TB]), "SC")
        return self.bc_sb[:, 0, :], self.bc_sb[:, 1, :], sc_bc

    def mark_used(self):
        self.hist = self.p.snap()


def ln_apply(p, x_c, out_c, inv_bc, o_bc, scratch):
    nc = p.nc
    nc.vector.tensor_mul(scratch, x_c, inv_bc)
    dr(nc.vector)
    inst = nc.vector.tensor_add(out_c, scratch, o_bc)
    dr(nc.vector)
    return p.inc(inst, "VE")


def _f32(x):
    return np.ascontiguousarray(np.asarray(x, dtype=np.float32))


def _ch_tiles(vec, nct):
    return _f32(vec).reshape(nct, 128).T.copy()


def _lhsT_tiles(arr):
    ic, oc = arr.shape
    assert ic % 128 == 0
    return np.ascontiguousarray(arr.reshape(ic // 128, 128, oc).transpose(1, 0, 2))


def host_weights(inputs):
    w = {}
    w["L0_W_in"] = _lhsT_tiles(_f32(inputs["L0_attn_W_in"]))
    w["L0_W_out"] = _lhsT_tiles(_f32(inputs["L0_attn_W_out"]))
    w["L0f_W1"] = _lhsT_tiles(_f32(inputs["L0_ffn_W1"]))
    w["L0f_W2"] = _lhsT_tiles(_f32(inputs["L0_ffn_W2"]))
    for l in range(4):
        win = _f32(inputs["L1_attn_in_proj_w"][l]).T
        w[f"M{l}_Win"] = np.ascontiguousarray(
            win.reshape(8, 128, 32, 128).transpose(1, 2, 0, 3))
        wx = _f32(inputs["L1_attn_x_proj_w"][l]).T
        wxp = np.zeros((2048, 128), np.float32)
        wxp[:, 0] = wx[:, 0]
        wxp[:, 32:48] = wx[:, 1:17]
        wxp[:, 64:80] = wx[:, 17:33]
        w[f"M{l}_Wx"] = _lhsT_tiles(wxp)
        wo = _f32(inputs["L1_attn_out_proj_w"][l]).T
        w[f"M{l}_Wout"] = np.ascontiguousarray(
            wo.reshape(16, 128, 8, 128).transpose(1, 2, 0, 3))
    w["OP1"] = _lhsT_tiles(_f32(inputs["L1_attn_op1_w"]).T)
    w["OP2"] = _lhsT_tiles(_f32(inputs["L1_attn_op2_w"]).T)
    for i in range(6):
        w[f"F{i}_Wqkv"] = _lhsT_tiles(_f32(inputs["L1_ffn_Wqkv"][i]))
        w[f"F{i}_Wo"] = _lhsT_tiles(_f32(inputs["L1_ffn_Wo"][i]))
        w1 = _f32(inputs["L1_ffn_W1"][i])
        w1p = np.zeros((1024, 5632), np.float32)
        w1p[:, 0:2730] = w1[:, 0:2730]
        w1p[:, 2816:2816 + 2730] = w1[:, 2730:5460]
        w[f"F{i}_W1"] = _lhsT_tiles(w1p)
        w2 = _f32(inputs["L1_ffn_W2"][i])
        w2p = np.zeros((2816, 1024), np.float32)
        w2p[:2730] = w2
        w[f"F{i}_W2"] = _lhsT_tiles(w2p)
    return w


def host_smalls(inputs):
    s = {}
    s["L0_b_in_t"] = _ch_tiles(inputs["L0_attn_b_in"], 24)
    s["L0_short_w_t"] = _f32(inputs["L0_attn_short_w"]).reshape(24, 128, 3).transpose(1, 0, 2).copy()
    s["L0_short_b_t"] = _ch_tiles(inputs["L0_attn_short_b"], 24)
    s["L0_inner_w_t"] = _f32(inputs["L0_attn_inner_w"]).reshape(8, 128, 4).transpose(1, 0, 2).copy()
    s["L0_inner_b_t"] = _ch_tiles(inputs["L0_attn_inner_b"], 8)
    s["L0_b_out_t"] = _ch_tiles(inputs["L0_attn_b_out"], 8)
    s["L0f_b1_t"] = _ch_tiles(inputs["L0_ffn_b1"], 32)
    s["L0f_b2_t"] = _ch_tiles(inputs["L0_ffn_b2"], 8)
    s["ident128"] = np.eye(128, dtype=bfloat16)
    s["M_innorm_g"] = _ch_tiles(inputs["L1_attn_in_norm_g"], 8)
    s["M_innorm_b"] = _ch_tiles(inputs["L1_attn_in_norm_b"], 8)
    for l in range(4):
        s[f"M{l}_ln_g"] = _ch_tiles(inputs["L1_attn_ln_g"][l], 8)
        s[f"M{l}_ln_b"] = _ch_tiles(inputs["L1_attn_ln_b"][l], 8)
        cwv = _f32(inputs["L1_attn_conv_w"][l])
        s[f"M{l}_conv_w"] = np.ascontiguousarray(
            cwv.reshape(2, 16, 128, 4).transpose(2, 0, 1, 3))
        cbv = _f32(inputs["L1_attn_conv_b"][l])
        s[f"M{l}_conv_b"] = np.ascontiguousarray(
            cbv.reshape(2, 16, 128).transpose(2, 0, 1))
        s[f"M{l}_dt_w"] = _ch_tiles(inputs["L1_attn_dt_proj_w"][l][:, 0], 16)
        s[f"M{l}_dt_b"] = _ch_tiles(inputs["L1_attn_dt_proj_b"][l], 16)
        s[f"M{l}_D"] = _ch_tiles(inputs["L1_attn_D"][l], 16)
        s[f"M{l}_mln_g"] = _ch_tiles(inputs["L1_attn_mamba_ln_g"][l], 16)
        s[f"M{l}_mln_b"] = _ch_tiles(inputs["L1_attn_mamba_ln_b"][l], 16)
    sel = np.zeros((128, 33, 128), dtype=bfloat16)
    sel[0, 0, :] = 1.0
    for n in range(16):
        sel[32 + n, 1 + n, :] = 1.0
        sel[64 + n, 17 + n, :] = 1.0
    s["sel_rows"] = sel
    s["zeros96"] = np.zeros((128, 96), bfloat16)
    s["OP1_b"] = _ch_tiles(inputs["L1_attn_op1_b"], 8)
    s["OP2_b"] = _ch_tiles(inputs["L1_attn_op2_b"], 8)
    half = np.arange(0, 64, 2, dtype=np.float64)
    inv_freq = 1.0 / 10000.0 ** (half / 64.0)
    tt = np.arange(T, dtype=np.float64)
    fr = tt[:, None] * inv_freq[None, :]
    freqs = np.concatenate([fr, fr], axis=1)
    base_scale = (half + 0.4 * 64) / (1.4 * 64)
    power = (tt - T // 2) / (256 // 2)
    scale = base_scale[None, :] ** power[:, None]
    scale = np.concatenate([scale, scale], axis=1)
    cos_t = np.cos(freqs); sin_t = np.sin(freqs)
    qc = (cos_t * scale).T.astype(np.float32)
    qs_ = (sin_t * scale).T.astype(np.float32)
    kc = (cos_t / scale).T.astype(np.float32)
    ks_ = (sin_t / scale).T.astype(np.float32)
    s["rot_qc"] = np.ascontiguousarray(np.tile(qc, (2, 1)))
    s["rot_qs"] = np.ascontiguousarray(np.tile(qs_, (2, 1)))
    s["rot_kc"] = np.ascontiguousarray(np.tile(kc, (2, 1)))
    s["rot_ks"] = np.ascontiguousarray(np.tile(ks_, (2, 1)))
    R = np.zeros((64, 64), np.float32)
    for d in range(32):
        R[d, d + 32] = -1.0
        R[d + 32, d] = 1.0
    Rt = R.T
    Rbd = np.zeros((128, 128), np.float32)
    Rbd[:64, :64] = Rt
    Rbd[64:, 64:] = Rt
    s["RbdT"] = Rbd.astype(bfloat16)
    bd2 = np.zeros((128, 2), np.float32)
    bd2[:64, 0] = 1.0
    bd2[64:, 1] = 1.0
    s["bd2"] = bd2
    ones_top = np.zeros((1, 128), np.float32); ones_top[0, :64] = 1.0
    ones_bot = np.zeros((1, 128), np.float32); ones_bot[0, 64:] = 1.0
    s["ones_top"] = ones_top
    s["ones_bot"] = ones_bot
    bdT = np.zeros((2, 128), np.float32)
    bdT[0, :64] = 1.0
    bdT[1, 64:] = 1.0
    s["bdT"] = bdT
    pp = np.arange(128, dtype=np.float32)
    s["mask_st"] = np.stack([pp, 128 + pp], axis=1)
    s["mask_en"] = np.stack([pp + 257, 128 + pp + 257], axis=1)
    s["mask_en0"] = np.stack([pp + 1, 128 + pp + 1], axis=1)
    NEG = np.float32(-3.0e38)
    ii = np.arange(128)[:, None]
    jj = np.arange(512)[None, :]
    mk = np.zeros((2, 128, 512), np.float32)
    for m in range(2):
        ok = (jj >= m * 128 + ii) & (jj <= m * 128 + ii + 256)
        mk[m] = np.where(ok, 0.0, NEG)
    s["mask_add_t"] = np.ascontiguousarray(mk.transpose(1, 0, 2))
    jj0 = np.arange(256)[None, :]
    mk0 = np.zeros((2, 128, 256), np.float32)
    for m in range(2):
        ok = jj0 <= m * 128 + ii
        mk0[m] = np.where(ok, 0.0, NEG)
    s["mask_add0_t"] = np.ascontiguousarray(mk0.transpose(1, 0, 2))
    s["ident128f"] = np.eye(128, dtype=np.float32)
    for i in range(6):
        qsv = _f32(inputs["L1_ffn_q_scale"][i])
        ksv = _f32(inputs["L1_ffn_k_scale"][i])
        s[f"F{i}_qs"] = np.tile(qsv, 2).reshape(128, 1).copy()
        s[f"F{i}_ks"] = np.tile(ksv, 2).reshape(128, 1).copy()
        s[f"F{i}_qsp"] = np.tile(np.roll(qsv, -32), 2).reshape(128, 1).copy()
        s[f"F{i}_ksp"] = np.tile(np.roll(ksv, -32), 2).reshape(128, 1).copy()
        s[f"F{i}_ln_g"] = _ch_tiles(inputs["L1_ffn_ln_g"][i], 8)
        s[f"F{i}_ln_b"] = _ch_tiles(inputs["L1_ffn_ln_b"][i], 8)
        b1 = _f32(inputs["L1_ffn_b1"][i])
        b1a = np.zeros(2816, np.float32); b1a[:2730] = b1[:2730]
        b1g = np.zeros(2816, np.float32); b1g[:2730] = b1[2730:]
        s[f"F{i}_b1a"] = _ch_tiles(b1a, 22)
        s[f"F{i}_b1g"] = _ch_tiles(b1g, 22)
        s[f"F{i}_b2"] = _ch_tiles(inputs["L1_ffn_b2"][i], 8)
    return s


def build_wcast(weights):
    p = Prog()
    nc = p.nc
    CH = 16384
    stage = p.sbuf("wc_stage", (128, 2, CH), F32)
    out = p.sbuf("wc_out", (128, 2, CH), BF16)
    items = []
    for name, arr in weights.items():
        cols = arr.size // 128
        src = p.dram_in(name, (128, cols))
        dst = p.dram_out(name + "_bf", (128, cols), BF16)
        items.append((src, dst, cols))
    hist = {0: None, 1: None}
    k = 0
    for src, dst, cols in items:
        for c0 in range(0, cols, CH):
            w = min(CH, cols - c0)
            s = k % 2
            if hist[s] is not None:
                nc.sync.wait_ge(p.sem("SC"), hist[s][0])
                nc.sync.wait_ge(p.sem(f"ST{s}"), hist[s][1])
            ld = p.inc(nc.sync.dma_start(stage[:, s, :w], src[:, c0:c0 + w]), f"WL{s}", dma=True)
            p.wait(nc.scalar, f"WL{s}", ld)
            sc = p.inc(nc.scalar.copy(out[:, s, :w], stage[:, s, :w]), "SC")
            p.wait(nc.sync, "SC", sc)
            st = p.inc(nc.sync.dma_start(dst[:, c0:c0 + w], out[:, s, :w]), f"ST{s}", dma=True)
            hist[s] = (sc, st)
            k += 1
    p.finish()
    return p


def load_w_bf(p, name, n_ic, n_oc):
    nc = p.nc
    if name in p.wbf:
        d = p.wbf[name].rearrange("p (i o) -> p i o", i=n_ic)
    else:
        d = p.dram_in(name + "_bf", (128, n_ic * n_oc), BF16).rearrange(
            "p (i o) -> p i o", i=n_ic)
    s = p.sbuf(name + "_sb", (128, n_ic, n_oc), BF16)
    p.inc(nc.sync.dma_start(s[:, :, :], d[:, :, :]), "LD", dma=True)
    return s


def load_small(p, name, shape, dtype=F32):
    nc = p.nc
    d = p.dram_in(name, shape, dtype)
    s = p.sbuf(name + "_sb", shape, dtype)
    p.inc(nc.sync.dma_start(s[...], d[...]), "LD", dma=True)
    return s


def emit_l0_attn(p: Prog):
    nc = p.nc
    xin = ch_major(p.stream("xin", (D, T)))
    xout = ch_major(p.stream("xout", (D, T), out=True))
    w_in = load_w_bf(p, "L0_W_in", 8, 3072)
    w_out = load_w_bf(p, "L0_W_out", 8, 1024)
    b_in = load_small(p, "L0_b_in_t", (128, 24))
    sw = load_small(p, "L0_short_w_t", (128, 24, 3))
    sbv = load_small(p, "L0_short_b_t", (128, 24))
    iw = load_small(p, "L0_inner_w_t", (128, 8, 4))
    ib = load_small(p, "L0_inner_b_t", (128, 8))
    bo = load_small(p, "L0_b_out_t", (128, 8))
    ident = load_small(p, "ident128", (128, 128), BF16)
    ldw = p.cnt["LD"]

    dg_s = p.sbuf("dg_s", (128, 24, 3, 128), BF16)
    dg_i = p.sbuf("dg_i", (128, 8, 4, 128), BF16)
    p.wait(nc.vector, "LD", ldw)
    for c in range(24):
        for k in range(3):
            inst = nc.vector.tensor_scalar(dg_s[:, c, k, :], ident[:, :], sw[:, c, k:k + 1],
                                           None, ALU.mult)
    for c in range(8):
        for k in range(4):
            inst = nc.vector.tensor_scalar(dg_i[:, c, k, :], ident[:, :], iw[:, c, k:k + 1],
                                           None, ALU.mult)
    dgv = p.inc(inst, "VE")
    p.wait(nc.tensor, "VE", dgv)

    x_sb = p.sbuf("x_sb", (128, 2, 8, TB), F32)
    xn_sb = p.sbuf("xn_sb", (128, 8, TB), BF16)
    cin = p.sbuf("cin", (128, 24, 2 + TB), BF16)
    qk_sb = p.sbuf("qk_sb", (128, 16, TB), F32)
    vg = p.sbuf("vg", (128, 8, 3 + TB), BF16)
    y_sb = p.sbuf("y_sb", (128, 8, TB), BF16)
    o_sb = p.sbuf("o_sb", (128, 8, TB), F32)
    scr = p.sbuf("scr", (128, TB), F32)
    ln = LN(p, "l0a_ln")
    ps_mm = [p.psum(f"l0a_mm{i}", (128, 512), F32) for i in range(2)]

    nc.vector.memset(cin[:, :, 0:2], 0.0)
    p.inc(nc.vector.memset(vg[:, :, 0:3], 0.0), "VE")

    x_hist = [None, None]
    ps_guard = [{}, {}]
    blk_hist = None

    for tau in range(NT):
        s = tau % 2
        t0 = tau * TB
        if x_hist[s] is not None:
            p.wait_counts(nc.sync, x_hist[s])
        ld = p.inc(nc.sync.dma_start(x_sb[:, s, :, :], xin[:, :, t0:t0 + TB]), f"LX{s}", dma=True)
        xt = [x_sb[:, s, c, :] for c in range(8)]
        inv_bc, o_bc, pe_bc = ln.stats(xt, {f"LX{s}": ld})
        p.wait(nc.vector, "SC", pe_bc)
        if blk_hist is not None:
            p.wait_counts(nc.vector, {"PE": blk_hist["PE"]})
        for c in range(8):
            ln_apply(p, xt[c], xn_sb[:, c, :], inv_bc, o_bc, scr[:, :])
        xnv = p.cnt["VE"]
        ln.mark_used()
        p.wait(nc.tensor, "VE", xnv)
        for oc in range(24):
            bank = ps_mm[oc % 2]
            p.wait_counts(nc.tensor, ps_guard[oc % 2])
            for ic in range(8):
                inst = nc.tensor.matmul(bank[:, :TB], w_in[:, ic, oc * 128:(oc + 1) * 128],
                                        xn_sb[:, ic, :], start=(ic == 0), stop=(ic == 7))
            mm = p.inc(inst, "PE")
            p.wait(nc.scalar, "PE", mm)
            ps_guard[oc % 2] = {"SC": p.inc(
                nc.scalar.activation(cin[:, oc, 2:2 + TB], bank[:, :TB], AF.Identity,
                                     bias=b_in[:, oc:oc + 1]), "SC")}
        cin_sc = p.cnt["SC"]
        p.wait(nc.tensor, "SC", cin_sc)
        sc_k_done = 0
        for oc in list(range(8, 16)) + list(range(0, 8)) + list(range(16, 24)):
            bank = ps_mm[oc % 2]
            p.wait_counts(nc.tensor, ps_guard[oc % 2])
            for k in range(3):
                inst = nc.tensor.matmul(bank[:, :TB], dg_s[:, oc, k, :], cin[:, oc, k:k + TB],
                                        start=(k == 0), stop=(k == 2))
            mm = p.inc(inst, "PE")
            if oc < 16:
                p.wait(nc.scalar, "PE", mm)
                sc_k_done = p.inc(
                    nc.scalar.activation(qk_sb[:, oc, :], bank[:, :TB], AF.Identity,
                                         bias=sbv[:, oc:oc + 1]), "SC")
                ps_guard[oc % 2] = {"SC": sc_k_done}
            else:
                c = oc - 16
                p.wait(nc.vector, "PE", mm)
                p.wait(nc.vector, "SC", sc_k_done)
                ve = p.inc(nc.vector.scalar_tensor_tensor(
                    vg[:, c, 3:3 + TB], bank[:, :TB], sbv[:, oc:oc + 1], qk_sb[:, 8 + c, :],
                    ALU.add, ALU.mult), "VE")
                ps_guard[oc % 2] = {"VE": ve}
        ve_v = p.cnt["VE"]
        p.wait(nc.tensor, "VE", ve_v)
        for c in range(8):
            bank = ps_mm[c % 2]
            p.wait_counts(nc.tensor, ps_guard[c % 2])
            for k in range(4):
                inst = nc.tensor.matmul(bank[:, :TB], dg_i[:, c, k, :], vg[:, c, k:k + TB],
                                        start=(k == 0), stop=(k == 3))
            mm = p.inc(inst, "PE")
            p.wait(nc.vector, "PE", mm)
            ps_guard[c % 2] = {"VE": p.inc(nc.vector.scalar_tensor_tensor(
                y_sb[:, c, :], bank[:, :TB], ib[:, c:c + 1], qk_sb[:, c, :],
                ALU.add, ALU.mult), "VE")}
        yv = p.cnt["VE"]
        p.wait(nc.tensor, "VE", yv)
        if blk_hist is not None:
            p.wait(nc.scalar, "ST0", blk_hist.get("ST0", 0))
            p.wait(nc.scalar, "ST1", blk_hist.get("ST1", 0))
        sc_o = 0
        for oc in range(8):
            bank = ps_mm[oc % 2]
            p.wait_counts(nc.tensor, ps_guard[oc % 2])
            for ic in range(8):
                inst = nc.tensor.matmul(bank[:, :TB], w_out[:, ic, oc * 128:(oc + 1) * 128],
                                        y_sb[:, ic, :], start=(ic == 0), stop=(ic == 7))
            mm = p.inc(inst, "PE")
            p.wait(nc.scalar, "PE", mm)
            sc_o = p.inc(nc.scalar.activation(o_sb[:, oc, :], bank[:, :TB], AF.Identity,
                                              bias=bo[:, oc:oc + 1]), "SC")
            ps_guard[oc % 2] = {"SC": sc_o}
        ot = [o_sb[:, c, :] for c in range(8)]
        inv2, o2, pe_bc2 = ln.stats(ot, {"SC": sc_o})
        p.wait(nc.vector, "SC", pe_bc2)
        for c in range(8):
            nc.vector.tensor_mul(scr[:, :], ot[c], inv2)
            dr(nc.vector)
            nc.vector.tensor_add(scr[:, :], scr[:, :], o2)
            dr(nc.vector)
            inst = nc.vector.tensor_add(o_sb[:, c, :], scr[:, :], xt[c])
            dr(nc.vector)
        res_ve = p.inc(inst, "VE")
        ln.mark_used()
        p.wait(nc.sync, "VE", res_ve)
        p.inc(nc.sync.dma_start(xout[:, :, t0:t0 + TB], o_sb[:, :, :]), f"ST{s}", dma=True)
        x_hist[s] = {"VE": res_ve, "PE": p.cnt["PE"], "SC": p.cnt["SC"]}
        blk_hist = p.snap()
        if tau < NT - 1:
            p.wait(nc.scalar, "PE", p.cnt["PE"])
            dr(nc.scalar)
            nc.scalar.copy(cin[:, :, 0:2], cin[:, :, TB:TB + 2])
            p.inc(nc.scalar.copy(vg[:, :, 0:3], vg[:, :, TB:TB + 3]), "SC")
    p.finish()
    return p


def emit_l0_ffn(p: Prog):
    nc = p.nc
    xin = ch_major(p.stream("xin", (D, T)))
    xout = ch_major(p.stream("xout", (D, T), out=True))
    w1 = load_w_bf(p, "L0f_W1", 8, 4096)
    w2 = load_w_bf(p, "L0f_W2", 32, 1024)
    b1 = load_small(p, "L0f_b1_t", (128, 32))
    b2 = load_small(p, "L0f_b2_t", (128, 8))
    p.wait(nc.tensor, "LD")
    p.wait(nc.scalar, "LD")
    p.wait(nc.vector, "LD")

    x_sb = p.sbuf("x_sb", (128, 2, 8, TB), F32)
    xn_sb = p.sbuf("xn_sb", (128, 8, TB), BF16)
    h_sb = p.sbuf("h_sb", (128, 32, TB), BF16)
    o_sb = p.sbuf("o_sb", (128, 8, TB), F32)
    scr = p.sbuf("scr", (128, TB), F32)
    ln = LN(p, "l0f_ln")
    ps_mm = [p.psum(f"l0f_mm{i}", (128, 512), F32) for i in range(2)]

    x_hist = [None, None]
    ps_hist = [0, 0]
    blk_hist = None
    for tau in range(NT):
        s = tau % 2
        t0 = tau * TB
        dr(nc.vector)
        dr(nc.scalar)
        dr(nc.gpsimd)
        if x_hist[s] is not None:
            p.wait_counts(nc.sync, x_hist[s])
        ld = p.inc(nc.sync.dma_start(x_sb[:, s, :, :], xin[:, :, t0:t0 + TB]), f"LX{s}", dma=True)
        xt = [x_sb[:, s, c, :] for c in range(8)]
        inv_bc, o_bc, pe_bc = ln.stats(xt, {f"LX{s}": ld})
        p.wait(nc.vector, "SC", pe_bc)
        if blk_hist is not None:
            p.wait_counts(nc.vector, {"PE": blk_hist["PE"]})
        for c in range(8):
            ln_apply(p, xt[c], xn_sb[:, c, :], inv_bc, o_bc, scr[:, :])
        xnv = p.cnt["VE"]
        ln.mark_used()
        p.wait(nc.tensor, "VE", xnv)
        for oc in range(32):
            bank = ps_mm[oc % 2]
            p.wait(nc.tensor, "SC", ps_hist[oc % 2])
            for ic in range(8):
                inst = nc.tensor.matmul(bank[:, :TB], w1[:, ic, oc * 128:(oc + 1) * 128],
                                        xn_sb[:, ic, :], start=(ic == 0), stop=(ic == 7))
            mm = p.inc(inst, "PE")
            p.wait(nc.scalar, "PE", mm)
            ps_hist[oc % 2] = p.inc(
                nc.scalar.activation(h_sb[:, oc, :], bank[:, :TB], AF.Relu,
                                     bias=b1[:, oc:oc + 1]), "SC")
        hsc = p.cnt["SC"]
        p.wait(nc.tensor, "SC", hsc)
        if blk_hist is not None:
            p.wait(nc.scalar, "ST", blk_hist["ST"])
        sc_o = 0
        for oc in range(8):
            bank = ps_mm[oc % 2]
            p.wait(nc.tensor, "SC", ps_hist[oc % 2])
            for ic in range(32):
                inst = nc.tensor.matmul(bank[:, :TB], w2[:, ic, oc * 128:(oc + 1) * 128],
                                        h_sb[:, ic, :], start=(ic == 0), stop=(ic == 31))
            mm = p.inc(inst, "PE")
            p.wait(nc.scalar, "PE", mm)
            sc_o = p.inc(nc.scalar.activation(o_sb[:, oc, :], bank[:, :TB], AF.Identity,
                                              bias=b2[:, oc:oc + 1]), "SC")
            ps_hist[oc % 2] = sc_o
        ot = [o_sb[:, c, :] for c in range(8)]
        inv2, o2, pe_bc2 = ln.stats(ot, {"SC": sc_o})
        p.wait(nc.vector, "SC", pe_bc2)
        for c in range(8):
            nc.vector.tensor_mul(scr[:, :], ot[c], inv2)
            dr(nc.vector)
            nc.vector.tensor_add(scr[:, :], scr[:, :], o2)
            dr(nc.vector)
            inst = nc.vector.tensor_add(o_sb[:, c, :], scr[:, :], xt[c])
            dr(nc.vector)
        res_ve = p.inc(inst, "VE")
        ln.mark_used()
        p.wait(nc.sync, "VE", res_ve)
        p.inc(nc.sync.dma_start(xout[:, :, t0:t0 + TB], o_sb[:, :, :]), "ST", dma=True)
        x_hist[s] = {"VE": res_ve, "PE": p.cnt["PE"], "SC": p.cnt["SC"]}
        blk_hist = p.snap()
    p.finish()
    return p




def emit_l1_pre(p: Prog):
    nc = p.nc
    xin = ch_major(p.stream("xin", (D, T)))
    xout = ch_major(p.stream("xout", (D, T), out=True))
    g = load_small(p, "M_innorm_g", (128, 8))
    bb = load_small(p, "M_innorm_b", (128, 8))
    p.wait(nc.tensor, "LD")
    p.wait(nc.scalar, "LD")
    p.wait(nc.vector, "LD")
    x_sb = p.sbuf("x_sb", (128, 2, 8, TB), F32)
    o_sb = p.sbuf("o_sb", (128, 2, 8, TB), F32)
    scr = p.sbuf("scr", (128, TB), F32)
    ln = LN(p, "pre_ln")
    x_hist = [None, None]
    for tau in range(NT):
        sl = tau % 2
        t0 = tau * TB
        dr(nc.vector)
        dr(nc.scalar)
        dr(nc.gpsimd)
        if x_hist[sl] is not None:
            p.wait_counts(nc.sync, x_hist[sl])
        ld = p.inc(nc.sync.dma_start(x_sb[:, sl, :, :], xin[:, :, t0:t0 + TB]), f"LX{sl}", dma=True)
        xt = [x_sb[:, sl, c, :] for c in range(8)]
        inv_bc, o_bc, pe_bc = ln.stats(xt, {f"LX{sl}": ld})
        p.wait(nc.vector, "SC", pe_bc)
        for c in range(8):
            nc.vector.tensor_mul(scr[:, :], xt[c], inv_bc)
            dr(nc.vector)
            nc.vector.tensor_add(scr[:, :], scr[:, :], o_bc)
            dr(nc.vector)
            inst = nc.vector.tensor_scalar(o_sb[:, sl, c, :], scr[:, :], g[:, c:c + 1],
                                           bb[:, c:c + 1], ALU.mult, ALU.add)
            dr(nc.vector)
        ve = p.inc(inst, "VE")
        ln.mark_used()
        p.wait(nc.sync, "VE", ve)
        st = p.inc(nc.sync.dma_start(xout[:, :, t0:t0 + TB], o_sb[:, sl, :, :]), f"ST{sl}", dma=True)
        x_hist[sl] = {"VE": ve, "PE": p.cnt["PE"], "SC": p.cnt["SC"], f"ST{sl}": st}
    p.finish()
    return p


TBM = 512
NTM = T // TBM
MAMBA_DEBUG = False


def emit_mamba(p: Prog, l):
    nc = p.nc
    DI = 16
    NST = 16
    xin = ch_major(p.stream("xin", (D, T)))
    xout = ch_major(p.stream("xout", (D, T), out=True))
    if f"M{l}_Win" in p.wbf:
        win_d = p.wbf[f"M{l}_Win"].rearrange("p (o i c) -> p o i c", o=32, i=8)
        wout_d = p.wbf[f"M{l}_Wout"].rearrange("p (o i c) -> p o i c", o=8, i=16)
    else:
        win_d = p.dram_in(f"M{l}_Win_bf", (128, 32 * 8 * 128), BF16).rearrange(
            "p (o i c) -> p o i c", o=32, i=8)
        wout_d = p.dram_in(f"M{l}_Wout_bf", (128, 8 * 16 * 128), BF16).rearrange(
            "p (o i c) -> p o i c", o=8, i=16)
    z0h_early = [p.sbuf("z0h", (128, 16, 3), BF16), p.sbuf("z1h", (128, 16, 3), BF16)]
    wx = load_w_bf(p, f"M{l}_Wx", 16, 128)
    ln_g = load_small(p, f"M{l}_ln_g", (128, 8))
    ln_b = load_small(p, f"M{l}_ln_b", (128, 8))
    cw = load_small(p, f"M{l}_conv_w", (128, 2, 16, 4))
    cb = load_small(p, f"M{l}_conv_b", (128, 2, 16))
    dtw = load_small(p, f"M{l}_dt_w", (128, 16))
    dtb = load_small(p, f"M{l}_dt_b", (128, 16))
    Dv = load_small(p, f"M{l}_D", (128, 16))
    mg = load_small(p, f"M{l}_mln_g", (128, 16))
    mb = load_small(p, f"M{l}_mln_b", (128, 16))
    ident = load_small(p, "ident128", (128, 128), BF16)
    sel = load_small(p, "sel_rows", (128, 33, 128), BF16)
    zz = p.dram_in("zeros96", (128, 96), BF16)
    p.inc(nc.sync.dma_start(z0h_early[0][:, :, :], zz[:, 0:48].rearrange("p (a b) -> p a b", b=3)),
          "LD", dma=True)
    p.inc(nc.sync.dma_start(z0h_early[1][:, :, :], zz[:, 48:96].rearrange("p (a b) -> p a b", b=3)),
          "LD", dma=True)
    ldw = p.cnt["LD"]

    dg = p.sbuf("dg", (128, 2, 16, 4, 128), BF16)
    p.wait(nc.vector, "LD", ldw)
    for cv in range(2):
        for j in range(16):
            for k in range(4):
                inst = nc.vector.tensor_scalar(dg[:, cv, j, k, :], ident[:, :],
                                               cw[:, cv, j, k:k + 1], None, ALU.mult)
    dgv = p.inc(inst, "VE")
    p.wait(nc.tensor, "VE", dgv)

    x_sb = p.sbuf("x_sb", (128, 1, 8, TBM), F32)
    xn_sb = p.sbuf("xn_sb", (128, 8, TBM), BF16)
    win_sb = p.sbuf("win_sb", (128, 4, 8, 128), BF16)
    wout_sb = p.sbuf("wout_sb", (128, 4, 16, 128), BF16)
    z0 = p.sbuf("z0", (128, 3 + TBM), BF16)
    z1 = p.sbuf("z1", (128, 3 + TBM), BF16)
    z0h, z1h = z0h_early
    z2 = p.sbuf("z2", (128, 16, TBM), BF16)
    sig = p.sbuf("sig", (128, 16, TBM), BF16)
    praw = p.sbuf("praw", (128, TBM), F32) if MAMBA_DEBUG else None
    praw_b = p.sbuf("praw_b", (128, TBM), BF16)
    dt_sb = p.sbuf("dt_sb", (128, TBM), F32)
    u_sb = p.sbuf("u_sb", (128, TBM), F32)
    Bbc = p.sbuf("Bbc", (128, 16, TBM), BF16)
    Cbc = p.sbuf("Cbc", (128, 16, TBM), BF16)
    a_sb = p.sbuf("a_sb", (128, 2, TBM), F32)
    b_sb = p.sbuf("b_sb", (128, 2, TBM), F32)
    h_sb = p.sbuf("h_sb", (128, 2, TBM), F32)
    yh_sb = p.sbuf("yh_sb", (128, 2, TBM), BF16)
    hc = p.sbuf("hc", (128, 16, 16), F32)
    y_sb = p.sbuf("y_sb", (128, 16, TBM), BF16)
    scr = p.sbuf("scr", (128, TBM), F32)
    scr2 = p.sbuf("scr2", (128, TBM), F32)
    ln = LN(p, "m_ln")

    ps_mm = [p.psum(f"m_mm{i}", (128, 512), F32) for i in range(2)]
    ps_dtr = p.psum("m_dtr", (128, 512), F32)
    ps_cbc = p.psum("m_cbc", (128, 512), F32)
    ps_y = p.psum("m_y", (128, 512), F32)

    if MAMBA_DEBUG:
        dbg = {
            "dbg_xn": ch_major(p.dram_out("dbg_xn", (1024, T), BF16)),
            "dbg_z2": ch_major(p.dram_out("dbg_z2", (2048, T), BF16)),
            "dbg_sig": ch_major(p.dram_out("dbg_sig", (2048, T), BF16)),
            "dbg_praw": p.dram_out("dbg_praw", (128, T), F32),
            "dbg_y": ch_major(p.dram_out("dbg_y", (2048, T), BF16)),
            "dbg_dt": p.dram_out("dbg_dt", (128, T), F32),
        }

    p.wait(nc.scalar, "LD")
    p.wait(nc.tensor, "LD")
    x_hist = [None, None]
    ps_guard = [{}, {}]
    dtr_guard = {}
    cbc_guard = {}
    y_guard = {}
    prev_ygv = 0
    prev_mln_sc = 0
    prev_yD_ve = 0
    w_guard = [{} for _ in range(4)]
    wo_guard = [{} for _ in range(4)]
    zh_guard = {}
    for tau in range(NTM):
        sl = tau % 2
        t0 = tau * TBM
        dr(nc.vector)
        dr(nc.scalar)
        dr(nc.gpsimd)
        for xh in x_hist:
            if xh is not None:
                p.wait_counts(nc.sync, xh)
        ld = p.inc(nc.sync.dma_start(x_sb[:, 0, :, :], xin[:, :, t0:t0 + TBM]), f"LX{sl}", dma=True)
        xt = [x_sb[:, 0, c, :] for c in range(8)]
        if tau > 0:
            p.wait_counts(nc.vector, {"PE": x_hist[1 - sl]["PE"]})
        for hh in range(2):
            xth = [x_sb[:, 0, c, hh * 256:(hh + 1) * 256] for c in range(8)]
            inv_bc, o_bc, pe_bc = ln.stats(xth, {f"LX{sl}": ld})
            p.wait(nc.vector, "SC", pe_bc)
            for c in range(8):
                ho = hh * 256
                nc.vector.tensor_mul(scr[:, :256], xth[c], inv_bc)
                dr(nc.vector)
                nc.vector.tensor_add(scr[:, :256], scr[:, :256], o_bc)
                dr(nc.vector)
                inst = nc.vector.tensor_scalar(xn_sb[:, c, ho:ho + 256], scr[:, :256],
                                               ln_g[:, c:c + 1], ln_b[:, c:c + 1],
                                               ALU.mult, ALU.add)
                dr(nc.vector)
            p.inc(inst, "VE")
            ln.mark_used()
        xnv = p.cnt["VE"]
        p.wait(nc.tensor, "VE", xnv)
        for oc in range(32):
            ws = oc % 4
            p.wait_counts(nc.sync, w_guard[ws])
            wld = p.inc(nc.sync.dma_start(win_sb[:, ws, :, :], win_d[:, oc, :, :]), f"LW{ws}", dma=True)
            bank = ps_mm[oc % 2]
            p.wait_counts(nc.tensor, ps_guard[oc % 2])
            p.wait(nc.tensor, f"LW{ws}", wld)
            for ic in range(8):
                inst = nc.tensor.matmul(bank[:, :TBM], win_sb[:, ws, ic, :], xn_sb[:, ic, :],
                                        start=(ic == 0), stop=(ic == 7))
            mm = p.inc(inst, "PE")
            w_guard[ws] = {"PE": mm}
            if oc < 16:
                j = oc
                p.wait(nc.scalar, "PE", mm)
                p.wait_counts(nc.scalar, zh_guard)
                nc.scalar.copy(z0[:, 0:3], z0h[:, j, :])
                dr(nc.scalar)
                sc0 = p.inc(nc.scalar.activation(z0[:, 3:3 + TBM], bank[:, :TBM], AF.Silu), "SC")
                dr(nc.scalar)
                nc.scalar.copy(z0h[:, j, :], z0[:, TBM:TBM + 3])
                ps_guard[oc % 2] = {"SC": sc0}
                bank2 = ps_mm[(oc + 1) % 2]
                p.wait_counts(nc.tensor, ps_guard[(oc + 1) % 2])
                p.wait(nc.tensor, "SC", sc0)
                for k in range(4):
                    inst = nc.tensor.matmul(bank2[:, :TBM], dg[:, 0, j, k, :], z0[:, k:k + TBM],
                                            start=(k == 0), stop=(k == 3))
                mm2 = p.inc(inst, "PE")
                p.wait(nc.scalar, "PE", mm2)
                nc.scalar.copy(z1[:, 0:3], z1h[:, j, :])
                dr(nc.scalar)
                sc1 = p.inc(nc.scalar.activation(z1[:, 3:3 + TBM], bank2[:, :TBM], AF.Silu,
                                                 bias=cb[:, 0, j:j + 1]), "SC")
                dr(nc.scalar)
                nc.scalar.copy(z1h[:, j, :], z1[:, TBM:TBM + 3])
                zh_guard = {"SC": sc1}
                ps_guard[(oc + 1) % 2] = {"SC": sc1}
                bank3 = ps_mm[oc % 2]
                p.wait_counts(nc.tensor, ps_guard[oc % 2])
                p.wait(nc.tensor, "SC", sc1)
                for k in range(4):
                    inst = nc.tensor.matmul(bank3[:, :TBM], dg[:, 1, j, k, :], z1[:, k:k + TBM],
                                            start=(k == 0), stop=(k == 3))
                mm3 = p.inc(inst, "PE")
                p.wait(nc.scalar, "PE", mm3)
                p.wait(nc.scalar, "VE", prev_yD_ve)
                sc2 = p.inc(nc.scalar.activation(z2[:, j, :], bank3[:, :TBM], AF.Silu,
                                                 bias=cb[:, 1, j:j + 1]), "SC")
                ps_guard[oc % 2] = {"SC": sc2}
            else:
                j = oc - 16
                p.wait(nc.scalar, "PE", mm)
                p.wait(nc.scalar, "VE", prev_ygv)
                sc = p.inc(nc.scalar.activation(sig[:, j, :], bank[:, :TBM], AF.Sigmoid), "SC")
                ps_guard[oc % 2] = {"SC": sc}
        z2sc = p.cnt["SC"]
        if MAMBA_DEBUG:
            p.wait(nc.sync, "VE", xnv)
            p.inc(nc.sync.dma_start(dbg["dbg_xn"][:, :, t0:t0 + TBM], xn_sb[:, :, :]), "DBG", dma=True)
            p.wait(nc.sync, "SC", z2sc)
            p.inc(nc.sync.dma_start(dbg["dbg_z2"][:, :, t0:t0 + TBM], z2[:, :, :]), "DBG", dma=True)
            p.inc(nc.sync.dma_start(dbg["dbg_sig"][:, :, t0:t0 + TBM], sig[:, :, :]), "DBG", dma=True)
        bank = ps_mm[0]
        p.wait_counts(nc.tensor, ps_guard[0])
        p.wait(nc.tensor, "SC", z2sc)
        for ic in range(16):
            inst = nc.tensor.matmul(bank[:, :TBM], wx[:, ic, :], z2[:, ic, :],
                                    start=(ic == 0), stop=(ic == 15))
        mmx = p.inc(inst, "PE")
        p.wait(nc.scalar, "PE", mmx)
        if MAMBA_DEBUG:
            scp = p.inc(nc.scalar.copy(praw[:, :], bank[:, :TBM]), "SC")
        scpb = p.inc(nc.scalar.copy(praw_b[:, :], bank[:, :TBM]), "SC")
        ps_guard[0] = {"SC": scpb}
        p.wait_counts(nc.tensor, dtr_guard)
        p.wait(nc.tensor, "SC", scpb)
        mmd = p.inc(nc.tensor.matmul(ps_dtr[:, :TBM], sel[:, 0, :], praw_b[:, :],
                                     start=True, stop=True), "PE")
        p.wait(nc.scalar, "GP", p.cnt["GP"])
        p.wait(nc.scalar, "VE", p.cnt["VE"])
        for n in range(2 * NST):
            p.wait_counts(nc.tensor, cbc_guard)
            srow = (1 + n) if n < NST else (17 + n - NST)
            mmb = p.inc(nc.tensor.matmul(ps_cbc[:, :TBM], sel[:, srow, :], praw_b[:, :],
                                         start=True, stop=True), "PE")
            p.wait(nc.scalar, "PE", mmb)
            dst = Bbc[:, n, :] if n < NST else Cbc[:, n - NST, :]
            cbc_guard = {"SC": p.inc(nc.scalar.copy(dst, ps_cbc[:, :TBM]), "SC")}
        bbc_sc = p.cnt["SC"]
        for j in range(16):
            p.wait(nc.scalar, "PE", mmd)
            p.wait_counts(nc.scalar, {"VE": p.cnt["VE"]})
            nc.scalar.activation(scr2[:, :], ps_dtr[:, :TBM], AF.Abs,
                                 scale=dtw[:, j:j + 1], bias=dtb[:, j:j + 1])
            dr(nc.scalar)
            nc.scalar.activation(scr2[:, :], scr2[:, :], AF.Exp, scale=-1.0)
            dr(nc.scalar)
            nc.scalar.activation(scr2[:, :], scr2[:, :], AF.Ln, bias=1.0)
            scdt = p.inc(nc.scalar.activation(dt_sb[:, :], ps_dtr[:, :TBM], AF.Relu,
                                 scale=dtw[:, j:j + 1], bias=dtb[:, j:j + 1]), "SC")
            p.wait(nc.vector, "SC", scdt)
            vedt = p.inc(nc.vector.tensor_add(dt_sb[:, :], dt_sb[:, :], scr2[:, :]), "VE")
            dr(nc.vector)
            p.wait(nc.vector, "GP", p.cnt["GP"])
            veu = p.inc(nc.vector.tensor_mul(u_sb[:, :], dt_sb[:, :], z2[:, j, :]), "VE")
            p.wait(nc.gpsimd, "VE", veu)
            p.wait(nc.gpsimd, "SC", bbc_sc)
            ab_guard = [0, 0]
            for n in range(NST):
                sn = n % 2
                p.wait(nc.scalar, "VE", max(vedt, ab_guard[sn]))
                sca = p.inc(nc.scalar.activation(a_sb[:, sn, :], dt_sb[:, :], AF.Identity,
                                                 scale=-float(n + 1), bias=1.0), "SC")
                if ab_guard[sn]:
                    p.wait(nc.gpsimd, "VE", ab_guard[sn])
                gpb = p.inc(nc.gpsimd.tensor_tensor(b_sb[:, sn, :], u_sb[:, :], Bbc[:, n, :],
                                                    ALU.mult), "GP")
                p.wait(nc.vector, "SC", sca)
                p.wait(nc.vector, "GP", gpb)
                init = 0.0 if tau == 0 else hc[:, j, n:n + 1]
                nc.vector.tensor_tensor_scan(h_sb[:, sn, :], a_sb[:, sn, :], b_sb[:, sn, :],
                                             init, ALU.mult, ALU.add)
                if n == 0:
                    p.wait(nc.vector, "SC", bbc_sc)
                veyh = p.inc(nc.vector.tensor_mul(yh_sb[:, sn, :], h_sb[:, sn, :],
                                                  Cbc[:, n, :]), "VE")
                nc.vector.tensor_copy(hc[:, j, n:n + 1], h_sb[:, sn, TBM - 1:TBM])
                ab_guard[sn] = veyh
                if n == 0:
                    p.wait_counts(nc.tensor, y_guard)
                p.wait(nc.tensor, "VE", veyh)
                mmy = p.inc(nc.tensor.matmul(ps_y[:, :TBM], ident[:, :], yh_sb[:, sn, :],
                                             start=(n == 0), stop=(n == NST - 1)), "PE")
            p.wait(nc.vector, "PE", mmy)
            p.wait(nc.vector, "SC", prev_mln_sc)
            y_guard = {"VE": p.inc(nc.vector.scalar_tensor_tensor(
                y_sb[:, j, :], z2[:, j, :], Dv[:, j:j + 1], ps_y[:, :TBM],
                ALU.mult, ALU.add), "VE")}
        if MAMBA_DEBUG:
            p.wait(nc.sync, "SC", scp)
            p.inc(nc.sync.dma_start(dbg["dbg_praw"][:, t0:t0 + TBM], praw[:, :]), "DBG", dma=True)
            p.wait_counts(nc.sync, y_guard)
            yb = y_sb[:, :, :].rearrange("p c t -> p (c t)")
            p.inc(nc.sync.dma_start(dbg["dbg_y"][:, :, t0:t0 + TBM], y_sb[:, :, :]), "DBG", dma=True)
            p.inc(nc.sync.dma_start(dbg["dbg_dt"][:, t0:t0 + TBM], dt_sb[:, :]), "DBG", dma=True)
        dtr_guard = {"SC": p.cnt["SC"]}
        prev_yD_ve = p.cnt["VE"]
        yv = p.cnt["VE"]
        for hh in range(2):
            ho = hh * 256
            yth = [y_sb[:, c, ho:ho + 256] for c in range(16)]
            inv_bc, o_bc, pe_bc = ln.stats(yth, {"VE": yv})
            p.wait(nc.vector, "SC", pe_bc)
            for c in range(16):
                nc.vector.tensor_mul(scr[:, :256], yth[c], inv_bc)
                dr(nc.vector)
                nc.vector.tensor_add(scr[:, :256], scr[:, :256], o_bc)
                dr(nc.vector)
                nc.vector.tensor_scalar(scr[:, :256], scr[:, :256], mg[:, c:c + 1],
                                        mb[:, c:c + 1], ALU.mult, ALU.add)
                dr(nc.vector)
                inst = nc.vector.tensor_mul(y_sb[:, c, ho:ho + 256], scr[:, :256],
                                            sig[:, c, ho:ho + 256])
                dr(nc.vector)
            p.inc(inst, "VE")
            ln.mark_used()
        prev_mln_sc = p.cnt["SC"]
        ygv = p.cnt["VE"]
        prev_ygv = ygv
        p.wait(nc.tensor, "VE", ygv)
        if x_hist[1 - sl] is not None:
            p.wait(nc.scalar, f"ST{1 - sl}", x_hist[1 - sl].get(f"ST{1 - sl}", 0))
        resv = 0
        for oc in range(8):
            ws = oc % 4
            p.wait_counts(nc.sync, wo_guard[ws])
            wld = p.inc(nc.sync.dma_start(wout_sb[:, ws, :, :], wout_d[:, oc, :, :]), f"LO{ws}", dma=True)
            bank = ps_mm[oc % 2]
            p.wait_counts(nc.tensor, ps_guard[oc % 2])
            p.wait(nc.tensor, f"LO{ws}", wld)
            for ic in range(16):
                inst = nc.tensor.matmul(bank[:, :TBM], wout_sb[:, ws, ic, :], y_sb[:, ic, :],
                                        start=(ic == 0), stop=(ic == 15))
            mm = p.inc(inst, "PE")
            wo_guard[ws] = {"PE": mm}
            p.wait(nc.vector, "PE", mm)
            resv = p.inc(nc.vector.tensor_add(x_sb[:, 0, oc, :], x_sb[:, 0, oc, :],
                                              bank[:, :TBM]), "VE")
            ps_guard[oc % 2] = {"VE": resv}
        p.wait(nc.sync, "VE", resv)
        st = p.inc(nc.sync.dma_start(xout[:, :, t0:t0 + TBM], x_sb[:, 0, :, :]), f"ST{sl}", dma=True)
        x_hist[sl] = {"VE": resv, "PE": p.cnt["PE"], "SC": p.cnt["SC"], f"ST{sl}": st}
    p.finish()
    return p




def emit_opmlp(p: Prog):
    nc = p.nc
    xin = ch_major(p.stream("xin", (D, T)))
    tin = ch_major(p.stream("tin", (D, T)))
    rin = ch_major(p.stream("rin", (D, T)))
    xout = ch_major(p.stream("xout", (D, T), out=True))
    w1 = load_w_bf(p, "OP1", 8, 1024)
    w2 = load_w_bf(p, "OP2", 8, 1024)
    b1 = load_small(p, "OP1_b", (128, 8))
    b2 = load_small(p, "OP2_b", (128, 8))
    p.wait(nc.tensor, "LD")
    p.wait(nc.scalar, "LD")
    p.wait(nc.vector, "LD")

    x_sb = p.sbuf("x_sb", (128, 2, 8, TB), F32)
    t_sb = p.sbuf("t_sb", (128, 2, 8, TB), F32)
    r_sb = p.sbuf("r_sb", (128, 2, 8, TB), F32)
    xb = p.sbuf("xb", (128, 8, TB), BF16)
    h_sb = p.sbuf("h_sb", (128, 8, TB), BF16)
    o_sb = p.sbuf("o_sb", (128, 8, TB), F32)
    scr = p.sbuf("scr", (128, TB), F32)
    ln = LN(p, "op_ln")
    ps_mm = [p.psum(f"op_mm{i}", (128, 512), F32) for i in range(2)]

    x_hist = [None, None]
    ps_guard = [{}, {}]
    blk_hist = None
    for tau in range(NT):
        dr(nc.vector)
        dr(nc.scalar)
        sl = tau % 2
        t0 = tau * TB
        if x_hist[sl] is not None:
            p.wait_counts(nc.sync, x_hist[sl])
        ld = p.inc(nc.sync.dma_start(x_sb[:, sl, :, :], xin[:, :, t0:t0 + TB]), f"LX{sl}", dma=True)
        ld2 = p.inc(nc.sync.dma_start(t_sb[:, sl, :, :], tin[:, :, t0:t0 + TB]), f"LT{sl}", dma=True)
        ld3 = p.inc(nc.sync.dma_start(r_sb[:, sl, :, :], rin[:, :, t0:t0 + TB]), f"LR{sl}", dma=True)
        p.wait(nc.scalar, f"LX{sl}", ld)
        if blk_hist is not None:
            p.wait_counts(nc.scalar, {"PE": blk_hist["PE"]})
        for c in range(8):
            inst = nc.scalar.copy(xb[:, c, :], x_sb[:, sl, c, :])
        xbv = p.inc(inst, "SC")
        p.wait(nc.tensor, "SC", xbv)
        for oc in range(8):
            bank = ps_mm[oc % 2]
            p.wait_counts(nc.tensor, ps_guard[oc % 2])
            for ic in range(8):
                inst = nc.tensor.matmul(bank[:, :TB], w1[:, ic, oc * 128:(oc + 1) * 128],
                                        xb[:, ic, :], start=(ic == 0), stop=(ic == 7))
            mm = p.inc(inst, "PE")
            p.wait(nc.scalar, "PE", mm)
            ps_guard[oc % 2] = {"SC": p.inc(
                nc.scalar.activation(h_sb[:, oc, :], bank[:, :TB], AF.Gelu,
                                     bias=b1[:, oc:oc + 1]), "SC")}
        hsc = p.cnt["SC"]
        p.wait(nc.tensor, "SC", hsc)
        p.wait(nc.vector, f"LT{sl}", ld2)
        p.wait(nc.vector, "ST0", p.cnt.get("ST0", 0))
        p.wait(nc.vector, "ST1", p.cnt.get("ST1", 0))
        ve_o = 0
        for oc in range(8):
            bank = ps_mm[oc % 2]
            p.wait_counts(nc.tensor, ps_guard[oc % 2])
            for ic in range(8):
                inst = nc.tensor.matmul(bank[:, :TB], w2[:, ic, oc * 128:(oc + 1) * 128],
                                        h_sb[:, ic, :], start=(ic == 0), stop=(ic == 7))
            mm = p.inc(inst, "PE")
            p.wait(nc.vector, "PE", mm)
            ve_o = p.inc(nc.vector.scalar_tensor_tensor(
                o_sb[:, oc, :], bank[:, :TB], b2[:, oc:oc + 1], t_sb[:, sl, oc, :],
                ALU.add, ALU.add), "VE")
            dr(nc.vector)
            ps_guard[oc % 2] = {"VE": ve_o}
        ot = [o_sb[:, c, :] for c in range(8)]
        inv2, o2, pe_bc2 = ln.stats(ot, {"VE": ve_o})
        p.wait(nc.vector, "SC", pe_bc2)
        p.wait(nc.vector, f"LR{sl}", ld3)
        for c in range(8):
            nc.vector.tensor_mul(scr[:, :], ot[c], inv2)
            dr(nc.vector)
            nc.vector.tensor_add(scr[:, :], scr[:, :], o2)
            dr(nc.vector)
            inst = nc.vector.tensor_add(o_sb[:, c, :], scr[:, :], r_sb[:, sl, c, :])
            dr(nc.vector)
        res_ve = p.inc(inst, "VE")
        ln.mark_used()
        p.wait(nc.sync, "VE", res_ve)
        p.inc(nc.sync.dma_start(xout[:, :, t0:t0 + TB], o_sb[:, :, :]), f"ST{sl}", dma=True)
        x_hist[sl] = {"VE": res_ve, "PE": p.cnt["PE"], "SC": p.cnt["SC"]}
        blk_hist = p.snap()
    p.finish()
    return p


def emit_ln_only(p: Prog):
    nc = p.nc
    xin = ch_major(p.stream("xin", (D, T)))
    xout = ch_major(p.stream("xout", (D, T), out=True))
    x_sb = p.sbuf("x_sb", (128, 2, 8, TB), F32)
    o_sb = p.sbuf("o_sb", (128, 2, 8, TB), F32)
    ln = LN(p, "lno")
    x_hist = [None, None]
    for tau in range(NT):
        dr(nc.vector)
        dr(nc.scalar)
        sl = tau % 2
        t0 = tau * TB
        if x_hist[sl] is not None:
            p.wait_counts(nc.sync, x_hist[sl])
        ld = p.inc(nc.sync.dma_start(x_sb[:, sl, :, :], xin[:, :, t0:t0 + TB]), f"LX{sl}", dma=True)
        xt = [x_sb[:, sl, c, :] for c in range(8)]
        inv_bc, o_bc, pe_bc = ln.stats(xt, {f"LX{sl}": ld})
        p.wait(nc.vector, "SC", pe_bc)
        for c in range(8):
            ve = ln_apply(p, xt[c], o_sb[:, sl, c, :], inv_bc, o_bc, x_sb[:, sl, c, :] if False else o_sb[:, sl, c, :])
        ln.mark_used()
        p.wait(nc.sync, "VE", ve)
        st = p.inc(nc.sync.dma_start(xout[:, :, t0:t0 + TB], o_sb[:, sl, :, :]), f"ST{sl}", dma=True)
        x_hist[sl] = {"VE": ve, "PE": p.cnt["PE"], "SC": p.cnt["SC"], f"ST{sl}": st}
    p.finish()
    return p


def emit_final(p: Prog):
    nc = p.nc
    xin = ch_major(p.stream("xin", (D, T)))
    rin = ch_major(p.stream("rin", (D, T)))
    out = p.dram_out("out", (T, D))
    ident = load_small(p, "ident128f", (128, 128), F32)
    p.wait(nc.tensor, "LD")
    x_sb = p.sbuf("x_sb", (128, 2, 8, TB), F32)
    r_sb = p.sbuf("r_sb", (128, 2, 8, TB), F32)
    o_sb = p.sbuf("o_sb", (128, 8, TB), F32)
    ot_sb = p.sbuf("ot_sb", (128, 2, 8, 128), F32)
    scr = p.sbuf("scr", (128, TB), F32)
    ln = LN(p, "fin_ln")
    ps_t = [p.psum(f"fin_t{i}", (128, 512), F32) for i in range(2)]
    x_hist = [None, None]
    tr_guard = [{}, {}]
    ot_guard = [{}, {}]
    o_guard = {}
    for tau in range(NT):
        dr(nc.vector)
        dr(nc.scalar)
        sl = tau % 2
        t0 = tau * TB
        if x_hist[sl] is not None:
            p.wait_counts(nc.sync, x_hist[sl])
        ld = p.inc(nc.sync.dma_start(x_sb[:, sl, :, :], xin[:, :, t0:t0 + TB]), f"LX{sl}", dma=True)
        ld2 = p.inc(nc.sync.dma_start(r_sb[:, sl, :, :], rin[:, :, t0:t0 + TB]), f"LR{sl}", dma=True)
        xt = [x_sb[:, sl, c, :] for c in range(8)]
        inv_bc, o_bc, pe_bc = ln.stats(xt, {f"LX{sl}": ld})
        p.wait(nc.vector, "SC", pe_bc)
        p.wait(nc.vector, f"LR{sl}", ld2)
        p.wait_counts(nc.vector, o_guard)
        for c in range(8):
            nc.vector.tensor_mul(scr[:, :], xt[c], inv_bc)
            dr(nc.vector)
            nc.vector.tensor_add(scr[:, :], scr[:, :], o_bc)
            dr(nc.vector)
            inst = nc.vector.tensor_add(o_sb[:, c, :], scr[:, :], r_sb[:, sl, c, :])
            dr(nc.vector)
        res_ve = p.inc(inst, "VE")
        ln.mark_used()
        p.wait(nc.tensor, "VE", res_ve)
        o_guard = {"PE": None}
        for half in range(2):
            for grp in range(2):
                bank = ps_t[grp]
                p.wait_counts(nc.tensor, tr_guard[grp])
                for cc in range(4):
                    c = grp * 4 + cc
                    inst = nc.tensor.transpose(bank[:, cc * 128:(cc + 1) * 128],
                                               o_sb[:, c, half * 128:(half + 1) * 128],
                                               ident[:, :])
                mm = p.inc(inst, "PE")
                p.wait(nc.scalar, "PE", mm)
                p.wait_counts(nc.scalar, ot_guard[sl])
                sc = p.inc(nc.scalar.copy(
                    ot_sb[:, sl, grp * 4:(grp + 1) * 4, :].rearrange("p a b -> p (a b)"),
                    bank[:, :]), "SC")
                tr_guard[grp] = {"SC": sc}
            p.wait(nc.sync, "SC", p.cnt["SC"])
            st = p.inc(nc.sync.dma_start(
                out[t0 + half * 128: t0 + (half + 1) * 128, :],
                ot_sb[:, sl, :, :].rearrange("p a b -> p (a b)")), f"ST{sl}{half}", dma=True)
            ot_guard[sl] = {f"ST{sl}{half}": st}
        o_guard = {"PE": p.cnt["PE"]}
        x_hist[sl] = {"VE": res_ve, "PE": p.cnt["PE"], "SC": p.cnt["SC"]}
    p.finish()
    return p




def emit_ffn_attn(p: Prog, i):
    nc = p.nc
    xin = ch_major(p.stream("xin", (D, T)))
    xout = ch_major(p.stream("xout", (D, T), out=True))
    wqkv = load_w_bf(p, f"F{i}_Wqkv", 8, 3072)
    wo = load_w_bf(p, f"F{i}_Wo", 8, 1024)
    qs = load_small(p, f"F{i}_qs", (128, 1))
    ks = load_small(p, f"F{i}_ks", (128, 1))
    qsp = load_small(p, f"F{i}_qsp", (128, 1))
    ksp = load_small(p, f"F{i}_ksp", (128, 1))
    rbd = load_small(p, "RbdT", (128, 128), BF16)
    idb = load_small(p, "ident128", (128, 128), BF16)
    bd2 = load_small(p, "bd2", (128, 2))
    bdt = load_small(p, "bdT", (2, 128))
    mka = load_small(p, "mask_add_t", (128, 2, 512))
    mka0 = load_small(p, "mask_add0_t", (128, 2, 256))
    rot_d = {nm: p.dram_in(nm, (128, T)) for nm in ("rot_qc", "rot_qs", "rot_kc", "rot_ks")}
    p.wait(nc.tensor, "LD")
    p.wait(nc.scalar, "LD")
    p.wait(nc.vector, "LD")

    x_sb = p.sbuf("x_sb", (128, 2, 8, 1 + TB), F32)
    xb = p.sbuf("xb", (128, 8, 1 + TB), BF16)
    q_sb = p.sbuf("q_sb", (128, 8, TB), F32)
    k_sb = p.sbuf("k_sb", (128, 8, TB), F32)
    v_sb = p.sbuf("v_sb", (128, 2, 8, TB), BF16)
    qf_sb = p.sbuf("qf_sb", (128, 8, TB), BF16)
    kf_sb = p.sbuf("kf_sb", (128, 2, 8, TB), BF16)
    vt_sb = p.sbuf("vt_sb", (128, 2, 16, 2, 64), BF16)
    rt_sb = p.sbuf("rt_sb", (128, 4, TB), F32)
    tq_sb = p.sbuf("tq_sb", (128, 4, TB), F32)
    sqq = p.sbuf("sqq", (128, TB), F32)
    nrm = p.sbuf("nrm", (2, 2, 8, TB), F32)
    qn_b = p.sbuf("qn_b", (128, TB), BF16)
    invq_sb = p.sbuf("invq_sb", (128, TB), F32)
    rot_sb = p.sbuf("rot_sb", (128, TB), F32)
    scr = p.sbuf("scr", (128, TB), F32)
    scr2 = p.sbuf("scr2", (128, TB), F32)
    sim_m = p.sbuf("sim_m", (128, 2, 512), F32)
    attn_b = p.sbuf("attn_b", (128, 2, 512), BF16)
    attn_n = p.sbuf("attn_n", (128, 2, 512), BF16)
    mx = p.sbuf("mx", (128, 2, 4), F32)
    at_sb = p.sbuf("at_sb", (128, 2, 4, 128), BF16)
    ob_sb = p.sbuf("ob_sb", (128, 8, TB), BF16)

    ps_mm = [p.psum(f"fa_mm{k}", (128, 512), F32) for k in range(2)]
    ps_sim2 = [p.psum(f"fa_sim{k}", (128, 512), F32) for k in range(2)]
    ps_sim = ps_sim2[0]
    ps_at2 = p.psum("fa_at", (128, 1024), BF16)
    ps_av2 = p.psum("fa_av", (128, 512), F32)
    ps_bc = p.psum("fa_bc", (128, 512), F32)

    x_hist = [None, None]
    ps_guard = [{}, {}]
    sim_guard = {}
    at_guard = {}
    av_guard = [{}, {}]
    bc_guard = {}
    prev = {}
    for tau in range(NT):
        dr(nc.vector)
        dr(nc.scalar)
        sl = tau % 2
        t0 = tau * TB
        if x_hist[sl] is not None:
            p.wait_counts(nc.sync, x_hist[sl])
        if tau == 0:
            ld = p.inc(nc.sync.dma_start(x_sb[:, sl, :, 1:], xin[:, :, 0:TB]), f"LX{sl}", dma=True)
            p.wait(nc.vector, f"LX{sl}", ld)
            ve0 = p.inc(nc.vector.memset(x_sb[:, sl, :, 0:1], 0.0), "VE")
            xld = {f"LX{sl}": ld, "VE": ve0}
        else:
            ld = p.inc(nc.sync.dma_start(x_sb[:, sl, :, :], xin[:, :, t0 - 1:t0 + TB]),
                       f"LX{sl}", dma=True)
            xld = {f"LX{sl}": ld}
        ldr = 0
        for r, nm in enumerate(("rot_qc", "rot_qs", "rot_kc", "rot_ks")):
            p.wait_counts(nc.sync, prev.get("rt", {}))
            ldr = p.inc(nc.sync.dma_start(rt_sb[:, r, :], rot_d[nm][:, t0:t0 + TB]),
                        f"LQ{sl}", dma=True)
        p.wait(nc.vector, f"LQ{sl}", ldr)
        p.wait_counts(nc.vector, prev.get("tq", {}))
        nc.vector.tensor_scalar(tq_sb[:, 0, :], rt_sb[:, 0, :], qs[:, 0:1], None, ALU.mult)
        nc.vector.tensor_scalar(tq_sb[:, 1, :], rt_sb[:, 1, :], qsp[:, 0:1], None, ALU.mult)
        nc.vector.tensor_scalar(tq_sb[:, 2, :], rt_sb[:, 2, :], ks[:, 0:1], None, ALU.mult)
        tqv = p.inc(nc.vector.tensor_scalar(tq_sb[:, 3, :], rt_sb[:, 3, :], ksp[:, 0:1],
                                            None, ALU.mult), "VE")
        prev["rt"] = {"VE": tqv}
        p.wait_counts(nc.scalar, xld)
        p.wait_counts(nc.scalar, prev.get("xb", {}))
        for c in range(8):
            inst = nc.scalar.copy(xb[:, c, :], x_sb[:, sl, c, :])
        xbv = p.inc(inst, "SC")
        prev["xb"] = {}
        p.wait(nc.tensor, "SC", xbv)
        for oc in range(24):
            bank = ps_mm[oc % 2]
            p.wait_counts(nc.tensor, ps_guard[oc % 2])
            for ic in range(8):
                off = 1 if ic < 4 else 0
                inst = nc.tensor.matmul(bank[:, :TB], wqkv[:, ic, oc * 128:(oc + 1) * 128],
                                        xb[:, ic, off:off + TB], start=(ic == 0), stop=(ic == 7))
            mm = p.inc(inst, "PE")
            p.wait(nc.scalar, "PE", mm)
            if oc < 8:
                p.wait_counts(nc.scalar, prev.get("q", {}))
                sc = p.inc(nc.scalar.copy(q_sb[:, oc, :], bank[:, :TB]), "SC")
            elif oc < 16:
                p.wait_counts(nc.scalar, prev.get("k", {}))
                sc = p.inc(nc.scalar.copy(k_sb[:, oc - 8, :], bank[:, :TB]), "SC")
            else:
                p.wait_counts(nc.scalar, prev.get("v", {}))
                sc = p.inc(nc.scalar.copy(v_sb[:, sl, oc - 16, :], bank[:, :TB]), "SC")
            ps_guard[oc % 2] = {"SC": sc}
        qkv_sc = p.cnt["SC"]
        p.wait(nc.scalar, "SC", qkv_sc)
        for qk, src in ((0, q_sb), (1, k_sb)):
            for c in range(8):
                p.wait_counts(nc.scalar, {"PE": p.cnt["PE"]})
                dr(nc.scalar)
                ssc = p.inc(nc.scalar.activation(sqq[:, :], src[:, c, :], AF.Square), "SC")
                p.wait(nc.tensor, "SC", ssc)
                p.wait_counts(nc.tensor, bc_guard)
                mm = p.inc(nc.tensor.matmul(ps_bc[0:2, 0:TB], bd2[:, :], sqq[:, :],
                                            start=True, stop=True), "PE")
                p.wait(nc.scalar, "PE", mm)
                p.wait_counts(nc.scalar, prev.get("nrm", {}))
                bc_guard = {"SC": p.inc(nc.scalar.copy(nrm[0:2, qk, c, :], ps_bc[0:2, 0:TB]),
                                        "SC")}
        nsc = p.cnt["SC"]
        nflat = nrm[0:2, :, :, :].rearrange("p a c t -> p (a c t)")
        p.wait(nc.vector, "SC", nsc)
        vem = p.inc(nc.vector.tensor_scalar(nflat, nflat, 1e-24, None, ALU.max), "VE")
        p.wait(nc.scalar, "VE", vem)
        dr(nc.scalar)
        nc.scalar.activation(nflat, nflat, AF.Ln)
        dr(nc.scalar)
        ninv = p.inc(nc.scalar.activation(nflat, nflat, AF.Exp, scale=-0.5), "SC")
        prev["nrm"] = {"VE": 0}
        p.wait(nc.tensor, "SC", ninv)
        for qk, src, dst, ci, si in ((0, q_sb, qf_sb, 0, 1), (1, k_sb, kf_sb, 2, 3)):
            for c in range(8):
                p.wait_counts(nc.tensor, bc_guard)
                mmb = p.inc(nc.tensor.matmul(ps_bc[:, 0:TB], bdt[:, :], nrm[0:2, qk, c, :],
                                             start=True, stop=True), "PE")
                p.wait(nc.scalar, "PE", mmb)
                p.wait_counts(nc.scalar, prev.get("invq", {}))
                scb = p.inc(nc.scalar.copy(invq_sb[:, :], ps_bc[:, 0:TB]), "SC")
                bc_guard = {"SC": scb}
                p.wait(nc.vector, "SC", scb)
                p.wait(nc.vector, "VE", tqv)
                p.wait_counts(nc.vector, prev.get("qnb", {}))
                veq = p.inc(nc.vector.tensor_mul(qn_b[:, :], src[:, c, :], invq_sb[:, :]), "VE")
                prev["invq"] = {"VE": veq}
                p.wait(nc.tensor, "VE", veq)
                p.wait_counts(nc.tensor, prev.get("rot", {}))
                mmr = p.inc(nc.tensor.matmul(ps_sim[:, 0:TB], rbd[:, :], qn_b[:, :],
                                             start=True, stop=True), "PE")
                p.wait(nc.scalar, "PE", mmr)
                p.wait_counts(nc.scalar, prev.get("rotsb", {}))
                scr_ = p.inc(nc.scalar.copy(rot_sb[:, :], ps_sim[:, 0:TB]), "SC")
                prev["rot"] = {"SC": scr_}
                nc.vector.tensor_mul(scr[:, :], qn_b[:, :], tq_sb[:, ci, :])
                p.wait(nc.vector, "SC", scr_)
                nc.vector.tensor_mul(scr2[:, :], rot_sb[:, :], tq_sb[:, si, :])
                if qk == 0:
                    inst = nc.vector.tensor_add(dst[:, c, :], scr[:, :], scr2[:, :])
                else:
                    inst = nc.vector.tensor_add(dst[:, sl, c, :], scr[:, :], scr2[:, :])
                ver = p.inc(inst, "VE")
                prev["rotsb"] = {"VE": ver}
                prev["qnb"] = {"VE": ver}
        rot_ve = p.cnt["VE"]
        p.wait(nc.tensor, "SC", qkv_sc)
        vt_guard = dict(g_atsb[0]) if tau > 0 else {}
        for h in range(16):
            pb = (h % 2) * 64
            ct = h // 2
            for jc in range(2):
                p.wait_counts(nc.tensor, vt_guard)
                inst = nc.tensor.transpose(ps_at2[0:128, 0:64],
                                           v_sb[:, sl, ct, jc * 128:(jc + 1) * 128][pb:pb + 64, :],
                                           idb[:, :][pb:pb + 64, pb:pb + 64])
                mm = p.inc(inst, "PE")
                p.wait(nc.scalar, "PE", mm)
                vt_guard = {"SC": p.inc(nc.scalar.copy(vt_sb[:, sl, h, jc, :],
                                                       ps_at2[0:128, 0:64]), "SC")}
        vt_sc = p.cnt["SC"]
        p.wait(nc.tensor, "VE", rot_ve)
        p.wait(nc.tensor, "SC", vt_sc)
        if tau == 0:
            g_sim = [{}, {}]
            g_attnb = [{}, {}]
            g_attnn = [{}, {}]
            g_atps = [{}, {}]
            g_atsb = [{}, {}]
            g_avrd = [{}, {}]
            g_av = [{}, {}]
            itc = 0
        g_atsb[0] = {"SC": vt_sc}
        wid = 512 if tau > 0 else 256
        nch = wid // 128
        iters = [(ct, hh, m) for ct in range(8) for hh in range(2) for m in range(2)]
        pend = None

        def emit_head(idx):
            nonlocal_itc = None
            ct, hh, m = iters[idx]
            it = (tau * 32 + idx) % 2
            pb = hh * 64
            qT = qf_sb[:, ct, m * 128:(m + 1) * 128][pb:pb + 64, :]
            psim = ps_sim2[it]
            p.wait_counts(nc.tensor, g_sim[it])
            if idx % 4 == 0:
                p.wait_counts(nc.tensor, g_av[ct % 2])
            if tau > 0:
                nc.tensor.matmul(psim[:, 0:256], qT,
                                 kf_sb[:, 1 - sl, ct, :][pb:pb + 64, :],
                                 start=True, stop=True)
            mms = p.inc(nc.tensor.matmul(
                psim[:, (wid - 256):wid] if tau > 0 else psim[:, 0:256],
                qT, kf_sb[:, sl, ct, :][pb:pb + 64, :],
                start=True, stop=True), "PE")
            p.wait(nc.vector, "PE", mms)
            p.wait_counts(nc.vector, g_attnn[it])
            msk = mka[:, m, 0:wid] if tau > 0 else mka0[:, m, 0:wid]
            nc.vector.scalar_tensor_tensor(sim_m[:, it, 0:wid], psim[:, 0:wid], 8.0,
                                           msk, ALU.mult, ALU.add)
            nc.vector.drain()
            vet = p.inc(nc.vector.reduce_max(mx[:, it, 1:2], sim_m[:, it, 0:wid],
                                             AX.X, negate=True), "VE")
            g_sim[it] = {"VE": vet}
            p.wait(nc.scalar, "VE", vet)
            p.wait_counts(nc.scalar, g_attnb[it])
            sce = p.inc(nc.scalar.activation(attn_b[:, it, 0:wid], sim_m[:, it, 0:wid],
                                             AF.Exp, bias=mx[:, it, 1:2],
                                             accum_out=mx[:, it, 2:3]), "SC")
            g_attnn[it] = {"SC": sce}
            return (it, ct, hh, m, sce)

        def emit_tail(state):
            it, ct, hh, m, sce = state
            pb = hh * 64
            h = ct * 2 + hh
            avs = ps_av2[:, (ct % 2) * 256:(ct % 2) * 256 + 256]
            p.wait(nc.vector, "SC", sce)
            nc.vector.reciprocal(mx[:, it, 3:4], mx[:, it, 2:3])
            nc.vector.drain()
            p.wait_counts(nc.vector, g_atps[it])
            vesc = p.inc(nc.vector.tensor_scalar(attn_n[:, it, 0:wid],
                                                 attn_b[:, it, 0:wid],
                                                 mx[:, it, 3:4], None, ALU.mult), "VE")
            g_attnb[it] = {"VE": vesc}
            p.wait(nc.tensor, "VE", vesc)
            p.wait_counts(nc.tensor, g_atsb[it])
            for jc in range(nch):
                inst = nc.tensor.transpose(
                    ps_at2[:, it * 512 + jc * 128: it * 512 + (jc + 1) * 128],
                    attn_n[:, it, jc * 128:(jc + 1) * 128], idb[:, :])
            mmt = p.inc(inst, "PE")
            g_atps[it] = {"PE": mmt}
            p.wait(nc.scalar, "PE", mmt)
            p.wait_counts(nc.scalar, g_avrd[it])
            sca = p.inc(nc.scalar.copy(
                at_sb[:, it, 0:nch, :].rearrange("p a b -> p (a b)"),
                ps_at2[:, it * 512: it * 512 + wid]), "SC")
            g_atsb[it] = {"SC": sca}
            p.wait(nc.tensor, "SC", sca)
            if tau > 0:
                chunks = [(1 - sl, 0, 0), (1 - sl, 1, 1), (sl, 0, 2), (sl, 1, 3)]
            else:
                chunks = [(sl, 0, 0), (sl, 1, 1)]
            for (bp, vjc, ajc) in chunks:
                inst = nc.tensor.matmul(
                    avs[pb:pb + 64, m * 128:(m + 1) * 128],
                    vt_sb[:, bp, h, vjc, :], at_sb[:, it, ajc, :],
                    start=(ajc == chunks[0][2]), stop=(ajc == chunks[-1][2]))
            mmv = p.inc(inst, "PE")
            g_avrd[it] = {"PE": mmv}
            if hh == 1 and m == 1:
                avs_full = ps_av2[:, (ct % 2) * 256:(ct % 2) * 256 + 256]
                p.wait(nc.scalar, "PE", mmv)
                g_av[ct % 2] = {"SC": p.inc(nc.scalar.copy(ob_sb[:, ct, :], avs_full), "SC")}

        for idx in range(32):
            st_h = emit_head(idx)
            if pend is not None:
                emit_tail(pend)
            pend = st_h
        emit_tail(pend)
        ob_sc = p.cnt["SC"]
        prev["ob"] = {}
        p.wait(nc.tensor, "SC", ob_sc)
        res_ve = 0
        for oc in range(8):
            bank = ps_mm[oc % 2]
            p.wait_counts(nc.tensor, ps_guard[oc % 2])
            for ic in range(8):
                inst = nc.tensor.matmul(bank[:, :TB], wo[:, ic, oc * 128:(oc + 1) * 128],
                                        ob_sb[:, ic, :], start=(ic == 0), stop=(ic == 7))
            mm = p.inc(inst, "PE")
            p.wait(nc.vector, "PE", mm)
            p.wait_counts(nc.vector, x_hist[sl] or {})
            res_ve = p.inc(nc.vector.tensor_add(x_sb[:, sl, oc, 1:], x_sb[:, sl, oc, 1:],
                                                bank[:, :TB]), "VE")
            dr(nc.vector)
            ps_guard[oc % 2] = {"VE": res_ve}
        p.wait(nc.sync, "VE", res_ve)
        st = p.inc(nc.sync.dma_start(xout[:, :, t0:t0 + TB], x_sb[:, sl, :, 1:]),
                   f"ST{sl}", dma=True)
        x_hist[sl] = {"VE": res_ve, "PE": p.cnt["PE"], "SC": p.cnt["SC"], f"ST{sl}": st}
        prev["q"] = {"VE": rot_ve}
        prev["k"] = {"VE": rot_ve}
        prev["v"] = {"PE": p.cnt["PE"]}
        prev["xb"] = {"PE": p.cnt["PE"]}
        prev["tq"] = {"VE": rot_ve}
    p.finish()
    return p




def emit_ffn_ffn(p: Prog, i):
    nc = p.nc
    xin = ch_major(p.stream("xin", (D, T)))
    xout = ch_major(p.stream("xout", (D, T), out=True))
    w1 = load_w_bf(p, f"F{i}_W1", 8, 5632)
    w2 = load_w_bf(p, f"F{i}_W2", 22, 1024)
    lg = load_small(p, f"F{i}_ln_g", (128, 8))
    lb = load_small(p, f"F{i}_ln_b", (128, 8))
    b1a = load_small(p, f"F{i}_b1a", (128, 22))
    b1g = load_small(p, f"F{i}_b1g", (128, 22))
    b2 = load_small(p, f"F{i}_b2", (128, 8))
    p.wait(nc.tensor, "LD")
    p.wait(nc.scalar, "LD")
    p.wait(nc.vector, "LD")

    x_sb = p.sbuf("x_sb", (128, 2, 8, 1 + TB), F32)
    xn_sb = p.sbuf("xn_sb", (128, 8, TB), BF16)
    a_sb = p.sbuf("a_sb", (128, 22, TB), BF16)
    g_sb = p.sbuf("g_sb", (128, TB), BF16)
    hg_sb = p.sbuf("hg_sb", (128, 22, TB), BF16)
    scr = p.sbuf("scr", (128, TB), F32)
    ln = LN(p, "ff_ln")
    ps_mm = [p.psum(f"ff_mm{k}", (128, 512), F32) for k in range(2)]

    x_hist = [None, None]
    ps_guard = [{}, {}]
    prev = {}
    for tau in range(NT):
        dr(nc.vector)
        dr(nc.scalar)
        sl = tau % 2
        t0 = tau * TB
        if x_hist[sl] is not None:
            p.wait_counts(nc.sync, x_hist[sl])
        if tau == 0:
            ld = p.inc(nc.sync.dma_start(x_sb[:, sl, :, 1:], xin[:, :, 0:TB]), f"LX{sl}", dma=True)
            p.wait(nc.vector, f"LX{sl}", ld)
            ve0 = p.inc(nc.vector.memset(x_sb[:, sl, :, 0:1], 0.0), "VE")
            xld = {f"LX{sl}": ld, "VE": ve0}
        else:
            ld = p.inc(nc.sync.dma_start(x_sb[:, sl, :, :], xin[:, :, t0 - 1:t0 + TB]),
                       f"LX{sl}", dma=True)
            xld = {f"LX{sl}": ld}
        xsh = [x_sb[:, sl, c, (1 if c < 4 else 0):(1 if c < 4 else 0) + TB] for c in range(8)]
        inv_bc, o_bc, pe_bc = ln.stats(xsh, xld)
        p.wait(nc.vector, "SC", pe_bc)
        if prev:
            p.wait_counts(nc.vector, {"PE": prev["PE"]})
        for c in range(8):
            nc.vector.tensor_mul(scr[:, :], xsh[c], inv_bc)
            dr(nc.vector)
            nc.vector.tensor_add(scr[:, :], scr[:, :], o_bc)
            dr(nc.vector)
            inst = nc.vector.tensor_scalar(xn_sb[:, c, :], scr[:, :], lg[:, c:c + 1],
                                           lb[:, c:c + 1], ALU.mult, ALU.add)
            dr(nc.vector)
        xnv = p.inc(inst, "VE")
        ln.mark_used()
        p.wait(nc.tensor, "VE", xnv)
        for oc in range(22):
            bank = ps_mm[oc % 2]
            p.wait_counts(nc.tensor, ps_guard[oc % 2])
            for ic in range(8):
                inst = nc.tensor.matmul(bank[:, :TB], w1[:, ic, oc * 128:(oc + 1) * 128],
                                        xn_sb[:, ic, :], start=(ic == 0), stop=(ic == 7))
            mm = p.inc(inst, "PE")
            p.wait(nc.scalar, "PE", mm)
            p.wait_counts(nc.scalar, prev.get("a", {}))
            ps_guard[oc % 2] = {"SC": p.inc(
                nc.scalar.activation(a_sb[:, oc, :], bank[:, :TB], AF.Identity,
                                     bias=b1a[:, oc:oc + 1]), "SC")}
        for oc in range(22):
            bank = ps_mm[oc % 2]
            p.wait_counts(nc.tensor, ps_guard[oc % 2])
            for ic in range(8):
                inst = nc.tensor.matmul(bank[:, :TB], w1[:, ic, (22 + oc) * 128:(23 + oc) * 128],
                                        xn_sb[:, ic, :], start=(ic == 0), stop=(ic == 7))
            mm = p.inc(inst, "PE")
            p.wait(nc.scalar, "PE", mm)
            p.wait_counts(nc.scalar, prev.get("g", {}))
            scg = p.inc(nc.scalar.activation(g_sb[:, :], bank[:, :TB], AF.Gelu,
                                             bias=b1g[:, oc:oc + 1]), "SC")
            p.wait(nc.vector, "SC", scg)
            p.wait_counts(nc.vector, prev.get("hg", {}))
            ve = p.inc(nc.vector.tensor_mul(hg_sb[:, oc, :], a_sb[:, oc, :], g_sb[:, :]), "VE")
            dr(nc.vector)
            ps_guard[oc % 2] = {"VE": ve}
            prev["g"] = {"VE": ve}
        hgv = p.cnt["VE"]
        p.wait(nc.tensor, "VE", hgv)
        res_ve = 0
        for oc in range(8):
            bank = ps_mm[oc % 2]
            p.wait_counts(nc.tensor, ps_guard[oc % 2])
            for ic in range(22):
                inst = nc.tensor.matmul(bank[:, :TB], w2[:, ic, oc * 128:(oc + 1) * 128],
                                        hg_sb[:, ic, :], start=(ic == 0), stop=(ic == 21))
            mm = p.inc(inst, "PE")
            p.wait(nc.vector, "PE", mm)
            res_ve = p.inc(nc.vector.scalar_tensor_tensor(
                x_sb[:, sl, oc, 1:], bank[:, :TB], b2[:, oc:oc + 1], x_sb[:, sl, oc, 1:],
                ALU.add, ALU.add), "VE")
            dr(nc.vector)
            ps_guard[oc % 2] = {"VE": res_ve}
        p.wait(nc.sync, "VE", res_ve)
        st = p.inc(nc.sync.dma_start(xout[:, :, t0:t0 + TB], x_sb[:, sl, :, 1:]),
                   f"ST{sl}", dma=True)
        x_hist[sl] = {"VE": res_ve, "PE": p.cnt["PE"], "SC": p.cnt["SC"], f"ST{sl}": st}
        prev = {"PE": p.cnt["PE"], "a": {"PE": p.cnt["PE"]}, "hg": {"PE": p.cnt["PE"]},
                "g": prev.get("g", {})}
    p.finish()
    return p


def _run(p: Prog, in_map_per_core):
    res = bass_utils.run_bass_kernel_spmd(p.nc, in_map_per_core, core_ids=list(range(len(in_map_per_core))))
    return res.results




def emit_wcast_fused(p: Prog, weights_meta):
    nc = p.nc
    CH = 16384
    stage = p.sbuf("wc_stage", (128, 2, CH), F32)
    out = p.sbuf("wc_out", (128, 2, CH), BF16)
    hist = {0: None, 1: None}
    k = 0
    for name, cols in weights_meta:
        src = p.dram_in(name, (128, cols))
        dst = p.dram_tmp(name + "_bfi", (128, cols), BF16)
        p.wbf[name] = dst
        for c0 in range(0, cols, CH):
            w = min(CH, cols - c0)
            sl = k % 2
            if hist[sl] is not None:
                nc.sync.wait_ge(p.sem("SC"), hist[sl][0])
                nc.sync.wait_ge(p.sem(f"WS{sl}"), hist[sl][1])
            ld = p.inc(nc.sync.dma_start(stage[:, sl, :w], src[:, c0:c0 + w]),
                       f"WL{sl}", dma=True)
            p.wait(nc.scalar, f"WL{sl}", ld)
            sc = p.inc(nc.scalar.copy(out[:, sl, :w], stage[:, sl, :w]), "SC")
            p.wait(nc.sync, "SC", sc)
            st = p.inc(nc.sync.dma_start(dst[:, c0:c0 + w], out[:, sl, :w]),
                       f"WS{sl}", dma=True)
            hist[sl] = (sc, st)
            k += 1
    for sl in range(2):
        if hist[sl] is not None:
            p.wait(nc.sync, f"WS{sl}")


def build_fused(weights_meta):
    p = Prog()
    nc = p.nc
    p.scope_open()
    emit_wcast_fused(p, weights_meta)
    p.scope_close()
    sA = p.dram_tmp("strA", (D, T))
    sB = p.dram_tmp("strB", (D, T))
    sX2 = p.dram_tmp("strX2", (D, T))
    sT0 = p.dram_tmp("strT0", (D, T))
    sX3 = p.dram_tmp("strX3", (D, T))

    def run(emit, streams, args=()):
        p.scope_open()
        p.streams = streams
        emit(p, *args)
        p.scope_close()
        p.streams = {}

    run(emit_l0_attn, {"xin": p.dram_in("xin_full", (D, T)), "xout": sA})
    run(emit_l0_ffn, {"xin": sA, "xout": sX2})
    run(emit_l1_pre, {"xin": sX2, "xout": sT0})
    cur, nxt = sA, sB
    src = sT0
    for l in range(4):
        run(emit_mamba, {"xin": src, "xout": cur}, (l,))
        src = cur
        cur, nxt = nxt, cur
    run(emit_opmlp, {"xin": src, "tin": sT0, "rin": sX2, "xout": sX3})
    cur, nxt = sA, sB
    run(emit_ln_only, {"xin": sX3, "xout": cur})
    src = cur
    cur, nxt = nxt, cur
    for i in range(6):
        run(emit_ffn_attn, {"xin": src, "xout": cur}, (i,))
        src, cur, nxt = cur, nxt, cur
        run(emit_ffn_ffn, {"xin": src, "xout": cur}, (i,))
        src, cur, nxt = cur, nxt, cur
    p.scope_open()
    p.streams = {"xin": src, "rin": sX3}
    emit_final(p)
    p.streams = {}
    for name in list(p.cnt):
        if name.startswith("ST") or name == "DBG":
            p.wait(p.nc.sync, name)
    for cm in reversed(p._scope or []):
        cm.__exit__(None, None, None)
    p._scope = None
    for cm in reversed(p._stack):
        cm.__exit__(None, None, None)
    p._stack = []
    return p


WEIGHTS_META = None


def _launch(build, in_maps, n_cores):
    p = build()
    res = bass_utils.run_bass_kernel_spmd(p.nc, in_maps, core_ids=list(range(n_cores)))
    return p, res.results


def kernel(**inputs):
    W = host_weights(inputs)
    S = host_smalls(inputs)
    u = np.asarray(inputs["L0_attn_u"], dtype=np.float32)
    xs = [np.ascontiguousarray(u[c].T) for c in range(B)]
    Wflat = {k: np.ascontiguousarray(np.asarray(v, np.float32).reshape(128, -1))
             for k, v in W.items()}
    meta = [(k, v.shape[1]) for k, v in Wflat.items()]
    p = build_fused(meta)
    in_maps = []
    for c in range(B):
        m = {}
        for name in p.inputs:
            if name == "xin_full":
                m[name] = xs[c]
            elif name in Wflat:
                m[name] = Wflat[name]
            elif name in S:
                m[name] = S[name]
            else:
                raise KeyError(name)
        in_maps.append(m)
    res = bass_utils.run_bass_kernel_spmd(p.nc, in_maps, core_ids=list(range(B)))
    out = np.stack([np.asarray(r["out"]) for r in res.results], axis=0)
    return out.astype(np.float32)
